# version 22
# baseline (speedup 1.0000x reference)
import os
import numpy as np
import ml_dtypes
import concourse.bass as bass
import concourse.mybir as mybir
from concourse.bass_utils import run_bass_kernel_spmd
from concourse.alu_op_type import AluOpType as ALU
from contextlib import ExitStack

F32 = mybir.dt.float32
BF16 = mybir.dt.bfloat16
AF = mybir.ActivationFunctionType
AX = mybir.AxisListType
bfdt = ml_dtypes.bfloat16

S = 4096
D = 1024
NCH = 8
TCH = 512
NTC = S // TCH
NT = S // 128
H = 16
HD = 64
RLAT = 128
RP = 32
E = 8
II = 512
EPS = 1e-5
NEGBIG = -1e30

ENGS = ("sync", "tensor", "vector", "scalar", "gpsimd")


NSLOT = 4
QUEUES = ("qs", "qg", "qc")


class K:
    def __init__(self, nc, sems):
        self.nc = nc
        self.sems = sems
        self.cnt = {s: 0 for s in sems}
        self.qn = {q: 0 for q in QUEUES}


def _slot_count(n, j):
    return 0 if n <= j else (n - j + NSLOT - 1) // NSLOT


class Ph:

    def __init__(self, k):
        self.k = k
        self.st = {e: [] for e in ENGS}

    def em(self, eng, build, wait=(), inc=()):
        sems = self.k.sems
        exp_wait = []
        for s, v in wait:
            if s in QUEUES:
                n = v // 16
                for j in range(NSLOT):
                    c = _slot_count(n, j)
                    if c > 0:
                        exp_wait.append((f"{s}{j}", 16 * c))
            else:
                exp_wait.append((s, v))
        exp_inc = []
        for s, a in inc:
            if s in QUEUES:
                if os.environ.get("KSERIAL") and self.k.qn[s] > 0:
                    n = self.k.qn[s]
                    for j in range(NSLOT):
                        cc = _slot_count(n, j)
                        if cc > 0:
                            exp_wait.append((f"{s}{j}", 16 * cc))
                j = self.k.qn[s] % NSLOT
                self.k.qn[s] += 1
                exp_inc.append((f"{s}{j}", 16))
            else:
                self.k.cnt[s] += a
                exp_inc.append((s, a))
        w, ii = tuple(exp_wait), tuple(exp_inc)

        def fn(e):
            for s, v in w:
                if v > 0:
                    e.wait_ge(sems[s], v)
            r = build(e)
            for s, a in ii:
                r = r.then_inc(sems[s], a)

        self.st[eng].append(fn)

    def n(self, s):
        if s in QUEUES:
            return 16 * self.k.qn[s]
        return self.k.cnt[s]

    def close(self, drain_stores=True):
        if drain_stores:
            self.em("sync", lambda e: e.nop(),
                    wait=(("qc", self.n("qc")), ("qg", self.n("qg"))))
        with self.k.nc.Block() as block:
            for e in ENGS:
                lst = self.st[e]
                if lst:
                    getattr(block, e)(lambda eng, _l=tuple(lst): [f(eng) for f in _l])



def host_constants():
    c = {}
    pos = np.arange(S, dtype=np.float32)
    freq = np.arange(0, HD, 2, dtype=np.float32) / HD
    inv = (1.0 / 10000.0 ** freq).astype(np.float32)
    angrep = np.repeat(np.outer(pos, inv).astype(np.float32), 2, axis=-1)
    cos_o = np.cos(angrep).T.astype(np.float32)
    sin_o = np.sin(angrep).T.astype(np.float32)
    c["cos_o"] = np.concatenate([cos_o, cos_o], 0)
    c["sin_o"] = np.concatenate([sin_o, sin_o], 0)
    freqs2 = (1.0 / 100000.0 ** (np.arange(0, RP, 2, dtype=np.float32) / RP)).astype(np.float32)
    ang2rep = np.repeat(np.outer(pos, freqs2).astype(np.float32), 2, axis=-1)
    cos2 = np.cos(ang2rep).T.astype(np.float32)
    sin2 = np.sin(ang2rep).T.astype(np.float32)
    ones32 = np.ones((32, S), np.float32)
    zero32 = np.zeros((32, S), np.float32)
    c["cs2c"] = np.concatenate([ones32, cos2, ones32, cos2], 0)
    c["cs2s"] = np.concatenate([zero32, sin2, zero32, sin2], 0)
    r64 = np.zeros((HD, HD), np.float32)
    for i in range(0, HD, 2):
        r64[i, i + 1] = -1.0
        r64[i + 1, i] = 1.0
    Ro = np.zeros((128, 128), np.float32)
    Ro[0:64, 0:64] = r64
    Ro[64:128, 64:128] = r64
    c["Ro"] = np.ascontiguousarray(Ro.T).astype(bfdt)
    r32 = np.zeros((RP, RP), np.float32)
    for i in range(0, RP, 2):
        r32[i, i + 1] = -1.0
        r32[i + 1, i] = 1.0
    Rm = np.zeros((128, 128), np.float32)
    Rm[32:64, 32:64] = r32
    Rm[96:128, 96:128] = r32
    c["Rm"] = np.ascontiguousarray(Rm.T).astype(bfdt)
    c["idn_b"] = np.eye(128, dtype=np.float32).astype(bfdt)
    c["idn_f"] = np.eye(128, dtype=np.float32)
    c["ones_f"] = np.ones((128, 128), np.float32)
    c["ones_b"] = np.ones((128, 128), np.float32).astype(bfdt)
    u = np.zeros((128, 128), np.float32)
    for kk in range(128):
        u[kk, :kk] = 1.0
    c["U01"] = u.astype(bfdt)
    c["negI"] = (NEGBIG * np.eye(128, dtype=np.float32)).astype(bfdt)
    ind = np.zeros((128, 2), np.float32)
    ind[0:64, 0] = 1.0
    ind[64:128, 1] = 1.0
    c["indAB"] = ind.astype(bfdt)
    ind2 = np.zeros((96, 2), np.float32)
    ind2[0:64, 0] = 1.0
    ind2[32:96, 1] = 1.0
    c["indMk"] = ind2.astype(bfdt)
    oh = np.zeros((E, E * 128), np.float32)
    for e in range(E):
        oh[e, e * 128:(e + 1) * 128] = 1.0
    c["onehot"] = oh
    return c


CONST_SPECS = {
    "cos_o": ([128, S], F32), "sin_o": ([128, S], F32),
    "cs2c": ([128, S], F32), "cs2s": ([128, S], F32),
    "Ro": ([128, 128], BF16), "Rm": ([128, 128], BF16),
    "idn_b": ([128, 128], BF16), "idn_f": ([128, 128], F32),
    "ones_f": ([128, 128], F32), "ones_b": ([128, 128], BF16),
    "U01": ([128, 128], BF16), "negI": ([128, 128], BF16),
    "indAB": ([128, 2], BF16), "indMk": ([96, 2], BF16),
    "onehot": ([E, E * 128], F32),
}

INPUT_SPECS = {
    "x": ([S, D], F32),
    "wq_a": ([NCH, 128, D], F32), "wk_a": ([NCH, 128, D], F32),
    "wv_a": ([NCH, 128, D], F32), "wo_a": ([H, HD, D], F32),
    "bq_a": ([NCH, 128, 1], F32), "bk_a": ([NCH, 128, 1], F32),
    "bv_a": ([1, D], F32), "bo_a": ([NCH, 128, 1], F32),
    "wln1": ([NCH, 128, 1], F32), "wln2": ([NCH, 128, 1], F32),
    "wq_m": ([NCH, 128, D], F32), "wkva": ([NCH, 128, RLAT + RP], F32),
    "wkvb": ([RLAT, H * 32], F32), "wo_m": ([NCH, 96, D], F32),
    "wgate": ([NCH, 128, E], F32), "ebias": ([1, E], F32),
    "wg": ([E, NCH, 128, II], F32), "wu": ([E, NCH, 128, II], F32),
    "wd": ([E, 4, 128, D], F32),
    "swg": ([NCH, 128, II], F32), "swu": ([NCH, 128, II], F32),
    "swd": ([4, 128, D], F32),
}

SCRATCH_SPECS = {
    "xn_s": ([NCH, 128, S], BF16),
    "qo_s": ([NCH, 128, S], BF16), "ko_s": ([NCH, 128, S], BF16),
    "vo_s": ([S, H * 65], BF16), "A_s": ([NCH, 128, S], F32),
    "qm_s": ([NCH, 128, S], BF16), "km_s": ([NCH, 96, S], BF16),
    "kt_s": ([S, H * 33], BF16), "a_s": ([NCH, 128, S], F32),
    "wgb_s": ([E + 1, 128, NCH * II], BF16), "wub_s": ([E + 1, 128, NCH * II], BF16),
    "wdb_s": ([E + 1, 128, 4 * D], BF16),
    "rb1_scr": ([4, TCH], F32), "rb2_scr": ([4, TCH], F32),
    "cb_scr": ([8, 1024], F32),
}



def build(upto=99, dbg=()):
    nc = bass.Bass("TRN2", target_bir_lowering=False)
    es = ExitStack()

    T = {}
    for n, (shp, dt) in INPUT_SPECS.items():
        T[n] = nc.dram_tensor(n, shp, dt, kind="ExternalInput")
    for n, (shp, dt) in CONST_SPECS.items():
        T[n] = nc.dram_tensor(n, shp, dt, kind="ExternalInput")
    for n, (shp, dt) in SCRATCH_SPECS.items():
        kind = "ExternalOutput" if n in dbg else "Internal"
        T[n] = nc.dram_tensor(n, shp, dt, kind=kind)
    T["out"] = nc.dram_tensor("out", [S, D], F32, kind="ExternalOutput")
    if "nb_dump" in dbg:
        T["nb_dump"] = nc.dram_tensor("nb_dump", [128, 2 * NCH], F32, kind="ExternalOutput")

    sems = {}
    for s in ("pe", "act", "dve", "gp"):
        sems[s] = es.enter_context(nc.semaphore(s))
    for q in QUEUES:
        for j in range(NSLOT):
            sems[f"{q}{j}"] = es.enter_context(nc.semaphore(f"{q}{j}"))
    k = K(nc, sems)

    P = {}
    for n in ("Ro", "Rm", "idn_b", "idn_f", "ones_f", "ones_b", "U01", "negI",
              "indAB", "indMk"):
        shp, dt = CONST_SPECS[n]
        P[n] = es.enter_context(nc.sbuf_tensor(n + "_c", shp, dt))
    for n in ("bq_a", "bk_a", "bo_a", "wln1", "wln2"):
        P[n] = es.enter_context(nc.sbuf_tensor(n + "_v", [128, NCH], F32))
    P["bvb"] = es.enter_context(nc.sbuf_tensor("bvb", [128, D], F32))
    P["ebb"] = es.enter_context(nc.sbuf_tensor("ebb", [128, 4 * E], F32))
    P["negB1"] = es.enter_context(nc.sbuf_tensor("negB1", [128, NCH], F32))
    P["negB2"] = es.enter_context(nc.sbuf_tensor("negB2", [128, NCH], F32))
    P["nbrow"] = es.enter_context(nc.sbuf_tensor("nbrow", [1, 2 * NCH], F32))
    P["epsv"] = es.enter_context(nc.sbuf_tensor("epsv", [128, 1], F32))

    on = (lambda i: i in upto) if isinstance(upto, (set, frozenset, tuple, list)) \
        else (lambda i: upto >= i)
    ph_init(k, T, P)
    xn_es = ExitStack()
    xn_f = xn_es.enter_context(nc.sbuf_tensor("xn_f", [128, NCH * S], BF16))
    if on(1):
        ph0(k, T, P, xn_f)
    if on(2):
        ph1a(k, T, P, xn_f)
    xn_es.close()
    if on(3):
        ph1b(k, T, P)
    if on(4):
        ph2a(k, T, P)
    if on(5):
        ph2b(k, T, P)
    if on(6):
        ph3(k, T, P)
    if "nb_dump" in dbg:
        p = Ph(k)
        if True:
            p.em("sync", lambda e: e.dma_start(T["nb_dump"].ap()[:, 0:NCH], P["negB1"][:, :]),
                 inc=(("qs", 16),))
            p.em("sync", lambda e: e.dma_start(T["nb_dump"].ap()[:, NCH:2 * NCH], P["negB2"][:, :]),
                 inc=(("qs", 16),))
        p.em("sync", lambda e: e.nop(), wait=(("qs", p.n("qs")),))
        p.close()
    ph_fini(k)
    es.close()
    return nc


def ph_fini(k):
    p = Ph(k)
    p.em("scalar", lambda e: e.nop(),
         wait=(("qs", p.n("qs")), ("qg", p.n("qg")), ("qc", p.n("qc"))))
    p.close()



def ph_init(k, T, P):
    p = Ph(k)
    for n in ("Ro", "Rm", "idn_b", "idn_f", "ones_f", "ones_b", "U01", "negI",
              "indAB", "indMk"):
        shp = CONST_SPECS[n][0]
        p.em("sync", lambda e, _n=n, _s=shp: e.dma_start(
            P[_n][0:_s[0], 0:_s[1]], T[_n].ap()), inc=(("qs", 16),))
    for n in ("bq_a", "bk_a", "bo_a", "wln1", "wln2"):
        for c in range(NCH):
            p.em("sync", lambda e, _n=n, _c=c: e.dma_start(
                P[_n][:, _c:_c + 1], T[_n].ap()[_c]), inc=(("qs", 16),))
    p.em("vector", lambda e: e.memset(P["epsv"][:, 0:1], EPS), inc=(("dve", 1),))
    les = ExitStack()
    bvrow = les.enter_context(k.nc.sbuf_tensor("bvrow", [1, D], F32))
    ebrow = les.enter_context(k.nc.sbuf_tensor("ebrow", [1, 4 * E], F32))
    p.em("sync", lambda e: e.dma_start(bvrow[0:1, :], T["bv_a"].ap()), inc=(("qs", 16),))
    p.em("sync", lambda e: e.dma_start(ebrow[0:1, 0:E], T["ebias"].ap()), inc=(("qs", 16),))
    ld = p.n("qs")
    for j in range(1, 4):
        p.em("vector", lambda e, _j=j: e.tensor_copy(
            ebrow[0:1, _j * E:(_j + 1) * E], ebrow[0:1, 0:E]),
            wait=(("qs", ld), ("dve", p.n("dve"))), inc=(("dve", 1),))
    dv = p.n("dve")
    pb = les.enter_context(k.nc.psum_tensor("pb_init", [128, D], F32))
    pb2 = les.enter_context(k.nc.psum_tensor("pb2_init", [128, 4 * E], F32))
    for half in range(2):
        p.em("tensor", lambda e, _h=half: e.matmul(
            pb[:, _h * 512:(_h + 1) * 512], P["ones_f"][0:1, :],
            bvrow[0:1, _h * 512:(_h + 1) * 512], start=True, stop=True),
            wait=(("qs", ld),), inc=(("pe", 1),))
    p.em("tensor", lambda e: e.matmul(pb2[:, :], P["ones_f"][0:1, :], ebrow[0:1, :],
                                      start=True, stop=True),
         wait=(("dve", dv),), inc=(("pe", 1),))
    pe = p.n("pe")
    p.em("scalar", lambda e: e.copy(P["bvb"][:, 0:512], pb[:, 0:512]),
         wait=(("pe", pe),), inc=(("act", 1),))
    p.em("scalar", lambda e: e.copy(P["bvb"][:, 512:1024], pb[:, 512:1024]), inc=(("act", 1),))
    p.em("scalar", lambda e: e.copy(P["ebb"][:, :], pb2[:, :]), inc=(("act", 1),))
    p.close()
    les.close()



def ph0(k, T, P, xn_f):
    nc = k.nc
    p = Ph(k)
    les = ExitStack()
    sbl = lambda n, shp, dt: les.enter_context(nc.sbuf_tensor(n, shp, dt))
    psl = lambda n, shp, dt: les.enter_context(nc.psum_tensor(n, shp, dt))
    DP = 3
    xtok = [sbl(f"xtok{b}", [128, D], F32) for b in range(DP)]
    xntok = [sbl(f"xntok{b}", [128, D], BF16) for b in range(DP)]
    stats = [sbl(f"stats0{b}", [128, 12], F32) for b in range(2)]
    mv = [sbl(f"mv0{b}", [128, 2], F32) for b in range(2)]
    sqv = [sbl(f"sqv0{b}", [128, 1], F32) for b in range(2)]
    rinv = [sbl(f"rinv0{b}", [128, 1], F32) for b in range(2)]
    mneg = [sbl(f"mneg0{b}", [128, 1], F32) for b in range(2)]
    tp = [psl(f"tp0_{b}", [128, 128], BF16) for b in range(4)]

    xn_act = [0] * NT
    tr_done = [0] * NT
    ev_done = [0] * NT
    ag_done = [0, 0]
    sq_done = [0, 0]
    ri_used = [0, 0]
    tp_ev = [0] * 4

    for i in range(NT):
        b = i % DP
        sb_ = i % 2
        w = [("act", xn_act[i - DP])] if i >= DP else []
        p.em("sync", lambda e, _b=b, _i=i: e.dma_start(
            xtok[_b][:, :], T["x"].ap()[_i * 128:(_i + 1) * 128, :]),
            wait=w, inc=(("qs", 16),))
        ld = p.n("qs")
        wstats = [("qs", ld), ("dve", ag_done[sb_])]
        p.em("vector", lambda e, _b=b, _s=sb_: e.bn_stats(stats[_s][:, 0:6], xtok[_b][:, 0:512]),
             wait=wstats, inc=(("dve", 1),))
        p.em("vector", lambda e, _b=b, _s=sb_: e.bn_stats(stats[_s][:, 6:12], xtok[_b][:, 512:1024]),
             inc=(("dve", 1),))
        bs = p.n("dve")
        p.em("vector", lambda e, _s=sb_: e.bn_aggr(mv[_s][:, :], stats[_s][:, :]),
             wait=(("dve", bs), ("act", sq_done[sb_])), inc=(("dve", 1),))
        ag_done[sb_] = p.n("dve")
        p.em("scalar", lambda e, _s=sb_: e.activation(sqv[_s][:, 0:1], mv[_s][:, 1:2], AF.Sqrt,
                                                      bias=P["epsv"][:, 0:1]),
             wait=(("dve", ag_done[sb_]), ("dve", ri_used[sb_])), inc=(("act", 1),))
        sq_done[sb_] = p.n("act")
        p.em("vector", lambda e, _s=sb_: e.reciprocal(rinv[_s][:, 0:1], sqv[_s][:, 0:1]),
             wait=(("act", sq_done[sb_]), ("act", ri_used[sb_])), inc=(("dve", 1),))
        p.em("vector", lambda e, _s=sb_: e.tensor_scalar(mneg[_s][:, 0:1], mv[_s][:, 0:1],
                                                         rinv[_s][:, 0:1], -1.0,
                                                         op0=ALU.mult, op1=ALU.mult),
             wait=(("dve", p.n("dve")),), inc=(("dve", 1),))
        mn = p.n("dve")
        wxn = [("dve", mn)]
        if i >= DP:
            wxn.append(("pe", tr_done[i - DP]))
        p.em("scalar", lambda e, _b=b, _s=sb_: e.activation(
            xntok[_b][:, :], xtok[_b][:, :], AF.Identity,
            bias=mneg[_s][:, 0:1], scale=rinv[_s][:, 0:1]), wait=wxn, inc=(("act", 1),))
        xn_act[i] = p.n("act")
        ri_used[sb_] = xn_act[i]
        for c in range(NCH):
            tb = c % 4
            wtr = [("act", xn_act[i]), ("dve", tp_ev[tb])]
            p.em("tensor", lambda e, _b=b, _c=c, _tb=tb: e.transpose(
                tp[_tb][:, :], xntok[_b][:, _c * 128:(_c + 1) * 128], P["idn_b"][:, :]),
                wait=wtr, inc=(("pe", 1),))
            p.em("vector", lambda e, _c=c, _i=i, _tb=tb: e.tensor_copy(
                xn_f[:, _c * S + _i * 128: _c * S + (_i + 1) * 128], tp[_tb][:, :]),
                wait=(("pe", p.n("pe")),), inc=(("dve", 1),))
            tp_ev[tb] = p.n("dve")
        tr_done[i] = p.n("pe")
        ev_done[i] = p.n("dve")
        p.em("gpsimd", lambda e, _i=i: e.dma_start(
            T["xn_s"].ap()[:, :, _i * 128:(_i + 1) * 128].rearrange("c p t -> p c t"),
            xn_f[:, :].rearrange("p (c s) -> p c s", s=S)[:, :, _i * 128:(_i + 1) * 128]),
            wait=(("dve", ev_done[i]),), inc=(("qg", 16),))
    p.close()
    les.close()


def ph1a(k, T, P, xn_f):
    nc = k.nc
    p = Ph(k)
    les = ExitStack()
    sbl = lambda n, shp, dt: les.enter_context(nc.sbuf_tensor(n, shp, dt))
    psl = lambda n, shp, dt: les.enter_context(nc.psum_tensor(n, shp, dt))

    wsb = {}
    for n, dn in (("wq", "wq_a"), ("wk", "wk_a"), ("wv", "wv_a")):
        wsb[n] = sbl(n + "_sb", [128, NCH * D], BF16)
        p.em("gpsimd", lambda e, _n=n, _d=dn: e.dma_start(
            wsb[_n][:, :].rearrange("p (c d) -> p c d", d=D),
            T[_d].ap().rearrange("c p d -> p c d")), inc=(("qg", 16),))
    wld = p.n("qg")
    ctf = sbl("ctf", [128, S], F32)
    stf = sbl("stf", [128, S], F32)
    p.em("sync", lambda e: e.dma_start(ctf[:, :], T["cos_o"].ap()), inc=(("qs", 16),))
    p.em("sync", lambda e: e.dma_start(stf[:, :], T["sin_o"].ap()), inc=(("qs", 16),))
    tabld = p.n("qs")

    DP = 4
    qraw = [sbl(f"qraw{b}", [128, TCH], BF16) for b in range(DP)]
    qc_t = [sbl(f"qc_t{b}", [128, TCH], F32) for b in range(DP)]
    qr_t = [sbl(f"qr_t{b}", [128, TCH], F32) for b in range(DP)]
    qacc = [sbl(f"qacc{b}", [128, S], BF16) for b in range(2)]
    sq_t = sbl("sq_t", [128, S], BF16)
    vo_b = [sbl(f"vo_b{b}", [128, H * 65], BF16) for b in range(2)]
    nmax = sbl("nmax1w", [1, 2 * NCH * NTC], F32)
    nbred = sbl("nbred", [1, 2 * NCH], F32)
    nbsq = sbl("nbsq", [1, 2 * NCH], F32)

    pj = [psl(f"pj{b}", [128, TCH], F32) for b in range(3)]
    rot = [psl(f"rot{b}", [128, TCH], F32) for b in range(3)]
    vp = psl("vp0", [128, TCH], F32)
    nb_sh = psl("nb_sh", [128, TCH], F32)

    for b in range(2):
        p.em("vector", lambda e, _b=b: e.memset(
            vo_b[_b][:, :].rearrange("p (h c) -> p h c", c=65)[:, :, 64:65], 1.0),
            inc=(("dve", 1),))

    pj_evac = [0, 0, 0]
    raw_used_tt = [0] * DP
    rot_used = [0, 0, 0]
    qadd_gp = [("gp", 0)] * DP
    qacc_st = [0, 0]
    qacc_sq = [0, 0]
    v_tt = 0
    vo_stq = [0, 0]
    prev_red = 0
    prev_nrm = 0
    sq_nrm_rd = 0

    for wi, (wn, bn, dst) in enumerate((("wq", "bq_a", "qo_s"), ("wk", "bk_a", "ko_s"))):
        for o in range(NCH):
            go = wi * NCH + o
            ab = go % 2
            for tc in range(NTC):
                g = tc
                ob = g % DP
                pb = g % 3
                wproj = [("qg", wld), ("act", pj_evac[pb])]
                for c in range(NCH):
                    p.em("tensor", lambda e, _c=c, _o=o, _pb=pb, _wn=wn, _tc=tc: e.matmul(
                        pj[_pb][:, :], wsb[_wn][:, _c * D + _o * 128:_c * D + _o * 128 + 128],
                        xn_f[:, _c * S + _tc * TCH:_c * S + (_tc + 1) * TCH],
                        start=(_c == 0), stop=(_c == NCH - 1)),
                        wait=wproj if c == 0 else (),
                        inc=(("pe", 1),) if c == NCH - 1 else ())
                pe_pj = p.n("pe")
                p.em("scalar", lambda e, _ob=ob, _pb=pb, _bn=bn, _o=o: e.activation(
                    qraw[_ob][:, :], pj[_pb][:, :], AF.Identity,
                    bias=P[_bn][:, _o:_o + 1]),
                    wait=(("pe", pe_pj), ("dve", raw_used_tt[ob])), inc=(("act", 1),))
                pj_evac[pb] = p.n("act")
                act_raw = p.n("act")
                p.em("tensor", lambda e, _ob=ob, _pb=pb: e.matmul(
                    rot[_pb][:, :], P["Ro"][:, :], qraw[_ob][:, :], start=True, stop=True),
                    wait=(("act", act_raw), ("dve", rot_used[pb])), inc=(("pe", 1),))
                pe_rot = p.n("pe")
                p.em("vector", lambda e, _ob=ob, _tc=tc: e.tensor_tensor(
                    qc_t[_ob][:, :], qraw[_ob][:, :],
                    ctf[:, _tc * TCH:(_tc + 1) * TCH], op=ALU.mult),
                    wait=(("qs", tabld), ("act", act_raw), qadd_gp[ob]),
                    inc=(("dve", 1),))
                raw_used_tt[ob] = p.n("dve")
                p.em("vector", lambda e, _ob=ob, _pb=pb, _tc=tc: e.tensor_tensor(
                    qr_t[_ob][:, :], rot[_pb][:, :],
                    stf[:, _tc * TCH:(_tc + 1) * TCH], op=ALU.mult),
                    wait=(("pe", pe_rot), qadd_gp[ob]), inc=(("dve", 1),))
                rot_used[pb] = p.n("dve")
                dv2 = p.n("dve")
                wadd = [("dve", dv2)]
                if tc == 0:
                    wadd += [("qc", qacc_st[ab]), ("act", qacc_sq[ab])]
                p.em("gpsimd", lambda e, _ob=ob, _ab=ab, _tc=tc: e.tensor_tensor(
                    qacc[_ab][:, _tc * TCH:(_tc + 1) * TCH],
                    qc_t[_ob][:, :], qr_t[_ob][:, :], op=ALU.add),
                    wait=wadd, inc=(("gp", 1),))
                qadd_gp[ob] = ("gp", p.n("gp"))
            gp_all = p.n("gp")
            p.em("scalar", lambda e, _dst=dst, _o=o, _ab=ab: e.dma_start(
                T[_dst].ap()[_o], qacc[_ab][:, :]),
                wait=(("gp", gp_all),), inc=(("qc", 16),))
            qacc_st[ab] = p.n("qc")
            p.em("scalar", lambda e, _ab=ab: e.activation(
                sq_t[:, :], qacc[_ab][:, :], AF.Square),
                wait=(("gp", gp_all), ("pe", sq_nrm_rd)), inc=(("act", 1),))
            qacc_sq[ab] = p.n("act")
            sq_act = p.n("act")
            for tc in range(NTC):
                p.em("tensor", lambda e, _tc=tc: e.matmul(
                    nb_sh[0:1, :], P["ones_b"][:, 0:1],
                    sq_t[:, _tc * TCH:(_tc + 1) * TCH], start=True, stop=True),
                    wait=(("act", sq_act), ("dve", prev_red)) if tc == 0 else
                         (("dve", prev_red),), inc=(("pe", 1),))
                prev_nrm = p.n("pe")
                col = go * NTC + tc
                p.em("vector", lambda e, _col=col: e.tensor_reduce(
                    nmax[0:1, _col:_col + 1], nb_sh[0:1, :], axis=AX.X, op=ALU.max),
                    wait=(("pe", prev_nrm),), inc=(("dve", 1),))
                prev_red = p.n("dve")
            sq_nrm_rd = prev_nrm
    for t in range(NT):
        vb = t % 2
        for half in range(2):
            wv0 = [("qg", wld), ("dve", v_tt)]
            for c in range(NCH):
                p.em("tensor", lambda e, _c=c, _t=t, _h=half: e.matmul(
                    vp[:, :], xn_f[:, _c * S + _t * 128:_c * S + (_t + 1) * 128],
                    wsb["wv"][:, _c * D + _h * TCH:_c * D + (_h + 1) * TCH],
                    start=(_c == 0), stop=(_c == NCH - 1)),
                    wait=wv0 if c == 0 else (),
                    inc=(("pe", 1),) if c == NCH - 1 else ())
            pe_v = p.n("pe")
            p.em("vector", lambda e, _vb=vb, _h=half: e.tensor_tensor(
                vo_b[_vb][:, :].rearrange("p (h c) -> p h c", c=65)[:, _h * 8:(_h + 1) * 8, 0:64],
                vp[:, :].rearrange("p (h c) -> p h c", c=64),
                P["bvb"][:, _h * TCH:(_h + 1) * TCH].rearrange("p (h c) -> p h c", c=64),
                op=ALU.add),
                wait=(("pe", pe_v), ("qc", vo_stq[vb])), inc=(("dve", 1),))
            v_tt = p.n("dve")
        p.em("scalar", lambda e, _vb=vb, _t=t: e.dma_start(
            T["vo_s"].ap()[_t * 128:(_t + 1) * 128, :], vo_b[_vb][:, :]),
            wait=(("dve", v_tt),), inc=(("qc", 16),))
        vo_stq[vb] = p.n("qc")

    finalize_bounds(p, P, nmax, nbred, nbsq, nb_sh, "negB1", 0)
    p.close()
    les.close()


def finalize_bounds(p, P, nmax, nbred, nbsq, nbp, dstname, rowoff):
    for col in range(2 * NCH):
        p.em("vector", lambda e, _c=col: e.tensor_reduce(
            nbred[0:1, _c:_c + 1], nmax[0:1, _c * NTC:(_c + 1) * NTC], axis=AX.X, op=ALU.max),
            wait=(("dve", p.n("dve")),) if col == 0 else (), inc=(("dve", 1),))
    p.em("scalar", lambda e: e.activation(nbsq[0:1, :], nbred[0:1, :], AF.Sqrt),
         wait=(("dve", p.n("dve")),), inc=(("act", 1),))
    a0 = p.n("act")
    for o in range(NCH):
        p.em("vector", lambda e, _o=o, _ro=rowoff: e.scalar_tensor_tensor(
            P["nbrow"][0:1, _ro * NCH + _o:_ro * NCH + _o + 1],
            nbsq[0:1, _o:_o + 1], -0.125, nbsq[0:1, NCH + _o:NCH + _o + 1],
            op0=ALU.mult, op1=ALU.mult),
            wait=(("act", a0),), inc=(("dve", 1),))
    dvp = p.n("dve")
    p.em("tensor", lambda e, _ro=rowoff: e.matmul(
        nbp[0:128, 0:NCH], P["ones_f"][0:1, :], P["nbrow"][0:1, _ro * NCH:(_ro + 1) * NCH],
        start=True, stop=True),
        wait=(("dve", dvp),), inc=(("pe", 1),))
    p.em("scalar", lambda e: e.copy(P[dstname][:, :], nbp[0:128, 0:NCH]),
         wait=(("pe", p.n("pe")),), inc=(("act", 1),))


def ph1b(k, T, P):
    nc = k.nc
    p = Ph(k)
    les = ExitStack()
    sbl = lambda n, shp, dt: les.enter_context(nc.sbuf_tensor(n, shp, dt))
    psl = lambda n, shp, dt: les.enter_context(nc.psum_tensor(n, shp, dt))

    wo_sb = sbl("wo_sb", [64, H * D], BF16)
    p.em("gpsimd", lambda e: e.dma_start(
        wo_sb[:, :].rearrange("p (h d) -> p h d", d=D),
        T["wo_a"].ap().rearrange("h p d -> p h d")), inc=(("qg", 16),))
    wld = p.n("qg")

    ko_sb = sbl("ko_sb", [128, NCH * S], BF16)
    vo_sb = sbl("vo_sb", [128, NT * H * 65], BF16)
    qp = [sbl(f"qp{b}", [128, TCH], BF16) for b in range(2)]
    pt = [sbl(f"pt{b}", [128, 1024], BF16) for b in range(3)]
    rcb = sbl("rcb", [128, TCH], F32)
    yn_sb = sbl("yn_sb", [64, H * TCH], BF16)
    Ab = [sbl(f"Ab{b}", [128, TCH], F32) for b in range(2)]

    sAB = psl("sAB", [128, 2048], F32)
    yA = [psl(f"yA{b}", [128, TCH], F32) for b in range(2)]
    yB = [psl(f"yB{b}", [128, TCH], F32) for b in range(2)]

    qp_used = [0, 0]
    yn_used = 0
    exp_last = 0
    exp_kt = {}
    av_kt = {}
    rb_read = [0, 0]
    rb_slot = [0, 0, 0, 0]
    rc_out = 0
    wo_bank = [("act", 0)] * 4
    yn_tt = [0] * NCH
    ab_st = [0, 0]

    prev_ld = p.n("qs")
    for qc in range(NTC):
        nk = 4 * qc + 4
        for c in range(NCH):
            p.em("sync", lambda e, _c=c, _qc=qc: e.dma_start(
                ko_sb[:, _c * S + _qc * TCH:_c * S + (_qc + 1) * TCH],
                T["ko_s"].ap()[_c][:, _qc * TCH:(_qc + 1) * TCH]),
                wait=(("qs", prev_ld),) if c == 0 else (), inc=(("qs", 16),))
        p.em("sync", lambda e, _qc=qc: e.dma_start(
            vo_sb[:, _qc * 4 * H * 65:(_qc + 1) * 4 * H * 65].rearrange(
                "p (k c) -> p k c", c=H * 65),
            T["vo_s"].ap()[_qc * TCH:(_qc + 1) * TCH, :].rearrange(
                "(k p) c -> p k c", p=128)), inc=(("qs", 16),))
        ld = p.n("qs")
        prev_ld = ld

        for c in range(NCH):
            hA, hB = 2 * c, 2 * c + 1
            cb_ = c % 2
            p.em("sync", lambda e, _c=c, _qc=qc, _b=c % 2: e.dma_start(
                qp[_b][:, :], T["qo_s"].ap()[_c][:, _qc * TCH:(_qc + 1) * TCH]),
                wait=(("pe", qp_used[c % 2]),), inc=(("qs", 16),))
            ld_q = p.n("qs")
            nkt = 4 * qc + 4

            def _scores(kt):
                r = kt % 2
                rp = kt % 3
                edge = kt >= 4 * qc
                jj = kt - 4 * qc
                qoff = 128 * jj if edge else 0
                N = 512 - qoff
                wsc = [("act", exp_kt.get(kt - 2, exp_last))]
                if kt == 0:
                    wsc += [("qs", ld), ("qs", ld_q), ("qg", wld)]
                for hh, (rlo, sbase) in enumerate(((0, 0), (64, 512))):
                    p.em("tensor", lambda e, _c=c, _kt=kt, _rlo=rlo, _r=r, _b=cb_,
                         _sb=sbase, _n=N, _qo=qoff, _e=edge: e.matmul(
                        sAB[:, _r * 1024 + _sb:_r * 1024 + _sb + _n],
                        ko_sb[_rlo:_rlo + 64, _c * S + _kt * 128:_c * S + (_kt + 1) * 128],
                        qp[_b][_rlo:_rlo + 64, _qo:TCH],
                        start=True, stop=not _e,
                        tile_position=(_rlo, 0)),
                        wait=wsc if hh == 0 else (),
                        inc=(("pe", 1),) if (hh == 1 and not edge) else ())
                    if edge:
                        p.em("tensor", lambda e, _r=r, _sb=sbase: e.matmul(
                            sAB[:, _r * 1024 + _sb:_r * 1024 + _sb + 128],
                            P["negI"][:, :], P["U01"][:, :],
                            start=False, stop=True),
                            inc=(("pe", 1),) if hh == 1 else ())
                pe_sc = p.n("pe")
                wexp = [("pe", pe_sc)]
                if kt >= 3:
                    wexp.append(("pe", av_kt[kt - 3]))
                p.em("scalar", lambda e, _r=r, _rp=rp, _X=N, _c=c: e.activation(
                    pt[_rp][:, :].rearrange("p (g x) -> p g x", g=2)[:, :, 0:_X],
                    sAB[:, _r * 1024:(_r + 1) * 1024].rearrange(
                        "p (g x) -> p g x", g=2)[:, :, 0:_X],
                    AF.Exp, bias=P["negB1"][:, _c:_c + 1], scale=0.125),
                    wait=wexp, inc=(("act", 1),))
                exp_kt[kt] = p.n("act")

            def _av(kt):
                rp = kt % 3
                edge = kt >= 4 * qc
                jj = kt - 4 * qc
                qoff = 128 * jj if edge else 0
                N = 512 - qoff
                wav = [("act", exp_kt[kt])]
                if kt == 0:
                    wav += [("dve", rb_read[cb_])]
                for hh, (yps, ybase, ytp, sbase, h) in enumerate(((yA[cb_], 0, (0, 0), 0, hA), (yB[cb_], 0, (0, 0), 512, hB))):
                    p.em("tensor", lambda e, _y=yps, _rp=rp, _sb=sbase,
                         _n=N, _qo=qoff, _kt=kt, _h=h, _yb=ybase, _yt=ytp,
                         _st=(kt == 0), _sp=(kt == nkt - 1): e.matmul(
                        _y[_yb:_yb + 65, _qo:_qo + _n],
                        vo_sb[:, _kt * H * 65 + _h * 65:_kt * H * 65 + (_h + 1) * 65],
                        pt[_rp][:, _sb:_sb + _n],
                        start=_st, stop=_sp, tile_position=_yt),
                        wait=wav if (hh == 0 or kt == 0) else (),
                        inc=(("pe", 1),) if hh == 1 else ())
                av_kt[kt] = p.n("pe")

            for kt in range(nkt):
                _scores(kt)
                if kt >= 2:
                    _av(kt - 2)
            if nkt >= 2:
                _av(nkt - 2)
            _av(nkt - 1)
            exp_last = exp_kt[nkt - 1]
            qp_used[cb_] = p.n("pe")
            pe_av_all = p.n("pe")
            for hh2, (yps, h) in enumerate(((yA[cb_], hA), (yB[cb_], hB))):
                slot = cb_ * 2 + hh2
                p.em("vector", lambda e, _y=yps: e.reciprocal(
                    rcb[64:65, :], _y[64:65, :]),
                    wait=(("pe", pe_av_all), ("qg", rc_out)), inc=(("dve", 1),))
                dv_rc = p.n("dve")
                p.em("gpsimd", lambda e, _s=slot: e.dma_start(
                    T["rb1_scr"].ap()[_s:_s + 1, :], rcb[64:65, :]),
                    wait=(("dve", dv_rc), ("qg", rb_slot[slot])), inc=(("qg", 16),))
                rc_out = p.n("qg")
                p.em("gpsimd", lambda e, _s=slot: e.dma_start(
                    rcb[0:64, :], bass.AP(T["rb1_scr"], _s * TCH, [[0, 64], [1, TCH]])),
                    wait=(("dve", rb_read[cb_]), ("qg", rc_out)), inc=(("qg", 16),))
                rb_in = p.n("qg")
                rb_slot[slot] = rb_in
                p.em("vector", lambda e, _y=yps, _h=h: e.tensor_tensor(
                    yn_sb[0:64, _h * TCH:(_h + 1) * TCH], _y[0:64, :], rcb[0:64, :],
                    op=ALU.mult),
                    wait=(("qg", rb_in), ("pe", yn_used)), inc=(("dve", 1),))
                rb_read[cb_] = p.n("dve")
                if hh2 == 1:
                    yn_tt[c] = p.n("dve")
        for o in range(NCH):
            r4 = o % 4
            ob = o % 2
            wwo = [wo_bank[r4], ("act", exp_last), ("dve", yn_tt[5])]
            for h in range(H):
                wh = wwo if h == 0 else \
                    ([("dve", yn_tt[h // 2])] if h // 2 >= 6 else ())
                p.em("tensor", lambda e, _o=o, _h=h, _r4=r4: e.matmul(
                    sAB[:, _r4 * 512:(_r4 + 1) * 512],
                    wo_sb[0:64, _h * D + _o * 128:_h * D + _o * 128 + 128],
                    yn_sb[0:64, _h * TCH:(_h + 1) * TCH],
                    start=(_h == 0), stop=(_h == H - 1)),
                    wait=wh, inc=(("pe", 1),) if h == H - 1 else ())
            pe_wo = p.n("pe")
            p.em("scalar", lambda e, _o=o, _ob=ob, _r4=r4: e.activation(
                Ab[_ob][:, :], sAB[:, _r4 * 512:(_r4 + 1) * 512], AF.Identity,
                bias=P["bo_a"][:, _o:_o + 1]),
                wait=(("pe", pe_wo), ("qc", ab_st[ob])), inc=(("act", 1),))
            wo_bank[r4] = ("act", p.n("act"))
            p.em("scalar", lambda e, _o=o, _ob=ob, _qc=qc: e.dma_start(
                T["A_s"].ap()[_o][:, _qc * TCH:(_qc + 1) * TCH], Ab[_ob][:, :]),
                wait=(("act", p.n("act")),), inc=(("qc", 16),))
            ab_st[ob] = p.n("qc")
        exp_last = p.n("act")
        yn_used = p.n("pe")
    p.close()
    les.close()


def ph2a(k, T, P):
    nc = k.nc
    p = Ph(k)
    les = ExitStack()
    sbl = lambda n, shp, dt: les.enter_context(nc.sbuf_tensor(n, shp, dt))
    psl = lambda n, shp, dt: les.enter_context(nc.psum_tensor(n, shp, dt))

    wq2_sb = sbl("wq2_sb", [128, NCH * D], BF16)
    wkva_sb = sbl("wkva_sb", [128, NCH * 160], BF16)
    wkvb_sb = sbl("wkvb_sb", [128, 512], BF16)
    p.em("gpsimd", lambda e: e.dma_start(
        wq2_sb[:, :].rearrange("p (c d) -> p c d", d=D),
        T["wq_m"].ap().rearrange("c p d -> p c d")), inc=(("qg", 16),))
    p.em("gpsimd", lambda e: e.dma_start(
        wkva_sb[:, :].rearrange("p (c d) -> p c d", d=160),
        T["wkva"].ap().rearrange("c p d -> p c d")), inc=(("qg", 16),))
    p.em("gpsimd", lambda e: e.dma_start(wkvb_sb[:, :], T["wkvb"].ap()), inc=(("qg", 16),))
    wld = p.n("qg")
    ct2f = sbl("ct2f", [128, S], F32)
    st2f = sbl("st2f", [128, S], F32)
    p.em("sync", lambda e: e.dma_start(ct2f[:, :], T["cs2c"].ap()), inc=(("qs", 16),))
    p.em("sync", lambda e: e.dma_start(st2f[:, :], T["cs2s"].ap()), inc=(("qs", 16),))
    tabld = p.n("qs")

    z1f = sbl("z1f", [128, NCH * S], BF16)
    xnc = [sbl(f"xnc{b}", [128, NCH * TCH], BF16) for b in range(2)]
    sqz = [sbl(f"sqz{b}", [128, TCH], BF16) for b in range(2)]
    rms1 = sbl("rms1", [1, TCH], F32)
    rinv1 = sbl("rinv1", [1, TCH], F32)
    rb1 = [sbl(f"rb1_{b}", [128, TCH], F32) for b in range(2)]
    DP = 4
    qraw = [sbl(f"qraw2{b}", [128, TCH], BF16) for b in range(DP)]
    qc_t = [sbl(f"qc_t2{b}", [128, TCH], F32) for b in range(DP)]
    qr_t = [sbl(f"qr_t2{b}", [128, TCH], F32) for b in range(DP)]
    qacc = [sbl(f"qacc2{b}", [128, S], BF16) for b in range(2)]
    sq_t = [sbl(f"sq_t2{b}", [128, TCH], BF16) for b in range(2)]
    kvl = [sbl(f"kvl{b}", [128, TCH], BF16) for b in range(2)]
    krr = sbl("krr", [128, TCH], BF16)
    kmb = [sbl(f"kmb{b}", [128, TCH], BF16) for b in range(2)]
    ktb = [sbl(f"ktb{b}", [128, H * 33], BF16) for b in range(2)]
    nmax = sbl("nmax2w", [1, 2 * NCH * NTC], F32)
    nbred = sbl("nbred2", [1, 2 * NCH], F32)
    nbsq = sbl("nbsq2", [1, 2 * NCH], F32)

    pj = [psl(f"pj2{b}", [128, TCH], F32) for b in range(3)]
    rot = [psl(f"rot2{b}", [128, TCH], F32) for b in range(3)]
    statb = psl("stat2", [128, TCH], F32)
    ktp = psl("ktp", [128, TCH], F32)

    for b in range(2):
        p.em("vector", lambda e, _b=b: e.memset(
            ktb[_b][:, :].rearrange("p (h c) -> p h c", c=33)[:, :, 32:33], 1.0),
            inc=(("dve", 1),))

    z1_done = [("dve", 0)] * NTC
    ldc = [0] * NCH
    statmm = [0, 0]
    sqrt_act = 0
    recip_dve = 0
    scrout = [0, 0]
    bcin = [0, 0]
    for tc in range(NTC):
        b = tc % 2
        for c in range(NCH):
            wx = []
            if c == 0 and tc >= 2:
                wx = [z1_done[tc - 2]]
            p.em("sync", lambda e, _c=c, _tc=tc, _b=b: e.dma_start(
                xnc[_b][:, _c * TCH:(_c + 1) * TCH],
                T["xn_s"].ap()[_c][:, _tc * TCH:(_tc + 1) * TCH]),
                wait=wx, inc=(("qs", 16),))
            ldc[c] = p.n("qs")
        for c in range(NCH):
            cb = c % 2
            p.em("scalar", lambda e, _c=c, _cb=cb, _b=b: e.activation(
                sqz[_cb][:, :], xnc[_b][:, _c * TCH:(_c + 1) * TCH], AF.Square),
                wait=(("qs", ldc[_c if False else c]), ("pe", statmm[cb])),
                inc=(("act", 1),))
            sq_a = p.n("act")
            wmm = [("act", sq_a)]
            if c == 0:
                wmm.append(("act", sqrt_act))
            p.em("tensor", lambda e, _c=c, _cb=cb: e.matmul(
                statb[0:1, :], P["ones_b"][:, 0:1], sqz[_cb][:, :],
                start=(_c == 0), stop=(_c == NCH - 1)),
                wait=wmm, inc=(("pe", 1),))
            statmm[cb] = p.n("pe")
        p.em("scalar", lambda e: e.activation(rms1[0:1, :], statb[0:1, :], AF.Sqrt,
                                              bias=P["epsv"][0:1, 0:1], scale=1.0 / D),
             wait=(("pe", statmm[(NCH - 1) % 2]), ("dve", recip_dve)), inc=(("act", 1),))
        sqrt_act = p.n("act")
        p.em("vector", lambda e: e.reciprocal(rinv1[0:1, :], rms1[0:1, :]),
             wait=[("act", sqrt_act)] + ([("qg", scrout[(tc - 1) % 2])] if tc >= 1 else []),
             inc=(("dve", 1),))
        recip_dve = p.n("dve")
        p.em("gpsimd", lambda e, _s=tc % 2: e.dma_start(
            T["rb1_scr"].ap()[_s:_s + 1, :], rinv1[0:1, :]),
            wait=[("dve", recip_dve)] + ([("qg", bcin[tc % 2])] if tc >= 2 else []),
            inc=(("qg", 16),))
        scrout[tc % 2] = p.n("qg")
        wbc = [("qg", scrout[tc % 2])]
        if tc >= 2:
            wbc.append(z1_done[tc - 2])
        p.em("gpsimd", lambda e, _b=b, _s=tc % 2: e.dma_start(
            rb1[_b][:, :], bass.AP(T["rb1_scr"], _s * TCH, [[0, 128], [1, TCH]])),
            wait=wbc, inc=(("qg", 16),))
        bcin[tc % 2] = p.n("qg")
        for c in range(NCH):
            wz = [("qg", bcin[tc % 2])] if c == 0 else []
            p.em("vector", lambda e, _c=c, _b=b, _tc=tc: e.scalar_tensor_tensor(
                z1f[:, _c * S + _tc * TCH:_c * S + (_tc + 1) * TCH],
                xnc[_b][:, _c * TCH:(_c + 1) * TCH],
                P["wln1"][:, _c:_c + 1], rb1[_b][:, :], op0=ALU.mult, op1=ALU.mult),
                wait=wz, inc=(("dve", 1),))
        z1_done[tc] = ("dve", p.n("dve"))

    pj_evac = [0, 0, 0]
    raw_used_tt = [0] * DP
    rot_used = [0, 0, 0]
    qadd_gp = [("gp", 0)] * DP
    qacc_st = [0, 0]
    qacc_sq = [0, 0]
    prev_red = 0
    nrm_pe = [0, 0]
    for o in range(NCH):
        ab = o % 2
        for tc in range(NTC):
            g = o * NTC + tc
            ob = g % DP
            pb = g % 3
            wproj = [("qg", wld), ("act", pj_evac[pb])]
            if o == 0:
                wproj.append(z1_done[tc])
            for c in range(NCH):
                p.em("tensor", lambda e, _c=c, _o=o, _pb=pb, _tc=tc: e.matmul(
                    pj[_pb][:, :], wq2_sb[:, _c * D + _o * 128:_c * D + _o * 128 + 128],
                    z1f[:, _c * S + _tc * TCH:_c * S + (_tc + 1) * TCH],
                    start=(_c == 0), stop=(_c == NCH - 1)),
                    wait=wproj if c == 0 else (),
                    inc=(("pe", 1),) if c == NCH - 1 else ())
            pe_pj = p.n("pe")
            p.em("scalar", lambda e, _ob=ob, _pb=pb: e.copy(
                qraw[_ob][:, :], pj[_pb][:, :]),
                wait=(("pe", pe_pj), ("dve", raw_used_tt[ob])), inc=(("act", 1),))
            pj_evac[pb] = p.n("act")
            act_raw = p.n("act")
            rb = g % 3
            p.em("tensor", lambda e, _ob=ob, _rb=rb: e.matmul(
                rot[_rb][:, :], P["Rm"][:, :], qraw[_ob][:, :], start=True, stop=True),
                wait=(("act", act_raw), ("dve", rot_used[rb])), inc=(("pe", 1),))
            pe_rot = p.n("pe")
            wtt = [("act", act_raw), qadd_gp[ob]]
            if g == 0:
                wtt.append(("qs", tabld))
            p.em("vector", lambda e, _ob=ob, _tc=tc: e.tensor_tensor(
                qc_t[_ob][:, :], qraw[_ob][:, :],
                ct2f[:, _tc * TCH:(_tc + 1) * TCH], op=ALU.mult),
                wait=wtt, inc=(("dve", 1),))
            raw_used_tt[ob] = p.n("dve")
            p.em("vector", lambda e, _ob=ob, _rb=rb, _tc=tc: e.tensor_tensor(
                qr_t[_ob][:, :], rot[_rb][:, :],
                st2f[:, _tc * TCH:(_tc + 1) * TCH], op=ALU.mult),
                wait=(("pe", pe_rot), qadd_gp[ob]), inc=(("dve", 1),))
            rot_used[rb] = p.n("dve")
            dv2 = p.n("dve")
            wadd = [("dve", dv2)]
            if tc == 0:
                wadd += [("qc", qacc_st[ab]), ("act", qacc_sq[ab])]
            p.em("gpsimd", lambda e, _ob=ob, _ab=ab, _tc=tc: e.tensor_tensor(
                qacc[_ab][:, _tc * TCH:(_tc + 1) * TCH],
                qc_t[_ob][:, :], qr_t[_ob][:, :], op=ALU.add),
                wait=wadd, inc=(("gp", 1),))
            qadd_gp[ob] = ("gp", p.n("gp"))
            gp_add = p.n("gp")
            sb = tc % 2
            p.em("scalar", lambda e, _ab=ab, _tc=tc, _sb=sb: e.activation(
                sq_t[_sb][:, :], qacc[_ab][:, _tc * TCH:(_tc + 1) * TCH], AF.Square),
                wait=(("gp", gp_add), ("pe", nrm_pe[sb])), inc=(("act", 1),))
            qacc_sq[ab] = p.n("act")
            p.em("tensor", lambda e, _sb=sb: e.matmul(
                ktp[0:1, :], P["ones_b"][:, 0:1], sq_t[_sb][:, :],
                start=True, stop=True),
                wait=(("act", p.n("act")), ("dve", prev_red)), inc=(("pe", 1),))
            nrm_pe[sb] = p.n("pe")
            col = o * NTC + tc
            p.em("vector", lambda e, _col=col: e.tensor_reduce(
                nmax[0:1, _col:_col + 1], ktp[0:1, :], axis=AX.X, op=ALU.max),
                wait=(("pe", nrm_pe[sb]),), inc=(("dve", 1),))
            prev_red = p.n("dve")
        gp_all = p.n("gp")
        p.em("scalar", lambda e, _o=o, _ab=ab: e.dma_start(
            T["qm_s"].ap()[_o], qacc[_ab][:, :]),
            wait=(("gp", gp_all), ("act", qacc_sq[ab])), inc=(("qc", 16),))
        qacc_st[ab] = p.n("qc")
    q_red_last = prev_red
    q_nrm_last = max(nrm_pe)

    kvl_act = 0
    krr_tt = 0
    kvp_bank = [0, 0]
    kmb_st = [0, 0]
    kmb_sq = [0, 0]
    kvp_ev = 0
    ktb_st = [0, 0]
    ktp_copy = 0
    first_ktok = True
    first_sq = True
    for tc in range(NTC):
        kb = tc % 2
        pb = tc % 3
        for c in range(NCH):
            p.em("tensor", lambda e, _c=c, _pb=pb, _tc=tc: e.matmul(
                pj[_pb][:, :], wkva_sb[:, _c * 160:_c * 160 + 128],
                z1f[:, _c * S + _tc * TCH:_c * S + (_tc + 1) * TCH],
                start=(_c == 0), stop=(_c == NCH - 1)),
                wait=(("act", pj_evac[pb]),) if c == 0 else (),
                inc=(("pe", 1),) if c == NCH - 1 else ())
        pe_kvl = p.n("pe")
        p.em("scalar", lambda e, _kb=kb, _pb=pb: e.copy(kvl[_kb][:, :], pj[_pb][:, :]),
             wait=(("pe", pe_kvl),), inc=(("act", 1),))
        pj_evac[pb] = p.n("act")
        kvl_act = p.n("act")
        for c in range(NCH):
            p.em("tensor", lambda e, _c=c, _tc=tc: e.matmul(
                rot[0][32:64, :], wkva_sb[:, _c * 160 + 128:_c * 160 + 160],
                z1f[:, _c * S + _tc * TCH:_c * S + (_tc + 1) * TCH],
                start=(_c == 0), stop=(_c == NCH - 1), tile_position=(0, 32)),
                wait=(("dve", rot_used[0]),) if c == 0 else (),
                inc=(("pe", 1),) if c == NCH - 1 else ())
        pe_kr = p.n("pe")
        p.em("scalar", lambda e: e.copy(krr[32:64, :], rot[0][32:64, :]),
             wait=(("pe", pe_kr), ("dve", krr_tt)), inc=(("act", 1),))
        act_krr = p.n("act")
        p.em("tensor", lambda e: e.matmul(
            rot[1][32:64, :], P["Rm"][32:64, 32:64], krr[32:64, :],
            start=True, stop=True, tile_position=(32, 32)),
            wait=(("act", act_krr), ("dve", rot_used[1])), inc=(("pe", 1),))
        pe_krot = p.n("pe")
        p.em("vector", lambda e, _tc=tc: e.tensor_tensor(
            qc_t[0][32:64, :], krr[32:64, :],
            ct2f[32:64, _tc * TCH:(_tc + 1) * TCH], op=ALU.mult),
            wait=(("act", act_krr), qadd_gp[0]), inc=(("dve", 1),))
        krr_tt = p.n("dve")
        p.em("vector", lambda e, _tc=tc: e.tensor_tensor(
            qr_t[0][32:64, :], rot[1][32:64, :],
            st2f[32:64, _tc * TCH:(_tc + 1) * TCH], op=ALU.mult),
            wait=(("pe", pe_krot), qadd_gp[0]), inc=(("dve", 1),))
        rot_used[0] = p.n("dve")
        rot_used[1] = p.n("dve")
        dvkr = p.n("dve")
        p.em("gpsimd", lambda e: e.tensor_tensor(
            kmb[0][32:64, :], qc_t[0][32:64, :], qr_t[0][32:64, :], op=ALU.add),
            wait=(("dve", dvkr), ("qs", kmb_st[0]), ("act", kvp_ev)), inc=(("gp", 1),))
        qadd_gp[0] = ("gp", p.n("gp"))
        p.em("gpsimd", lambda e: e.tensor_copy(kmb[1][32:64, :], kmb[0][32:64, :]),
             wait=(("gp", p.n("gp")), ("qs", kmb_st[1])), inc=(("gp", 1),))
        gp_rope = p.n("gp")
        pjB = pj[(tc + 1) % 3]
        for c in range(NCH):
            cb = c % 2
            bank = rot[2] if cb == 0 else pjB
            wkv = [("act", kvp_bank[cb])]
            if c == 0:
                wkv += [("act", kvl_act), ("dve", rot_used[2])]
            if c == 1:
                wkv.append(("act", pj_evac[(tc + 1) % 3]))
            p.em("tensor", lambda e, _c=c, _kb=kb, _bk=bank: e.matmul(
                _bk[0:64, :], wkvb_sb[:, _c * 64:(_c + 1) * 64], kvl[_kb][:, :],
                start=True, stop=True),
                wait=wkv, inc=(("pe", 1),))
            pe_kv = p.n("pe")
            rot_used[2] = 0
            wcp = [("pe", pe_kv), ("qs", kmb_st[cb]), ("gp", kmb_sq[cb])]
            if c < 2:
                wcp.append(("gp", gp_rope))
            p.em("scalar", lambda e, _cb=cb, _bk=bank: e.copy(
                kmb[_cb][0:32, :], _bk[0:32, :]),
                 wait=wcp, inc=(("act", 1),))
            p.em("scalar", lambda e, _cb=cb, _bk=bank: e.copy(
                kmb[_cb][64:96, :], _bk[32:64, :]),
                 inc=(("act", 1),))
            kvp_ev = p.n("act")
            kvp_bank[cb] = kvp_ev
            wsq = [("act", kvp_ev)]
            if first_sq:
                wsq.append(("pe", q_nrm_last))
                first_sq = False
            p.em("gpsimd", lambda e, _cb=cb: e.tensor_tensor(
                sq_t[_cb][0:96, :], kmb[_cb][0:96, :], kmb[_cb][0:96, :], op=ALU.mult),
                wait=wsq, inc=(("gp", 1),))
            kmb_sq[cb] = p.n("gp")
            p.em("tensor", lambda e, _cb=cb: e.matmul(
                statb[0:1, :], P["ones_b"][0:96, 0:1], sq_t[_cb][0:96, :],
                start=True, stop=True),
                wait=(("gp", kmb_sq[cb]), ("dve", prev_red)), inc=(("pe", 1),))
            pe_nrm = p.n("pe")
            col = (NCH + c) * NTC + tc
            p.em("vector", lambda e, _col=col: e.tensor_reduce(
                nmax[0:1, _col:_col + 1], statb[0:1, :], axis=AX.X, op=ALU.max),
                wait=(("pe", pe_nrm),), inc=(("dve", 1),))
            prev_red = p.n("dve")
            p.em("sync", lambda e, _c=c, _cb=cb, _tc=tc: e.dma_start(
                T["km_s"].ap()[_c][:, _tc * TCH:(_tc + 1) * TCH], kmb[_cb][0:96, :]),
                wait=(("act", kvp_ev), ("gp", gp_rope)), inc=(("qs", 16),))
            kmb_st[cb] = p.n("qs")
        pj_evac[(tc + 1) % 3] = kvp_ev
        for i4 in range(4):
            ib = i4 % 2
            wkt = [("act", ktp_copy)]
            if first_ktok:
                wkt.append(("dve", q_red_last))
                first_ktok = False
            if i4 == 0:
                wkt.append(("act", kvl_act))
            p.em("tensor", lambda e, _i=i4, _kb=kb: e.matmul(
                ktp[:, :], kvl[_kb][:, _i * 128:(_i + 1) * 128], wkvb_sb[:, :],
                start=True, stop=True),
                wait=wkt, inc=(("pe", 1),))
            pe_kt = p.n("pe")
            p.em("scalar", lambda e, _ib=ib: e.copy(
                ktb[_ib][:, :].rearrange("p (h c) -> p h c", c=33)[:, :, 0:32],
                ktp[:, :].rearrange("p (h c) -> p h c", c=32)),
                wait=(("pe", pe_kt), ("qs", ktb_st[_ib if False else ib])), inc=(("act", 1),))
            ktp_copy = p.n("act")
            p.em("sync", lambda e, _ib=ib, _t=tc * 4 + i4: e.dma_start(
                T["kt_s"].ap()[_t * 128:(_t + 1) * 128, :], ktb[_ib][:, :]),
                wait=(("act", ktp_copy),), inc=(("qs", 16),))
            ktb_st[ib] = p.n("qs")

    finalize_bounds(p, P, nmax, nbred, nbsq, statb, "negB2", 1)
    p.close()
    les.close()


def ph2b(k, T, P):
    nc = k.nc
    p = Ph(k)
    les = ExitStack()
    sbl = lambda n, shp, dt: les.enter_context(nc.sbuf_tensor(n, shp, dt))
    psl = lambda n, shp, dt: les.enter_context(nc.psum_tensor(n, shp, dt))

    wo2_sb = sbl("wo2_sb", [96, NCH * D], BF16)
    p.em("gpsimd", lambda e: e.dma_start(
        wo2_sb[:, :].rearrange("p (r d) -> p r d", d=D),
        T["wo_m"].ap().rearrange("r p d -> p r d")), inc=(("qg", 16),))
    wld = p.n("qg")

    km_sb = sbl("km_sb", [128, NCH * S], BF16)
    kt_sb = sbl("kt_sb", [128, NT * H * 33], BF16)
    qp2 = [sbl(f"qp2{b}", [128, TCH], BF16) for b in range(2)]
    xo = [sbl(f"xo{b}", [128, TCH], BF16) for b in range(2)]
    pt = [sbl(f"pt2{b}", [128, 1024], BF16) for b in range(3)]
    rcb = sbl("rcb2", [128, TCH], F32)
    yn2 = sbl("yn2", [96, NCH * TCH], BF16)
    ab2 = [sbl(f"ab2{b}", [128, TCH], F32) for b in range(2)]

    sAB = psl("sAB2", [128, 2048], F32)
    yA = [psl(f"yA2{b}", [128, TCH], F32) for b in range(2)]
    yB = [psl(f"yB2{b}", [128, TCH], F32) for b in range(2)]

    p.em("vector", lambda e: e.memset(yn2[32:64, :], 0.0), inc=(("dve", 1),))

    qp_used = [0, 0]
    yn_used = 0
    exp_last = 0
    exp_kt = {}
    av_kt = {}
    rb_read = [0, 0]
    rb_slot = [0, 0, 0, 0]
    rc_out = 0
    wo_bank = [("dve", 0)] * 4
    yn_tt = [0] * NCH
    wo_dve_all = 0
    xo_used = [0, 0]
    ab_st = [0, 0]

    prev_ld = p.n("qs")
    for qc in range(NTC):
        for c in range(NCH):
            p.em("sync", lambda e, _c=c, _qc=qc: e.dma_start(
                km_sb[0:96, _c * S + _qc * TCH:_c * S + (_qc + 1) * TCH],
                T["km_s"].ap()[_c][:, _qc * TCH:(_qc + 1) * TCH]),
                wait=(("qs", prev_ld),) if c == 0 else (), inc=(("qs", 16),))
            p.em("sync", lambda e, _c=c, _qc=qc: e.dma_start(
                km_sb[96:128, _c * S + _qc * TCH:_c * S + (_qc + 1) * TCH],
                T["km_s"].ap()[_c][32:64, _qc * TCH:(_qc + 1) * TCH]), inc=(("qs", 16),))
        p.em("sync", lambda e, _qc=qc: e.dma_start(
            kt_sb[:, _qc * 4 * H * 33:(_qc + 1) * 4 * H * 33].rearrange(
                "p (k c) -> p k c", c=H * 33),
            T["kt_s"].ap()[_qc * TCH:(_qc + 1) * TCH, :].rearrange(
                "(k p) c -> p k c", p=128)), inc=(("qs", 16),))
        ld = p.n("qs")
        prev_ld = ld

        for c in range(NCH):
            hA, hB = 2 * c, 2 * c + 1
            cb_ = c % 2
            p.em("sync", lambda e, _c=c, _qc=qc, _b=c % 2: e.dma_start(
                qp2[_b][:, :], T["qm_s"].ap()[_c][:, _qc * TCH:(_qc + 1) * TCH]),
                wait=(("pe", qp_used[c % 2]),), inc=(("qs", 16),))
            ld_q = p.n("qs")
            nkt = 4 * qc + 4

            def _scores(kt):
                r = kt % 2
                rp = kt % 3
                edge = kt >= 4 * qc
                jj = kt - 4 * qc
                qoff = 128 * jj if edge else 0
                N = 512 - qoff
                wsc = [("act", exp_kt.get(kt - 2, exp_last))]
                if kt == 0:
                    wsc += [("qs", ld), ("qs", ld_q), ("qg", wld)]
                if kt < 2:
                    wsc.append(("dve", wo_dve_all))
                for hh, (rlo, sbase) in enumerate(((0, 0), (64, 512))):
                    p.em("tensor", lambda e, _c=c, _kt=kt, _rlo=rlo, _r=r, _b=cb_,
                         _sb=sbase, _n=N, _qo=qoff, _e=edge: e.matmul(
                        sAB[:, _r * 1024 + _sb:_r * 1024 + _sb + _n],
                        km_sb[_rlo:_rlo + 64, _c * S + _kt * 128:_c * S + (_kt + 1) * 128],
                        qp2[_b][_rlo:_rlo + 64, _qo:TCH],
                        start=True, stop=not _e,
                        tile_position=(_rlo, 0)),
                        wait=wsc if hh == 0 else (),
                        inc=(("pe", 1),) if (hh == 1 and not edge) else ())
                    if edge:
                        p.em("tensor", lambda e, _r=r, _sb=sbase: e.matmul(
                            sAB[:, _r * 1024 + _sb:_r * 1024 + _sb + 128],
                            P["negI"][:, :], P["U01"][:, :],
                            start=False, stop=True),
                            inc=(("pe", 1),) if hh == 1 else ())
                pe_sc = p.n("pe")
                wexp = [("pe", pe_sc)]
                if kt >= 3:
                    wexp.append(("pe", av_kt[kt - 3]))
                p.em("scalar", lambda e, _r=r, _rp=rp, _X=N, _c=c: e.activation(
                    pt[_rp][:, :].rearrange("p (g x) -> p g x", g=2)[:, :, 0:_X],
                    sAB[:, _r * 1024:(_r + 1) * 1024].rearrange(
                        "p (g x) -> p g x", g=2)[:, :, 0:_X],
                    AF.Exp, bias=P["negB2"][:, _c:_c + 1], scale=0.125),
                    wait=wexp, inc=(("act", 1),))
                exp_kt[kt] = p.n("act")

            def _av(kt):
                rp = kt % 3
                edge = kt >= 4 * qc
                jj = kt - 4 * qc
                qoff = 128 * jj if edge else 0
                N = 512 - qoff
                wav = [("act", exp_kt[kt])]
                if kt == 0:
                    wav += [("dve", rb_read[cb_])]
                for hh, (yps, ybase, ytp, sbase, h) in enumerate(((yA[cb_], 0, (0, 0), 0, hA), (yB[cb_], 64, (0, 64), 512, hB))):
                    p.em("tensor", lambda e, _y=yps, _rp=rp, _sb=sbase,
                         _n=N, _qo=qoff, _kt=kt, _h=h, _yb=ybase, _yt=ytp,
                         _st=(kt == 0), _sp=(kt == nkt - 1): e.matmul(
                        _y[_yb:_yb + 33, _qo:_qo + _n],
                        kt_sb[:, _kt * H * 33 + _h * 33:_kt * H * 33 + (_h + 1) * 33],
                        pt[_rp][:, _sb:_sb + _n],
                        start=_st, stop=_sp, tile_position=_yt),
                        wait=wav if (hh == 0 or kt == 0) else (),
                        inc=(("pe", 1),) if hh == 1 else ())
                av_kt[kt] = p.n("pe")

            for kt in range(nkt):
                _scores(kt)
                if kt >= 2:
                    _av(kt - 2)
            if nkt >= 2:
                _av(nkt - 2)
            _av(nkt - 1)
            exp_last = exp_kt[nkt - 1]
            qp_used[cb_] = p.n("pe")
            pe_av_all = p.n("pe")
            for hh2, (yps, h, dlo, klo) in enumerate(
                    ((yA[cb_], hA, 32, 0), (yB[cb_], hB, 96, 64))):
                slot = cb_ * 2 + hh2
                p.em("vector", lambda e, _y=yps, _d=dlo: e.reciprocal(
                    rcb[_d:_d + 1, :], _y[_d:_d + 1, :]),
                    wait=(("pe", pe_av_all), ("qg", rc_out)), inc=(("dve", 1),))
                dv_rc = p.n("dve")
                p.em("gpsimd", lambda e, _d=dlo, _s=slot: e.dma_start(
                    T["rb2_scr"].ap()[_s:_s + 1, :], rcb[_d:_d + 1, :]),
                    wait=(("dve", dv_rc), ("qg", rb_slot[slot])), inc=(("qg", 16),))
                rc_out = p.n("qg")
                p.em("gpsimd", lambda e, _k=klo, _s=slot: e.dma_start(
                    rcb[_k:_k + 32, :], bass.AP(T["rb2_scr"], _s * TCH, [[0, 32], [1, TCH]])),
                    wait=(("dve", rb_read[cb_]), ("qg", rc_out)), inc=(("qg", 16),))
                rb_in = p.n("qg")
                rb_slot[slot] = rb_in
                p.em("vector", lambda e, _y=yps, _c=c, _k=klo: e.tensor_tensor(
                    yn2[_k:_k + 32, _c * TCH:(_c + 1) * TCH], _y[_k:_k + 32, :],
                    rcb[_k:_k + 32, :], op=ALU.mult),
                    wait=(("qg", rb_in), ("pe", yn_used)), inc=(("dve", 1),))
                rb_read[cb_] = p.n("dve")
                if hh2 == 1:
                    yn_tt[c] = p.n("dve")
        for o in range(NCH):
            r4 = o % 4
            ob = o % 2
            p.em("sync", lambda e, _o=o, _ob=ob, _qc=qc: e.dma_start(
                xo[_ob][:, :], T["xn_s"].ap()[_o][:, _qc * TCH:(_qc + 1) * TCH]),
                wait=(("dve", xo_used[ob]),), inc=(("qs", 16),))
            ld_x = p.n("qs")
            wwo = [wo_bank[r4], ("act", exp_last), ("dve", yn_tt[4])]
            for r in range(NCH):
                wh = wwo if r == 0 else \
                    ([("dve", yn_tt[r])] if r >= 5 else ())
                p.em("tensor", lambda e, _o=o, _r=r, _r4=r4: e.matmul(
                    sAB[:, _r4 * 512:(_r4 + 1) * 512],
                    wo2_sb[0:96, _r * D + _o * 128:_r * D + _o * 128 + 128],
                    yn2[0:96, _r * TCH:(_r + 1) * TCH],
                    start=(_r == 0), stop=(_r == NCH - 1)),
                    wait=wh, inc=(("pe", 1),) if r == NCH - 1 else ())
            pe_wo = p.n("pe")
            p.em("vector", lambda e, _o=o, _ob=ob, _r4=r4: e.tensor_tensor(
                ab2[_ob][:, :], sAB[:, _r4 * 512:(_r4 + 1) * 512], xo[_ob][:, :],
                op=ALU.add),
                wait=(("pe", pe_wo), ("qc", ab_st[ob]), ("qs", ld_x)), inc=(("dve", 1),))
            wo_bank[r4] = ("dve", p.n("dve"))
            xo_used[ob] = p.n("dve")
            p.em("scalar", lambda e, _o=o, _ob=ob, _qc=qc: e.dma_start(
                T["a_s"].ap()[_o][:, _qc * TCH:(_qc + 1) * TCH], ab2[_ob][:, :]),
                wait=(("dve", p.n("dve")),), inc=(("qc", 16),))
            ab_st[ob] = p.n("qc")
        wo_dve_all = p.n("dve")
        yn_used = p.n("pe")
    p.close()
    les.close()


def ph3(k, T, P):
    nc = k.nc
    p = Ph(k)
    les = ExitStack()
    sbl = lambda n, shp, dt: les.enter_context(nc.sbuf_tensor(n, shp, dt))
    psl = lambda n, shp, dt: les.enter_context(nc.psum_tensor(n, shp, dt))
    CH3 = 1024
    NCHK = S // CH3
    NE = E + 1

    wgate_sb = sbl("wgate_sb", [128, NCH * E], BF16)
    oh_sb = sbl("oh_sb", [E, E * 128], F32)
    p.em("gpsimd", lambda e: e.dma_start(
        wgate_sb[:, :].rearrange("p (c d) -> p c d", d=E),
        T["wgate"].ap().rearrange("c p d -> p c d")), inc=(("qg", 16),))
    p.em("gpsimd", lambda e: e.dma_start(oh_sb[0:E, :], T["onehot"].ap()),
         inc=(("qg", 16),))
    wgate_ld = p.n("qg")

    h_all = sbl("h_all", [128, NE * 4 * CH3], BF16)
    z2 = sbl("z2", [128, NCH * CH3], BF16)
    wg_sb = [sbl(f"wg_sb{b}", [128, NCH * II], BF16) for b in range(2)]
    wu_sb = [sbl(f"wu_sb{b}", [128, NCH * II], BF16) for b in range(2)]
    wd_sb = [sbl(f"wd_sb{b}", [128, 4 * D], BF16) for b in range(2)]
    rb2 = sbl("rb2", [128, CH3], F32)
    cbb = sbl("cbb", [128, CH3], F32)
    comb_f = sbl("comb_f", [8, CH3], F32)
    a1 = [sbl(f"a1_{b}", [128, CH3], F32) for b in range(2)]
    sqa = [sbl(f"sqa{b}", [128, TCH], BF16) for b in range(2)]
    rms3 = sbl("rms3", [1, TCH], F32)
    rinv3 = sbl("rinv3", [1, TCH], F32)
    sg = [sbl(f"sg{b}", [128, TCH], BF16) for b in range(2)]
    us = [sbl(f"us{b}", [128, TCH], BF16) for b in range(2)]
    lg = sbl("lg", [128, E], F32)
    lg2 = sbl("lg2", [128, E], F32)
    mx = sbl("mx3", [128, 1], F32)
    eg = sbl("eg", [128, E], F32)
    srt = sbl("srt", [128, E], F32)
    msk = sbl("msk", [128, E], F32)
    wun = sbl("wun", [128, E], F32)
    s12 = sbl("s12", [128, 1], F32)
    rs12 = sbl("rs12", [128, 1], F32)
    combt = sbl("combt", [128, E], F32)
    a3 = [sbl(f"a3_{b}", [128, 256], F32) for b in range(2)]
    A3 = [sbl(f"A3_{b}", [128, 256], F32) for b in range(2)]
    s1b = [sbl(f"s1b{b}", [128, 256], F32) for b in range(2)]
    s2b = [sbl(f"s2b{b}", [128, 256], F32) for b in range(2)]
    x3 = [sbl(f"x3_{b}", [128, D], F32) for b in range(2)]
    out_sb = [sbl(f"out_sb{b}", [128, D], F32) for b in range(2)]

    gu_g = psl("gu_g", [128, TCH], F32)
    gu_u = psl("gu_u", [128, TCH], F32)
    moe_ps = psl("moe_ps", [128, 2048], F32)
    tp3 = psl("tp3", [128, 256], F32)
    stat3 = psl("stat3", [128, TCH], F32)

    a1_act = [0, 0]
    a1_dve = [("dve", 0), ("dve", 0)]
    sqa_used = [0, 0]
    stat3_free = ("act", 0)
    z2_used = 0
    h_used = 0
    rb2_used = ("dve", 0)
    gug_sil = 0
    uu_tt = 0
    cbb_rd = 0
    silu_rd = [0, 0]
    uu_rd = [0, 0]
    wld_g = [0, 0]
    wldd = [0, 0]
    wst_g = [0, 0]
    wst_d = [0, 0]
    gate_dve = 0
    tpg_evac = 0
    lg_evac = 0
    cb_out_prev = 0
    cbb_in = 0
    moe_evac = 0
    a3_used = [0, 0]
    A3_used = [0, 0]
    s1_used = [0, 0]
    s2_used = [0, 0]
    x3_used = [0, 0]
    out_st = [0, 0]
    tp3_used = 0
    prev_batch = p.n("qs")

    for ch in range(NCHK):
        for o in range(NCH):
            ob = o % 2
            w = [("act", a1_act[ob]), a1_dve[ob]]
            if o == 0:
                w.append(("qs", prev_batch))
            p.em("sync", lambda e, _o=o, _ob=ob, _ch=ch: e.dma_start(
                a1[_ob][:, :], T["a_s"].ap()[_o][:, _ch * CH3:(_ch + 1) * CH3]),
                wait=w, inc=(("qs", 16),))
            ld = p.n("qs")
            for sub in range(2):
                bank = gu_g if sub == 0 else gu_u
                bank_war = ("act", gug_sil) if sub == 0 else ("dve", uu_tt)
                p.em("scalar", lambda e, _ob=ob, _sub=sub: e.activation(
                    sqa[_sub][:, :], a1[_ob][:, _sub * TCH:(_sub + 1) * TCH], AF.Square),
                    wait=(("qs", ld), ("pe", sqa_used[sub])), inc=(("act", 1),))
                sq_act = p.n("act")
                wmm = [("act", sq_act)]
                if o == 0:
                    wmm += [stat3_free, bank_war]
                p.em("tensor", lambda e, _o=o, _sub=sub, _bk=bank: e.matmul(
                    _bk[0:1, :], P["ones_b"][:, 0:1], sqa[_sub][:, :],
                    start=(_o == 0), stop=(_o == NCH - 1)),
                    wait=wmm, inc=(("pe", 1),))
                sqa_used[sub] = p.n("pe")
            a1_act[ob] = p.n("act")
        for sub in range(2):
            bank = gu_g if sub == 0 else gu_u
            p.em("scalar", lambda e, _bk=bank: e.activation(
                rms3[0:1, :], _bk[0:1, :], AF.Sqrt, bias=P["epsv"][0:1, 0:1],
                scale=1.0 / D), wait=(("pe", p.n("pe")), ("dve", p.n("dve"))),
                inc=(("act", 1),))
            p.em("vector", lambda e: e.reciprocal(rinv3[0:1, :], rms3[0:1, :]),
                 wait=(("act", p.n("act")),), inc=(("dve", 1),))
            p.em("tensor", lambda e: e.matmul(
                stat3[:, :], P["ones_f"][0:1, :], rinv3[0:1, :], start=True, stop=True),
                wait=(("dve", p.n("dve")), stat3_free), inc=(("pe", 1),))
            p.em("scalar", lambda e, _sub=sub: e.copy(
                rb2[:, _sub * TCH:(_sub + 1) * TCH], stat3[:, :]),
                wait=(("pe", p.n("pe")), rb2_used), inc=(("act", 1),))
            stat3_free = ("act", p.n("act"))
        rb2_act = p.n("act")
        gug_sil = rb2_act
        for o in range(NCH):
            ob = o % 2
            w = [("act", a1_act[ob]), a1_dve[ob]]
            if o == 0:
                w.append(("qs", p.n("qs")))
            p.em("sync", lambda e, _o=o, _ob=ob, _ch=ch: e.dma_start(
                a1[_ob][:, :], T["a_s"].ap()[_o][:, _ch * CH3:(_ch + 1) * CH3]),
                wait=w, inc=(("qs", 16),))
            ld = p.n("qs")
            p.em("vector", lambda e, _o=o, _ob=ob: e.scalar_tensor_tensor(
                z2[:, _o * CH3:(_o + 1) * CH3], a1[_ob][:, :],
                P["wln2"][:, _o:_o + 1], rb2[:, :], op0=ALU.mult, op1=ALU.mult),
                wait=(("qs", ld), ("act", rb2_act), ("pe", z2_used)),
                inc=(("dve", 1),))
            a1_dve[ob] = ("dve", p.n("dve"))
            rb2_used = ("dve", p.n("dve"))
        z2_dve = ("dve", p.n("dve"))
        prev_batch = p.n("qs")
        for tt in range(CH3 // 128):
            wlg = [z2_dve, ("qg", wgate_ld), ("act", lg_evac), stat3_free]
            for o in range(NCH):
                p.em("tensor", lambda e, _o=o, _tt=tt: e.matmul(
                    stat3[0:128, 0:E], z2[:, _o * CH3 + _tt * 128:_o * CH3 + (_tt + 1) * 128],
                    wgate_sb[:, _o * E:(_o + 1) * E],
                    start=(_o == 0), stop=(_o == NCH - 1)),
                    wait=wlg if o == 0 else (), inc=(("pe", 1),) if o == NCH - 1 else ())
            p.em("scalar", lambda e: e.copy(lg[:, :], stat3[0:128, 0:E]),
                 wait=(("pe", p.n("pe")), ("dve", gate_dve)), inc=(("act", 1),))
            lg_evac = p.n("act")
            gate_act = p.n("act")

            def gch(build):
                p.em("vector", build, wait=(("dve", p.n("dve")), ("act", gate_act)),
                     inc=(("dve", 1),))
            gch(lambda e: e.tensor_tensor(lg2[:, :], lg[:, :], P["ebb"][:, 0:E], op=ALU.add))
            gch(lambda e: e.tensor_reduce(mx[:, 0:1], lg2[:, :], axis=AX.X, op=ALU.max,
                                          negate=True))
            p.em("scalar", lambda e: e.activation(eg[:, :], lg2[:, :], AF.Exp,
                                                  bias=mx[:, 0:1]),
                 wait=(("dve", p.n("dve")),), inc=(("act", 1),))
            gate_act = p.n("act")
            gch(lambda e: e.max(srt[:, :], eg[:, :]))
            gch(lambda e: e.tensor_scalar(msk[:, :], eg[:, :], srt[:, 1:2], None,
                                          op0=ALU.is_ge))
            gch(lambda e: e.tensor_tensor(wun[:, :], eg[:, :], msk[:, :], op=ALU.mult))
            gch(lambda e: e.tensor_scalar(s12[:, 0:1], srt[:, 1:2], 1.0, None, op0=ALU.add))
            gch(lambda e: e.reciprocal(rs12[:, 0:1], s12[:, 0:1]))
            gch(lambda e: e.tensor_scalar(combt[:, :], wun[:, :], rs12[:, 0:1], None,
                                          op0=ALU.mult))
            gate_dve = p.n("dve")
            p.em("tensor", lambda e: e.transpose(
                tp3[0:E, 0:128], combt[:, :], P["idn_f"][:, :]),
                wait=(("dve", gate_dve), ("dve", tp3_used)), inc=(("pe", 1),))
            p.em("scalar", lambda e, _tt=tt: e.copy(
                comb_f[0:E, _tt * 128:(_tt + 1) * 128], tp3[0:E, 0:128]),
                wait=(("pe", p.n("pe")),), inc=(("act", 1),))
            tpg_evac = p.n("act")
        stat3_free = ("act", tpg_evac)
        uu_tt_p = uu_tt
        for e_i in range(NE):
            eb = e_i % 2
            wldw = [("pe", wld_g[eb]), ("qc", wst_g[eb])]
            if ch == 0:
                gsrc = T["wg"].ap()[e_i] if e_i < E else T["swg"].ap()
                usrc = T["wu"].ap()[e_i] if e_i < E else T["swu"].ap()
                p.em("gpsimd", lambda e, _b=eb, _s=gsrc: e.dma_start(
                    wg_sb[_b][:, :].rearrange("p (c d) -> p c d", d=II),
                    _s.rearrange("c p d -> p c d")),
                    wait=wldw, inc=(("qg", 16),))
                p.em("gpsimd", lambda e, _b=eb, _s=usrc: e.dma_start(
                    wu_sb[_b][:, :].rearrange("p (c d) -> p c d", d=II),
                    _s.rearrange("c p d -> p c d")), inc=(("qg", 16),))
                wld = p.n("qg")
                p.em("scalar", lambda e, _b=eb, _e=e_i: e.dma_start(
                    T["wgb_s"].ap()[_e], wg_sb[_b][:, :]),
                    wait=(("qg", wld),), inc=(("qc", 16),))
                p.em("scalar", lambda e, _b=eb, _e=e_i: e.dma_start(
                    T["wub_s"].ap()[_e], wu_sb[_b][:, :]), inc=(("qc", 16),))
                wst_g[eb] = p.n("qc")
            else:
                p.em("gpsimd", lambda e, _b=eb, _e=e_i: e.dma_start(
                    wg_sb[_b][:, :], T["wgb_s"].ap()[_e]),
                    wait=wldw, inc=(("qg", 16),))
                p.em("gpsimd", lambda e, _b=eb, _e=e_i: e.dma_start(
                    wu_sb[_b][:, :], T["wub_s"].ap()[_e]), inc=(("qg", 16),))
                wld = p.n("qg")
            if e_i == 0:
                p.em("gpsimd", lambda e: e.dma_start(T["cb_scr"].ap(), comb_f[0:E, :]),
                     wait=(("act", tpg_evac), ("qg", cb_out_prev)), inc=(("qg", 16),))
                cb_out = p.n("qg")
            if e_i < E:
                p.em("gpsimd", lambda e, _e=e_i: e.dma_start(
                    cbb[:, :], bass.AP(T["cb_scr"], _e * 1024, [[0, 128], [1, 1024]])),
                    wait=(("qg", cb_out), ("dve", cbb_rd)), inc=(("qg", 16),))
                cbb_in = p.n("qg")
                if e_i == E - 1:
                    cb_out_prev = cbb_in
            cbb_act = p.n("act")
            for sub in range(2):
                for ich in range(4):
                    gb = ich % 2
                    wg0 = [("qg", wld), z2_dve, ("act", gug_sil)]
                    for c in range(NCH):
                        p.em("tensor", lambda e, _c=c, _b=eb, _i=ich, _s=sub: e.matmul(
                            gu_g[:, :], wg_sb[_b][:, _c * II + _i * 128:_c * II + (_i + 1) * 128],
                            z2[:, _c * CH3 + _s * TCH:_c * CH3 + (_s + 1) * TCH],
                            start=(_c == 0), stop=(_c == NCH - 1)),
                            wait=wg0 if c == 0 else (),
                            inc=(("pe", 1),) if c == NCH - 1 else ())
                    pe_g = p.n("pe")
                    p.em("scalar", lambda e, _gb=gb: e.activation(
                        sg[_gb][:, :], gu_g[:, :], AF.Silu),
                        wait=(("pe", pe_g), ("dve", silu_rd[gb])), inc=(("act", 1),))
                    gug_sil = p.n("act")
                    sil_act = p.n("act")
                    wu0 = [("dve", uu_tt), ("act", rb2_act)]
                    for c in range(NCH):
                        p.em("tensor", lambda e, _c=c, _b=eb, _i=ich, _s=sub: e.matmul(
                            gu_u[:, :], wu_sb[_b][:, _c * II + _i * 128:_c * II + (_i + 1) * 128],
                            z2[:, _c * CH3 + _s * TCH:_c * CH3 + (_s + 1) * TCH],
                            start=(_c == 0), stop=(_c == NCH - 1)),
                            wait=wu0 if c == 0 else (),
                            inc=(("pe", 1),) if c == NCH - 1 else ())
                    pe_u = p.n("pe")
                    if e_i < E:
                        p.em("vector", lambda e, _gb=gb, _s=sub: e.tensor_tensor(
                            us[_gb][:, :], gu_u[:, :], cbb[:, _s * TCH:(_s + 1) * TCH],
                            op=ALU.mult),
                            wait=(("pe", pe_u), ("qg", cbb_in), ("dve", uu_rd[gb])),
                            inc=(("dve", 1),))
                    else:
                        p.em("vector", lambda e, _gb=gb: e.tensor_copy(
                            us[_gb][:, :], gu_u[:, :]),
                            wait=(("pe", pe_u), ("dve", uu_rd[gb])), inc=(("dve", 1),))
                    uu_tt = p.n("dve")
                    cbb_rd = p.n("dve")
                    p.em("vector", lambda e, _gb=gb, _e=e_i, _i=ich, _s=sub: e.tensor_tensor(
                        h_all[:, (_e * 4 + _i) * CH3 + _s * TCH:(_e * 4 + _i) * CH3 + (_s + 1) * TCH],
                        sg[_gb][:, :], us[_gb][:, :], op=ALU.mult),
                        wait=(("act", sil_act), ("dve", uu_tt), ("pe", h_used)),
                        inc=(("dve", 1),))
                    silu_rd[gb] = p.n("dve")
                    uu_rd[gb] = p.n("dve")
            wld_g[eb] = p.n("pe")
        z2_used = p.n("pe")
        h_dve = p.n("dve")
        for s4 in range(4):
            for e_i in range(NE):
                eb = e_i % 2
                wldw = [("pe", wldd[eb]), ("qc", wst_d[eb])]
                if ch == 0 and s4 == 0:
                    dsrc = T["wd"].ap()[e_i] if e_i < E else T["swd"].ap()
                    p.em("gpsimd", lambda e, _b=eb, _s=dsrc: e.dma_start(
                        wd_sb[_b][:, :].rearrange("p (c d) -> p c d", d=D),
                        _s.rearrange("c p d -> p c d")),
                        wait=wldw, inc=(("qg", 16),))
                    wld = p.n("qg")
                    p.em("scalar", lambda e, _b=eb, _e=e_i: e.dma_start(
                        T["wdb_s"].ap()[_e], wd_sb[_b][:, :]),
                        wait=(("qg", wld),), inc=(("qc", 16),))
                    wst_d[eb] = p.n("qc")
                else:
                    p.em("gpsimd", lambda e, _b=eb, _e=e_i: e.dma_start(
                        wd_sb[_b][:, :], T["wdb_s"].ap()[_e]),
                        wait=wldw, inc=(("qg", 16),))
                    wld = p.n("qg")
                wv0 = [("qg", wld), ("dve", h_dve)]
                if e_i == 0:
                    wv0.append(("dve", moe_evac))
                for o in range(NCH):
                    for ich in range(4):
                        st_flag = (e_i == 0 and ich == 0 and o % 2 == 0)
                        sp_flag = (e_i == NE - 1 and ich == 3 and o % 2 == 1)
                        p.em("tensor", lambda e, _o=o, _i=ich, _b=eb, _e=e_i, _s4=s4,
                             _st=st_flag, _sp=sp_flag: e.matmul(
                            moe_ps[:, _o * 256:(_o + 1) * 256],
                            wd_sb[_b][:, _i * D + _o * 128:_i * D + _o * 128 + 128],
                            h_all[:, (_e * 4 + _i) * CH3 + _s4 * 256:(_e * 4 + _i) * CH3 + (_s4 + 1) * 256],
                            start=_st, stop=_sp),
                            wait=wv0 if (o == 0 and ich == 0) else (),
                            inc=(("pe", 1),) if (o == NCH - 1 and ich == 3) else ())
                wldd[eb] = p.n("pe")
            pe_moe = p.n("pe")
            for jj in range(2):
                t_tile = ch * 8 + s4 * 2 + jj
                w = [("dve", x3_used[jj])]
                if jj == 0:
                    w.append(("qs", prev_batch))
                p.em("sync", lambda e, _t=t_tile, _j=jj: e.dma_start(
                    x3[_j][:, :], T["x"].ap()[_t * 128:(_t + 1) * 128, :]),
                    wait=w, inc=(("qs", 16),))
            for o in range(NCH):
                ob = o % 2
                p.em("sync", lambda e, _o=o, _ob=ob, _ch=ch, _s4=s4: e.dma_start(
                    a3[_ob][:, :],
                    T["a_s"].ap()[_o][:, _ch * CH3 + _s4 * 256:_ch * CH3 + (_s4 + 1) * 256]),
                    wait=(("dve", a3_used[ob]),), inc=(("qs", 16),))
                p.em("sync", lambda e, _o=o, _ob=ob, _ch=ch, _s4=s4: e.dma_start(
                    A3[_ob][:, :],
                    T["A_s"].ap()[_o][:, _ch * CH3 + _s4 * 256:_ch * CH3 + (_s4 + 1) * 256]),
                    wait=(("dve", A3_used[ob]),), inc=(("qs", 16),))
                ld3 = p.n("qs")
                p.em("vector", lambda e, _o=o, _ob=ob: e.tensor_tensor(
                    s1b[_ob][:, :], moe_ps[:, _o * 256:(_o + 1) * 256], a3[_ob][:, :],
                    op=ALU.add),
                    wait=(("qs", ld3), ("pe", pe_moe), ("dve", s1_used[ob])),
                    inc=(("dve", 1),))
                a3_used[ob] = p.n("dve")
                p.em("vector", lambda e, _ob=ob: e.tensor_tensor(
                    s2b[_ob][:, :], s1b[_ob][:, :], A3[_ob][:, :], op=ALU.add),
                    wait=(("dve", p.n("dve")), ("pe", s2_used[ob])), inc=(("dve", 1),))
                A3_used[ob] = p.n("dve")
                s1_used[ob] = p.n("dve")
                dv_s2 = p.n("dve")
                if o == NCH - 1:
                    moe_evac = dv_s2
                for jj in range(2):
                    p.em("tensor", lambda e, _ob=ob, _j=jj: e.transpose(
                        tp3[:, _j * 128:(_j + 1) * 128],
                        s2b[_ob][:, _j * 128:(_j + 1) * 128], P["idn_f"][:, :]),
                        wait=(("dve", dv_s2), ("dve", tp3_used), ("act", tpg_evac)),
                        inc=(("pe", 1),))
                    s2_used[ob] = p.n("pe")
                    pe_tp = p.n("pe")
                    p.em("vector", lambda e, _j=jj, _o=o: e.tensor_tensor(
                        out_sb[_j][:, _o * 128:(_o + 1) * 128],
                        tp3[:, _j * 128:(_j + 1) * 128],
                        x3[_j][:, _o * 128:(_o + 1) * 128], op=ALU.add),
                        wait=(("pe", pe_tp), ("qs", p.n("qs")), ("qc", out_st[jj])),
                        inc=(("dve", 1),))
                    tp3_used = p.n("dve")
                    x3_used[jj] = p.n("dve")
            for jj in range(2):
                t_tile = ch * 8 + s4 * 2 + jj
                p.em("scalar", lambda e, _t=t_tile, _j=jj: e.dma_start(
                    T["out"].ap()[_t * 128:(_t + 1) * 128, :], out_sb[_j][:, :]),
                    wait=(("dve", tp3_used),), inc=(("qc", 16),))
                out_st[jj] = p.n("qc")
            prev_batch = p.n("qs")
        h_used = p.n("pe")
    p.close()
    les.close()



def layout_inputs(ins):
    g = lambda n: np.asarray(ins[n])
    d = {}
    d["wq_a"] = g("L0_attn_Wq").reshape(NCH, 128, D)
    d["wk_a"] = g("L0_attn_Wk").reshape(NCH, 128, D)
    d["wv_a"] = g("L0_attn_Wv").reshape(NCH, 128, D)
    d["wo_a"] = g("L0_attn_Wo").reshape(H, HD, D)
    d["bq_a"] = g("L0_attn_bq").reshape(NCH, 128, 1)
    d["bk_a"] = g("L0_attn_bk").reshape(NCH, 128, 1)
    d["bv_a"] = g("L0_attn_bv").reshape(1, D)
    d["bo_a"] = g("L0_attn_bo").reshape(NCH, 128, 1)
    d["wln1"] = g("L0_ffn_w_ln1").reshape(NCH, 128, 1)
    d["wln2"] = g("L0_ffn_w_ln2").reshape(NCH, 128, 1)
    d["wq_m"] = g("L0_ffn_Wq").reshape(NCH, 128, D)
    d["wkva"] = g("L0_ffn_Wkva").reshape(NCH, 128, RLAT + RP)
    d["wkvb"] = g("L0_ffn_Wkvb")
    wo2 = g("L0_ffn_Wo").reshape(H, HD, D)[:, 0:32, :]
    wo_m = np.zeros((NCH, 96, D), np.float32)
    wo_m[:, 0:32] = wo2[0::2]
    wo_m[:, 64:96] = wo2[1::2]
    d["wo_m"] = wo_m
    d["wgate"] = g("L0_ffn_Wgate").reshape(NCH, 128, E)
    d["ebias"] = g("L0_ffn_expert_bias").reshape(1, E)
    d["wg"] = g("L0_ffn_Wg").reshape(E, NCH, 128, II)
    d["wu"] = g("L0_ffn_Wu").reshape(E, NCH, 128, II)
    d["wd"] = g("L0_ffn_Wd").reshape(E, 4, 128, D)
    d["swg"] = g("L0_ffn_sWg").reshape(NCH, 128, II)
    d["swu"] = g("L0_ffn_sWu").reshape(NCH, 128, II)
    d["swd"] = g("L0_ffn_sWd").reshape(4, 128, D)
    return d


def kernel(**inputs):
    nc = build()
    shared = layout_inputs(inputs)
    shared.update(host_constants())
    x_full = np.asarray(inputs["L0_attn_x"])
    in_maps = []
    for core in range(8):
        m = dict(shared)
        m["x"] = np.ascontiguousarray(x_full[core])
        in_maps.append(m)
    res = run_bass_kernel_spmd(nc, in_maps, core_ids=list(range(8)))
    return np.stack([r["out"] for r in res.results], axis=0)


# version 23
# speedup vs baseline: 1.1977x; 1.1977x over previous
import os
import numpy as np
import ml_dtypes
import concourse.bass as bass
import concourse.mybir as mybir
from concourse.bass_utils import run_bass_kernel_spmd
from concourse.alu_op_type import AluOpType as ALU
from contextlib import ExitStack

F32 = mybir.dt.float32
BF16 = mybir.dt.bfloat16
AF = mybir.ActivationFunctionType
AX = mybir.AxisListType
bfdt = ml_dtypes.bfloat16

S = 4096
D = 1024
NCH = 8
TCH = 512
NTC = S // TCH
NT = S // 128
H = 16
HD = 64
RLAT = 128
RP = 32
E = 8
II = 512
EPS = 1e-5
NEGBIG = -1e30

ENGS = ("sync", "tensor", "vector", "scalar", "gpsimd")


NSLOT = 4
QUEUES = ("qs", "qg", "qc")


class K:
    def __init__(self, nc, sems):
        self.nc = nc
        self.sems = sems
        self.cnt = {s: 0 for s in sems}
        self.qn = {q: 0 for q in QUEUES}


def _slot_count(n, j):
    return 0 if n <= j else (n - j + NSLOT - 1) // NSLOT


class Ph:

    def __init__(self, k):
        self.k = k
        self.st = {e: [] for e in ENGS}

    def em(self, eng, build, wait=(), inc=()):
        sems = self.k.sems
        exp_wait = []
        for s, v in wait:
            if s in QUEUES:
                n = v // 16
                for j in range(NSLOT):
                    c = _slot_count(n, j)
                    if c > 0:
                        exp_wait.append((f"{s}{j}", 16 * c))
            else:
                exp_wait.append((s, v))
        exp_inc = []
        for s, a in inc:
            if s in QUEUES:
                if os.environ.get("KSERIAL") and self.k.qn[s] > 0:
                    n = self.k.qn[s]
                    for j in range(NSLOT):
                        cc = _slot_count(n, j)
                        if cc > 0:
                            exp_wait.append((f"{s}{j}", 16 * cc))
                j = self.k.qn[s] % NSLOT
                self.k.qn[s] += 1
                exp_inc.append((f"{s}{j}", 16))
            else:
                self.k.cnt[s] += a
                exp_inc.append((s, a))
        w, ii = tuple(exp_wait), tuple(exp_inc)

        def fn(e):
            for s, v in w:
                if v > 0:
                    e.wait_ge(sems[s], v)
            r = build(e)
            for s, a in ii:
                r = r.then_inc(sems[s], a)

        self.st[eng].append(fn)

    def n(self, s):
        if s in QUEUES:
            return 16 * self.k.qn[s]
        return self.k.cnt[s]

    def close(self, drain_stores=True):
        if drain_stores:
            self.em("sync", lambda e: e.nop(),
                    wait=(("qc", self.n("qc")), ("qg", self.n("qg"))))
        with self.k.nc.Block() as block:
            for e in ENGS:
                lst = self.st[e]
                if lst:
                    getattr(block, e)(lambda eng, _l=tuple(lst): [f(eng) for f in _l])



def host_constants():
    c = {}
    pos = np.arange(S, dtype=np.float32)
    freq = np.arange(0, HD, 2, dtype=np.float32) / HD
    inv = (1.0 / 10000.0 ** freq).astype(np.float32)
    angrep = np.repeat(np.outer(pos, inv).astype(np.float32), 2, axis=-1)
    cos_o = np.cos(angrep).T.astype(np.float32)
    sin_o = np.sin(angrep).T.astype(np.float32)
    c["cos_o"] = np.concatenate([cos_o, cos_o], 0)
    c["sin_o"] = np.concatenate([sin_o, sin_o], 0)
    freqs2 = (1.0 / 100000.0 ** (np.arange(0, RP, 2, dtype=np.float32) / RP)).astype(np.float32)
    ang2rep = np.repeat(np.outer(pos, freqs2).astype(np.float32), 2, axis=-1)
    cos2 = np.cos(ang2rep).T.astype(np.float32)
    sin2 = np.sin(ang2rep).T.astype(np.float32)
    ones32 = np.ones((32, S), np.float32)
    zero32 = np.zeros((32, S), np.float32)
    c["cs2c"] = np.concatenate([ones32, cos2, ones32, cos2], 0)
    c["cs2s"] = np.concatenate([zero32, sin2, zero32, sin2], 0)
    r64 = np.zeros((HD, HD), np.float32)
    for i in range(0, HD, 2):
        r64[i, i + 1] = -1.0
        r64[i + 1, i] = 1.0
    Ro = np.zeros((128, 128), np.float32)
    Ro[0:64, 0:64] = r64
    Ro[64:128, 64:128] = r64
    c["Ro"] = np.ascontiguousarray(Ro.T).astype(bfdt)
    r32 = np.zeros((RP, RP), np.float32)
    for i in range(0, RP, 2):
        r32[i, i + 1] = -1.0
        r32[i + 1, i] = 1.0
    Rm = np.zeros((128, 128), np.float32)
    Rm[32:64, 32:64] = r32
    Rm[96:128, 96:128] = r32
    c["Rm"] = np.ascontiguousarray(Rm.T).astype(bfdt)
    c["idn_b"] = np.eye(128, dtype=np.float32).astype(bfdt)
    c["idn_f"] = np.eye(128, dtype=np.float32)
    c["ones_f"] = np.ones((128, 128), np.float32)
    c["ones_b"] = np.ones((128, 128), np.float32).astype(bfdt)
    u = np.zeros((128, 128), np.float32)
    for kk in range(128):
        u[kk, :kk] = 1.0
    c["U01"] = u.astype(bfdt)
    c["negI"] = (NEGBIG * np.eye(128, dtype=np.float32)).astype(bfdt)
    ind = np.zeros((128, 2), np.float32)
    ind[0:64, 0] = 1.0
    ind[64:128, 1] = 1.0
    c["indAB"] = ind.astype(bfdt)
    ind2 = np.zeros((96, 2), np.float32)
    ind2[0:64, 0] = 1.0
    ind2[32:96, 1] = 1.0
    c["indMk"] = ind2.astype(bfdt)
    oh = np.zeros((E, E * 128), np.float32)
    for e in range(E):
        oh[e, e * 128:(e + 1) * 128] = 1.0
    c["onehot"] = oh
    return c


CONST_SPECS = {
    "cos_o": ([128, S], F32), "sin_o": ([128, S], F32),
    "cs2c": ([128, S], F32), "cs2s": ([128, S], F32),
    "Ro": ([128, 128], BF16), "Rm": ([128, 128], BF16),
    "idn_b": ([128, 128], BF16), "idn_f": ([128, 128], F32),
    "ones_f": ([128, 128], F32), "ones_b": ([128, 128], BF16),
    "U01": ([128, 128], BF16), "negI": ([128, 128], BF16),
    "indAB": ([128, 2], BF16), "indMk": ([96, 2], BF16),
    "onehot": ([E, E * 128], F32),
}

INPUT_SPECS = {
    "x": ([S, D], F32),
    "wq_a": ([NCH, 128, D], F32), "wk_a": ([NCH, 128, D], F32),
    "wv_a": ([NCH, 128, D], F32), "wo_a": ([H, HD, D], F32),
    "bq_a": ([NCH, 128, 1], F32), "bk_a": ([NCH, 128, 1], F32),
    "bv_a": ([1, D], F32), "bo_a": ([NCH, 128, 1], F32),
    "wln1": ([NCH, 128, 1], F32), "wln2": ([NCH, 128, 1], F32),
    "wq_m": ([NCH, 128, D], F32), "wkva": ([NCH, 128, RLAT + RP], F32),
    "wkvb": ([RLAT, H * 32], F32), "wo_m": ([NCH, 96, D], F32),
    "wgate": ([NCH, 128, E], F32), "ebias": ([1, E], F32),
    "wg": ([E, NCH, 128, II], F32), "wu": ([E, NCH, 128, II], F32),
    "wd": ([E, 4, 128, D], F32),
    "swg": ([NCH, 128, II], F32), "swu": ([NCH, 128, II], F32),
    "swd": ([4, 128, D], F32),
}

SCRATCH_SPECS = {
    "xn_s": ([NCH, 128, S], BF16),
    "qo_s": ([NCH, 128, S], BF16), "ko_s": ([NCH, 128, S], BF16),
    "vo_s": ([S, H * 65], BF16), "A_s": ([NCH, 128, S], F32),
    "qm_s": ([NCH, 128, S], BF16), "km_s": ([NCH, 96, S], BF16),
    "kt_s": ([S, H * 33], BF16), "a_s": ([NCH, 128, S], F32),
    "wgb_s": ([E + 1, 128, NCH * II], BF16), "wub_s": ([E + 1, 128, NCH * II], BF16),
    "wdb_s": ([E + 1, 128, 4 * D], BF16),
    "rb1_scr": ([4, TCH], F32), "rb2_scr": ([4, TCH], F32),
    "cb_scr": ([8, 1024], F32),
}



def build(upto=99, dbg=()):
    nc = bass.Bass("TRN2", target_bir_lowering=False)
    es = ExitStack()

    T = {}
    for n, (shp, dt) in INPUT_SPECS.items():
        T[n] = nc.dram_tensor(n, shp, dt, kind="ExternalInput")
    for n, (shp, dt) in CONST_SPECS.items():
        T[n] = nc.dram_tensor(n, shp, dt, kind="ExternalInput")
    for n, (shp, dt) in SCRATCH_SPECS.items():
        kind = "ExternalOutput" if n in dbg else "Internal"
        T[n] = nc.dram_tensor(n, shp, dt, kind=kind)
    T["out"] = nc.dram_tensor("out", [S, D], F32, kind="ExternalOutput")
    if "nb_dump" in dbg:
        T["nb_dump"] = nc.dram_tensor("nb_dump", [128, 2 * NCH], F32, kind="ExternalOutput")

    sems = {}
    for s in ("pe", "act", "dve", "gp"):
        sems[s] = es.enter_context(nc.semaphore(s))
    for q in QUEUES:
        for j in range(NSLOT):
            sems[f"{q}{j}"] = es.enter_context(nc.semaphore(f"{q}{j}"))
    k = K(nc, sems)

    P = {}
    for n in ("Ro", "Rm", "idn_b", "idn_f", "ones_f", "ones_b", "U01", "negI",
              "indAB", "indMk"):
        shp, dt = CONST_SPECS[n]
        P[n] = es.enter_context(nc.sbuf_tensor(n + "_c", shp, dt))
    for n in ("bq_a", "bk_a", "bo_a", "wln1", "wln2"):
        P[n] = es.enter_context(nc.sbuf_tensor(n + "_v", [128, NCH], F32))
    P["bvb"] = es.enter_context(nc.sbuf_tensor("bvb", [128, D], F32))
    P["ebb"] = es.enter_context(nc.sbuf_tensor("ebb", [128, 4 * E], F32))
    P["negB1"] = es.enter_context(nc.sbuf_tensor("negB1", [128, NCH], F32))
    P["negB2"] = es.enter_context(nc.sbuf_tensor("negB2", [128, NCH], F32))
    P["nbrow"] = es.enter_context(nc.sbuf_tensor("nbrow", [1, 2 * NCH], F32))
    P["epsv"] = es.enter_context(nc.sbuf_tensor("epsv", [128, 1], F32))

    on = (lambda i: i in upto) if isinstance(upto, (set, frozenset, tuple, list)) \
        else (lambda i: upto >= i)
    ph_init(k, T, P)
    xn_es = ExitStack()
    xn_f = xn_es.enter_context(nc.sbuf_tensor("xn_f", [128, NCH * S], BF16))
    if on(1):
        ph0(k, T, P, xn_f)
    if on(2):
        ph1a(k, T, P, xn_f)
    xn_es.close()
    if on(3):
        ph1b(k, T, P)
    if on(4):
        ph2a(k, T, P)
    if on(5):
        ph2b(k, T, P)
    if on(6):
        ph3(k, T, P)
    if "nb_dump" in dbg:
        p = Ph(k)
        if True:
            p.em("sync", lambda e: e.dma_start(T["nb_dump"].ap()[:, 0:NCH], P["negB1"][:, :]),
                 inc=(("qs", 16),))
            p.em("sync", lambda e: e.dma_start(T["nb_dump"].ap()[:, NCH:2 * NCH], P["negB2"][:, :]),
                 inc=(("qs", 16),))
        p.em("sync", lambda e: e.nop(), wait=(("qs", p.n("qs")),))
        p.close()
    ph_fini(k)
    es.close()
    return nc


def ph_fini(k):
    p = Ph(k)
    p.em("scalar", lambda e: e.nop(),
         wait=(("qs", p.n("qs")), ("qg", p.n("qg")), ("qc", p.n("qc"))))
    p.close()



def ph_init(k, T, P):
    p = Ph(k)
    for n in ("Ro", "Rm", "idn_b", "idn_f", "ones_f", "ones_b", "U01", "negI",
              "indAB", "indMk"):
        shp = CONST_SPECS[n][0]
        p.em("sync", lambda e, _n=n, _s=shp: e.dma_start(
            P[_n][0:_s[0], 0:_s[1]], T[_n].ap()), inc=(("qs", 16),))
    for n in ("bq_a", "bk_a", "bo_a", "wln1", "wln2"):
        for c in range(NCH):
            p.em("sync", lambda e, _n=n, _c=c: e.dma_start(
                P[_n][:, _c:_c + 1], T[_n].ap()[_c]), inc=(("qs", 16),))
    p.em("vector", lambda e: e.memset(P["epsv"][:, 0:1], EPS), inc=(("dve", 1),))
    les = ExitStack()
    bvrow = les.enter_context(k.nc.sbuf_tensor("bvrow", [1, D], F32))
    ebrow = les.enter_context(k.nc.sbuf_tensor("ebrow", [1, 4 * E], F32))
    p.em("sync", lambda e: e.dma_start(bvrow[0:1, :], T["bv_a"].ap()), inc=(("qs", 16),))
    p.em("sync", lambda e: e.dma_start(ebrow[0:1, 0:E], T["ebias"].ap()), inc=(("qs", 16),))
    ld = p.n("qs")
    for j in range(1, 4):
        p.em("vector", lambda e, _j=j: e.tensor_copy(
            ebrow[0:1, _j * E:(_j + 1) * E], ebrow[0:1, 0:E]),
            wait=(("qs", ld), ("dve", p.n("dve"))), inc=(("dve", 1),))
    dv = p.n("dve")
    pb = les.enter_context(k.nc.psum_tensor("pb_init", [128, D], F32))
    pb2 = les.enter_context(k.nc.psum_tensor("pb2_init", [128, 4 * E], F32))
    for half in range(2):
        p.em("tensor", lambda e, _h=half: e.matmul(
            pb[:, _h * 512:(_h + 1) * 512], P["ones_f"][0:1, :],
            bvrow[0:1, _h * 512:(_h + 1) * 512], start=True, stop=True),
            wait=(("qs", ld),), inc=(("pe", 1),))
    p.em("tensor", lambda e: e.matmul(pb2[:, :], P["ones_f"][0:1, :], ebrow[0:1, :],
                                      start=True, stop=True),
         wait=(("dve", dv),), inc=(("pe", 1),))
    pe = p.n("pe")
    p.em("scalar", lambda e: e.copy(P["bvb"][:, 0:512], pb[:, 0:512]),
         wait=(("pe", pe),), inc=(("act", 1),))
    p.em("scalar", lambda e: e.copy(P["bvb"][:, 512:1024], pb[:, 512:1024]), inc=(("act", 1),))
    p.em("scalar", lambda e: e.copy(P["ebb"][:, :], pb2[:, :]), inc=(("act", 1),))
    p.close()
    les.close()



def ph0(k, T, P, xn_f):
    nc = k.nc
    p = Ph(k)
    les = ExitStack()
    sbl = lambda n, shp, dt: les.enter_context(nc.sbuf_tensor(n, shp, dt))
    psl = lambda n, shp, dt: les.enter_context(nc.psum_tensor(n, shp, dt))
    DP = 3
    xtok = [sbl(f"xtok{b}", [128, D], F32) for b in range(DP)]
    xntok = [sbl(f"xntok{b}", [128, D], BF16) for b in range(DP)]
    stats = [sbl(f"stats0{b}", [128, 12], F32) for b in range(2)]
    mv = [sbl(f"mv0{b}", [128, 2], F32) for b in range(2)]
    sqv = [sbl(f"sqv0{b}", [128, 1], F32) for b in range(2)]
    rinv = [sbl(f"rinv0{b}", [128, 1], F32) for b in range(2)]
    mneg = [sbl(f"mneg0{b}", [128, 1], F32) for b in range(2)]
    tp = [psl(f"tp0_{b}", [128, 128], BF16) for b in range(4)]

    xn_act = [0] * NT
    tr_done = [0] * NT
    ev_done = [0] * NT
    ag_done = [0, 0]
    sq_done = [0, 0]
    ri_used = [0, 0]
    tp_ev = [0] * 4

    for i in range(NT):
        b = i % DP
        sb_ = i % 2
        w = [("act", xn_act[i - DP])] if i >= DP else []
        p.em("sync", lambda e, _b=b, _i=i: e.dma_start(
            xtok[_b][:, :], T["x"].ap()[_i * 128:(_i + 1) * 128, :]),
            wait=w, inc=(("qs", 16),))
        ld = p.n("qs")
        wstats = [("qs", ld), ("dve", ag_done[sb_])]
        p.em("vector", lambda e, _b=b, _s=sb_: e.bn_stats(stats[_s][:, 0:6], xtok[_b][:, 0:512]),
             wait=wstats, inc=(("dve", 1),))
        p.em("vector", lambda e, _b=b, _s=sb_: e.bn_stats(stats[_s][:, 6:12], xtok[_b][:, 512:1024]),
             inc=(("dve", 1),))
        bs = p.n("dve")
        p.em("vector", lambda e, _s=sb_: e.bn_aggr(mv[_s][:, :], stats[_s][:, :]),
             wait=(("dve", bs), ("act", sq_done[sb_])), inc=(("dve", 1),))
        ag_done[sb_] = p.n("dve")
        p.em("scalar", lambda e, _s=sb_: e.activation(sqv[_s][:, 0:1], mv[_s][:, 1:2], AF.Sqrt,
                                                      bias=P["epsv"][:, 0:1]),
             wait=(("dve", ag_done[sb_]), ("dve", ri_used[sb_])), inc=(("act", 1),))
        sq_done[sb_] = p.n("act")
        p.em("vector", lambda e, _s=sb_: e.reciprocal(rinv[_s][:, 0:1], sqv[_s][:, 0:1]),
             wait=(("act", sq_done[sb_]), ("act", ri_used[sb_])), inc=(("dve", 1),))
        p.em("vector", lambda e, _s=sb_: e.tensor_scalar(mneg[_s][:, 0:1], mv[_s][:, 0:1],
                                                         rinv[_s][:, 0:1], -1.0,
                                                         op0=ALU.mult, op1=ALU.mult),
             wait=(("dve", p.n("dve")),), inc=(("dve", 1),))
        mn = p.n("dve")
        wxn = [("dve", mn)]
        if i >= DP:
            wxn.append(("pe", tr_done[i - DP]))
        p.em("scalar", lambda e, _b=b, _s=sb_: e.activation(
            xntok[_b][:, :], xtok[_b][:, :], AF.Identity,
            bias=mneg[_s][:, 0:1], scale=rinv[_s][:, 0:1]), wait=wxn, inc=(("act", 1),))
        xn_act[i] = p.n("act")
        ri_used[sb_] = xn_act[i]
        for c in range(NCH):
            tb = c % 4
            wtr = [("act", xn_act[i]), ("dve", tp_ev[tb])]
            p.em("tensor", lambda e, _b=b, _c=c, _tb=tb: e.transpose(
                tp[_tb][:, :], xntok[_b][:, _c * 128:(_c + 1) * 128], P["idn_b"][:, :]),
                wait=wtr, inc=(("pe", 1),))
            p.em("vector", lambda e, _c=c, _i=i, _tb=tb: e.tensor_copy(
                xn_f[:, _c * S + _i * 128: _c * S + (_i + 1) * 128], tp[_tb][:, :]),
                wait=(("pe", p.n("pe")),), inc=(("dve", 1),))
            tp_ev[tb] = p.n("dve")
        tr_done[i] = p.n("pe")
        ev_done[i] = p.n("dve")
        p.em("gpsimd", lambda e, _i=i: e.dma_start(
            T["xn_s"].ap()[:, :, _i * 128:(_i + 1) * 128].rearrange("c p t -> p c t"),
            xn_f[:, :].rearrange("p (c s) -> p c s", s=S)[:, :, _i * 128:(_i + 1) * 128]),
            wait=(("dve", ev_done[i]),), inc=(("qg", 16),))
    p.close()
    les.close()


def ph1a(k, T, P, xn_f):
    nc = k.nc
    p = Ph(k)
    les = ExitStack()
    sbl = lambda n, shp, dt: les.enter_context(nc.sbuf_tensor(n, shp, dt))
    psl = lambda n, shp, dt: les.enter_context(nc.psum_tensor(n, shp, dt))

    wsb = {}
    for n, dn in (("wq", "wq_a"), ("wk", "wk_a"), ("wv", "wv_a")):
        wsb[n] = sbl(n + "_sb", [128, NCH * D], BF16)
        p.em("gpsimd", lambda e, _n=n, _d=dn: e.dma_start(
            wsb[_n][:, :].rearrange("p (c d) -> p c d", d=D),
            T[_d].ap().rearrange("c p d -> p c d")), inc=(("qg", 16),))
    wld = p.n("qg")
    ctf = sbl("ctf", [128, S], F32)
    stf = sbl("stf", [128, S], F32)
    p.em("sync", lambda e: e.dma_start(ctf[:, :], T["cos_o"].ap()), inc=(("qs", 16),))
    p.em("sync", lambda e: e.dma_start(stf[:, :], T["sin_o"].ap()), inc=(("qs", 16),))
    tabld = p.n("qs")

    DP = 4
    qraw = [sbl(f"qraw{b}", [128, TCH], BF16) for b in range(DP)]
    qc_t = [sbl(f"qc_t{b}", [128, TCH], F32) for b in range(DP)]
    qr_t = [sbl(f"qr_t{b}", [128, TCH], F32) for b in range(DP)]
    qacc = [sbl(f"qacc{b}", [128, S], BF16) for b in range(2)]
    sq_t = sbl("sq_t", [128, S], BF16)
    vo_b = [sbl(f"vo_b{b}", [128, H * 65], BF16) for b in range(2)]
    nmax = sbl("nmax1w", [1, 2 * NCH * NTC], F32)
    nbred = sbl("nbred", [1, 2 * NCH], F32)
    nbsq = sbl("nbsq", [1, 2 * NCH], F32)

    pj = [psl(f"pj{b}", [128, TCH], F32) for b in range(3)]
    rot = [psl(f"rot{b}", [128, TCH], F32) for b in range(3)]
    vp = psl("vp0", [128, TCH], F32)
    nb_sh = psl("nb_sh", [128, TCH], F32)

    for b in range(2):
        p.em("vector", lambda e, _b=b: e.memset(
            vo_b[_b][:, :].rearrange("p (h c) -> p h c", c=65)[:, :, 64:65], 1.0),
            inc=(("dve", 1),))

    pj_evac = [0, 0, 0]
    raw_used_tt = [0] * DP
    rot_used = [0, 0, 0]
    qadd_gp = [("gp", 0)] * DP
    qacc_st = [0, 0]
    qacc_sq = [0, 0]
    v_tt = 0
    vo_stq = [0, 0]
    prev_red = 0
    prev_nrm = 0
    sq_nrm_rd = 0

    def flush_o(go, ab, dst, o):
        nonlocal prev_red, prev_nrm, sq_nrm_rd
        gp_all = p.n("gp")
        p.em("scalar", lambda e, _dst=dst, _o=o, _ab=ab: e.dma_start(
            T[_dst].ap()[_o], qacc[_ab][:, :]),
            wait=(("gp", gp_all),), inc=(("qc", 16),))
        qacc_st[ab] = p.n("qc")
        p.em("scalar", lambda e, _ab=ab: e.activation(
            sq_t[:, :], qacc[_ab][:, :], AF.Square),
            wait=(("gp", gp_all), ("pe", sq_nrm_rd)), inc=(("act", 1),))
        qacc_sq[ab] = p.n("act")
        sq_act = p.n("act")
        for tc in range(NTC):
            p.em("tensor", lambda e, _tc=tc: e.matmul(
                nb_sh[0:1, :], P["ones_b"][:, 0:1],
                sq_t[:, _tc * TCH:(_tc + 1) * TCH], start=True, stop=True),
                wait=(("act", sq_act), ("dve", prev_red)) if tc == 0 else
                     (("dve", prev_red),), inc=(("pe", 1),))
            prev_nrm = p.n("pe")
            col = go * NTC + tc
            p.em("vector", lambda e, _col=col: e.tensor_reduce(
                nmax[0:1, _col:_col + 1], nb_sh[0:1, :], axis=AX.X, op=ALU.max),
                wait=(("pe", prev_nrm),), inc=(("dve", 1),))
            prev_red = p.n("dve")
        sq_nrm_rd = prev_nrm

    def emit_pending(pd):
        ob, pb, tc, act_raw, ab = pd
        p.em("tensor", lambda e, _ob=ob, _pb=pb: e.matmul(
            rot[_pb][:, :], P["Ro"][:, :], qraw[_ob][:, :], start=True, stop=True),
            wait=(("act", act_raw), ("dve", rot_used[_pb if False else pb])),
            inc=(("pe", 1),))
        pe_rot = p.n("pe")
        p.em("vector", lambda e, _ob=ob, _tc=tc: e.tensor_tensor(
            qc_t[_ob][:, :], qraw[_ob][:, :],
            ctf[:, _tc * TCH:(_tc + 1) * TCH], op=ALU.mult),
            wait=(("qs", tabld), ("act", act_raw), qadd_gp[ob]),
            inc=(("dve", 1),))
        raw_used_tt[ob] = p.n("dve")
        p.em("vector", lambda e, _ob=ob, _pb=pb, _tc=tc: e.tensor_tensor(
            qr_t[_ob][:, :], rot[_pb][:, :],
            stf[:, _tc * TCH:(_tc + 1) * TCH], op=ALU.mult),
            wait=(("pe", pe_rot), qadd_gp[ob]), inc=(("dve", 1),))
        rot_used[pb] = p.n("dve")
        dv2 = p.n("dve")
        wadd = [("dve", dv2)]
        if tc == 0:
            wadd += [("qc", qacc_st[ab]), ("act", qacc_sq[ab])]
        p.em("gpsimd", lambda e, _ob=ob, _ab=ab, _tc=tc: e.tensor_tensor(
            qacc[_ab][:, _tc * TCH:(_tc + 1) * TCH],
            qc_t[_ob][:, :], qr_t[_ob][:, :], op=ALU.add),
            wait=wadd, inc=(("gp", 1),))
        qadd_gp[ob] = ("gp", p.n("gp"))

    pending = None
    pend_fin = None
    for wi, (wn, bn, dst) in enumerate((("wq", "bq_a", "qo_s"), ("wk", "bk_a", "ko_s"))):
        for o in range(NCH):
            go = wi * NCH + o
            ab = go % 2
            for tc in range(NTC):
                g = tc
                ob = g % DP
                pb = g % 3
                wproj = [("qg", wld), ("act", pj_evac[pb])]
                for c in range(NCH):
                    p.em("tensor", lambda e, _c=c, _o=o, _pb=pb, _wn=wn, _tc=tc: e.matmul(
                        pj[_pb][:, :], wsb[_wn][:, _c * D + _o * 128:_c * D + _o * 128 + 128],
                        xn_f[:, _c * S + _tc * TCH:_c * S + (_tc + 1) * TCH],
                        start=(_c == 0), stop=(_c == NCH - 1)),
                        wait=wproj if c == 0 else (),
                        inc=(("pe", 1),) if c == NCH - 1 else ())
                pe_pj = p.n("pe")
                if pending is not None:
                    emit_pending(pending)
                    if pend_fin is not None:
                        flush_o(*pend_fin)
                        pend_fin = None
                p.em("scalar", lambda e, _ob=ob, _pb=pb, _bn=bn, _o=o: e.activation(
                    qraw[_ob][:, :], pj[_pb][:, :], AF.Identity,
                    bias=P[_bn][:, _o:_o + 1]),
                    wait=(("pe", pe_pj), ("dve", raw_used_tt[ob])), inc=(("act", 1),))
                pj_evac[pb] = p.n("act")
                pending = (ob, pb, tc, p.n("act"), ab)
                if tc == NTC - 1:
                    pend_fin = (go, ab, dst, o)
    emit_pending(pending)
    flush_o(*pend_fin)
    pending = None
    pend_fin = None
    for t in range(NT):
        vb = t % 2
        for half in range(2):
            wv0 = [("qg", wld), ("dve", v_tt)]
            for c in range(NCH):
                p.em("tensor", lambda e, _c=c, _t=t, _h=half: e.matmul(
                    vp[:, :], xn_f[:, _c * S + _t * 128:_c * S + (_t + 1) * 128],
                    wsb["wv"][:, _c * D + _h * TCH:_c * D + (_h + 1) * TCH],
                    start=(_c == 0), stop=(_c == NCH - 1)),
                    wait=wv0 if c == 0 else (),
                    inc=(("pe", 1),) if c == NCH - 1 else ())
            pe_v = p.n("pe")
            p.em("vector", lambda e, _vb=vb, _h=half: e.tensor_tensor(
                vo_b[_vb][:, :].rearrange("p (h c) -> p h c", c=65)[:, _h * 8:(_h + 1) * 8, 0:64],
                vp[:, :].rearrange("p (h c) -> p h c", c=64),
                P["bvb"][:, _h * TCH:(_h + 1) * TCH].rearrange("p (h c) -> p h c", c=64),
                op=ALU.add),
                wait=(("pe", pe_v), ("qc", vo_stq[vb])), inc=(("dve", 1),))
            v_tt = p.n("dve")
        p.em("scalar", lambda e, _vb=vb, _t=t: e.dma_start(
            T["vo_s"].ap()[_t * 128:(_t + 1) * 128, :], vo_b[_vb][:, :]),
            wait=(("dve", v_tt),), inc=(("qc", 16),))
        vo_stq[vb] = p.n("qc")

    finalize_bounds(p, P, nmax, nbred, nbsq, nb_sh, "negB1", 0)
    p.close()
    les.close()


def finalize_bounds(p, P, nmax, nbred, nbsq, nbp, dstname, rowoff):
    for col in range(2 * NCH):
        p.em("vector", lambda e, _c=col: e.tensor_reduce(
            nbred[0:1, _c:_c + 1], nmax[0:1, _c * NTC:(_c + 1) * NTC], axis=AX.X, op=ALU.max),
            wait=(("dve", p.n("dve")),) if col == 0 else (), inc=(("dve", 1),))
    p.em("scalar", lambda e: e.activation(nbsq[0:1, :], nbred[0:1, :], AF.Sqrt),
         wait=(("dve", p.n("dve")),), inc=(("act", 1),))
    a0 = p.n("act")
    for o in range(NCH):
        p.em("vector", lambda e, _o=o, _ro=rowoff: e.scalar_tensor_tensor(
            P["nbrow"][0:1, _ro * NCH + _o:_ro * NCH + _o + 1],
            nbsq[0:1, _o:_o + 1], -0.125, nbsq[0:1, NCH + _o:NCH + _o + 1],
            op0=ALU.mult, op1=ALU.mult),
            wait=(("act", a0),), inc=(("dve", 1),))
    dvp = p.n("dve")
    p.em("tensor", lambda e, _ro=rowoff: e.matmul(
        nbp[0:128, 0:NCH], P["ones_f"][0:1, :], P["nbrow"][0:1, _ro * NCH:(_ro + 1) * NCH],
        start=True, stop=True),
        wait=(("dve", dvp),), inc=(("pe", 1),))
    p.em("scalar", lambda e: e.copy(P[dstname][:, :], nbp[0:128, 0:NCH]),
         wait=(("pe", p.n("pe")),), inc=(("act", 1),))


def ph1b(k, T, P):
    nc = k.nc
    p = Ph(k)
    les = ExitStack()
    sbl = lambda n, shp, dt: les.enter_context(nc.sbuf_tensor(n, shp, dt))
    psl = lambda n, shp, dt: les.enter_context(nc.psum_tensor(n, shp, dt))

    wo_sb = sbl("wo_sb", [64, H * D], BF16)
    p.em("gpsimd", lambda e: e.dma_start(
        wo_sb[:, :].rearrange("p (h d) -> p h d", d=D),
        T["wo_a"].ap().rearrange("h p d -> p h d")), inc=(("qg", 16),))
    wld = p.n("qg")

    ko_sb = sbl("ko_sb", [128, NCH * S], BF16)
    vo_sb = sbl("vo_sb", [128, NT * H * 65], BF16)
    qp = [sbl(f"qp{b}", [128, TCH], BF16) for b in range(2)]
    pt = [sbl(f"pt{b}", [128, 1024], BF16) for b in range(3)]
    rcb = sbl("rcb", [128, TCH], F32)
    yn_sb = sbl("yn_sb", [64, H * TCH], BF16)
    Ab = [sbl(f"Ab{b}", [128, TCH], F32) for b in range(2)]

    sAB = psl("sAB", [128, 2048], F32)
    yA = [psl(f"yA{b}", [128, TCH], F32) for b in range(2)]
    yB = [psl(f"yB{b}", [128, TCH], F32) for b in range(2)]

    qp_used = [0, 0]
    yn_used = 0
    exp_last = 0
    exp_kt = {}
    av_kt = {}
    rb_read = [0, 0]
    rb_slot = [0, 0, 0, 0]
    rc_out = 0
    wo_bank = [("act", 0)] * 4
    yn_tt = [0] * NCH
    ab_st = [0, 0]

    prev_ld = p.n("qs")
    for qc in range(NTC):
        nk = 4 * qc + 4
        for c in range(NCH):
            p.em("sync", lambda e, _c=c, _qc=qc: e.dma_start(
                ko_sb[:, _c * S + _qc * TCH:_c * S + (_qc + 1) * TCH],
                T["ko_s"].ap()[_c][:, _qc * TCH:(_qc + 1) * TCH]),
                wait=(("qs", prev_ld),) if c == 0 else (), inc=(("qs", 16),))
        p.em("sync", lambda e, _qc=qc: e.dma_start(
            vo_sb[:, _qc * 4 * H * 65:(_qc + 1) * 4 * H * 65].rearrange(
                "p (k c) -> p k c", c=H * 65),
            T["vo_s"].ap()[_qc * TCH:(_qc + 1) * TCH, :].rearrange(
                "(k p) c -> p k c", p=128)), inc=(("qs", 16),))
        ld = p.n("qs")
        prev_ld = ld

        for c in range(NCH):
            hA, hB = 2 * c, 2 * c + 1
            cb_ = c % 2
            p.em("sync", lambda e, _c=c, _qc=qc, _b=c % 2: e.dma_start(
                qp[_b][:, :], T["qo_s"].ap()[_c][:, _qc * TCH:(_qc + 1) * TCH]),
                wait=(("pe", qp_used[c % 2]),), inc=(("qs", 16),))
            ld_q = p.n("qs")
            nkt = 4 * qc + 4

            def _scores(kt):
                r = kt % 2
                rp = kt % 3
                edge = kt >= 4 * qc
                jj = kt - 4 * qc
                qoff = 128 * jj if edge else 0
                N = 512 - qoff
                wsc = [("act", exp_kt.get(kt - 2, exp_last))]
                if kt == 0:
                    wsc += [("qs", ld), ("qs", ld_q), ("qg", wld)]
                for hh, (rlo, sbase) in enumerate(((0, 0), (64, 512))):
                    p.em("tensor", lambda e, _c=c, _kt=kt, _rlo=rlo, _r=r, _b=cb_,
                         _sb=sbase, _n=N, _qo=qoff, _e=edge: e.matmul(
                        sAB[:, _r * 1024 + _sb:_r * 1024 + _sb + _n],
                        ko_sb[_rlo:_rlo + 64, _c * S + _kt * 128:_c * S + (_kt + 1) * 128],
                        qp[_b][_rlo:_rlo + 64, _qo:TCH],
                        start=True, stop=not _e,
                        tile_position=(_rlo, 0)),
                        wait=wsc if hh == 0 else (),
                        inc=(("pe", 1),) if (hh == 1 and not edge) else ())
                    if edge:
                        p.em("tensor", lambda e, _r=r, _sb=sbase: e.matmul(
                            sAB[:, _r * 1024 + _sb:_r * 1024 + _sb + 128],
                            P["negI"][:, :], P["U01"][:, :],
                            start=False, stop=True),
                            inc=(("pe", 1),) if hh == 1 else ())
                pe_sc = p.n("pe")
                wexp = [("pe", pe_sc)]
                if kt >= 3:
                    wexp.append(("pe", av_kt[kt - 3]))
                p.em("scalar", lambda e, _r=r, _rp=rp, _X=N, _c=c: e.activation(
                    pt[_rp][:, :].rearrange("p (g x) -> p g x", g=2)[:, :, 0:_X],
                    sAB[:, _r * 1024:(_r + 1) * 1024].rearrange(
                        "p (g x) -> p g x", g=2)[:, :, 0:_X],
                    AF.Exp, bias=P["negB1"][:, _c:_c + 1], scale=0.125),
                    wait=wexp, inc=(("act", 1),))
                exp_kt[kt] = p.n("act")

            def _av(kt):
                rp = kt % 3
                edge = kt >= 4 * qc
                jj = kt - 4 * qc
                qoff = 128 * jj if edge else 0
                N = 512 - qoff
                wav = [("act", exp_kt[kt])]
                if kt == 0:
                    wav += [("dve", rb_read[cb_])]
                for hh, (yps, ybase, ytp, sbase, h) in enumerate(((yA[cb_], 0, (0, 0), 0, hA), (yB[cb_], 0, (0, 0), 512, hB))):
                    p.em("tensor", lambda e, _y=yps, _rp=rp, _sb=sbase,
                         _n=N, _qo=qoff, _kt=kt, _h=h, _yb=ybase, _yt=ytp,
                         _st=(kt == 0), _sp=(kt == nkt - 1): e.matmul(
                        _y[_yb:_yb + 65, _qo:_qo + _n],
                        vo_sb[:, _kt * H * 65 + _h * 65:_kt * H * 65 + (_h + 1) * 65],
                        pt[_rp][:, _sb:_sb + _n],
                        start=_st, stop=_sp, tile_position=_yt),
                        wait=wav if (hh == 0 or kt == 0) else (),
                        inc=(("pe", 1),) if hh == 1 else ())
                av_kt[kt] = p.n("pe")

            for kt in range(nkt):
                _scores(kt)
                if kt >= 2:
                    _av(kt - 2)
            if nkt >= 2:
                _av(nkt - 2)
            _av(nkt - 1)
            exp_last = exp_kt[nkt - 1]
            qp_used[cb_] = p.n("pe")
            pe_av_all = p.n("pe")
            for hh2, (yps, h) in enumerate(((yA[cb_], hA), (yB[cb_], hB))):
                slot = cb_ * 2 + hh2
                p.em("vector", lambda e, _y=yps: e.reciprocal(
                    rcb[64:65, :], _y[64:65, :]),
                    wait=(("pe", pe_av_all), ("qg", rc_out)), inc=(("dve", 1),))
                dv_rc = p.n("dve")
                p.em("gpsimd", lambda e, _s=slot: e.dma_start(
                    T["rb1_scr"].ap()[_s:_s + 1, :], rcb[64:65, :]),
                    wait=(("dve", dv_rc), ("qg", rb_slot[slot])), inc=(("qg", 16),))
                rc_out = p.n("qg")
                p.em("gpsimd", lambda e, _s=slot: e.dma_start(
                    rcb[0:64, :], bass.AP(T["rb1_scr"], _s * TCH, [[0, 64], [1, TCH]])),
                    wait=(("dve", rb_read[cb_]), ("qg", rc_out)), inc=(("qg", 16),))
                rb_in = p.n("qg")
                rb_slot[slot] = rb_in
                p.em("vector", lambda e, _y=yps, _h=h: e.tensor_tensor(
                    yn_sb[0:64, _h * TCH:(_h + 1) * TCH], _y[0:64, :], rcb[0:64, :],
                    op=ALU.mult),
                    wait=(("qg", rb_in), ("pe", yn_used)), inc=(("dve", 1),))
                rb_read[cb_] = p.n("dve")
                if hh2 == 1:
                    yn_tt[c] = p.n("dve")
        for o in range(NCH):
            r4 = o % 4
            ob = o % 2
            wwo = [wo_bank[r4], ("act", exp_last), ("dve", yn_tt[5])]
            for h in range(H):
                wh = wwo if h == 0 else \
                    ([("dve", yn_tt[h // 2])] if h // 2 >= 6 else ())
                p.em("tensor", lambda e, _o=o, _h=h, _r4=r4: e.matmul(
                    sAB[:, _r4 * 512:(_r4 + 1) * 512],
                    wo_sb[0:64, _h * D + _o * 128:_h * D + _o * 128 + 128],
                    yn_sb[0:64, _h * TCH:(_h + 1) * TCH],
                    start=(_h == 0), stop=(_h == H - 1)),
                    wait=wh, inc=(("pe", 1),) if h == H - 1 else ())
            pe_wo = p.n("pe")
            p.em("scalar", lambda e, _o=o, _ob=ob, _r4=r4: e.activation(
                Ab[_ob][:, :], sAB[:, _r4 * 512:(_r4 + 1) * 512], AF.Identity,
                bias=P["bo_a"][:, _o:_o + 1]),
                wait=(("pe", pe_wo), ("qc", ab_st[ob])), inc=(("act", 1),))
            wo_bank[r4] = ("act", p.n("act"))
            p.em("scalar", lambda e, _o=o, _ob=ob, _qc=qc: e.dma_start(
                T["A_s"].ap()[_o][:, _qc * TCH:(_qc + 1) * TCH], Ab[_ob][:, :]),
                wait=(("act", p.n("act")),), inc=(("qc", 16),))
            ab_st[ob] = p.n("qc")
        exp_last = p.n("act")
        yn_used = p.n("pe")
    p.close()
    les.close()


def ph2a(k, T, P):
    nc = k.nc
    p = Ph(k)
    les = ExitStack()
    sbl = lambda n, shp, dt: les.enter_context(nc.sbuf_tensor(n, shp, dt))
    psl = lambda n, shp, dt: les.enter_context(nc.psum_tensor(n, shp, dt))

    wq2_sb = sbl("wq2_sb", [128, NCH * D], BF16)
    wkva_sb = sbl("wkva_sb", [128, NCH * 160], BF16)
    wkvb_sb = sbl("wkvb_sb", [128, 512], BF16)
    p.em("gpsimd", lambda e: e.dma_start(
        wq2_sb[:, :].rearrange("p (c d) -> p c d", d=D),
        T["wq_m"].ap().rearrange("c p d -> p c d")), inc=(("qg", 16),))
    p.em("gpsimd", lambda e: e.dma_start(
        wkva_sb[:, :].rearrange("p (c d) -> p c d", d=160),
        T["wkva"].ap().rearrange("c p d -> p c d")), inc=(("qg", 16),))
    p.em("gpsimd", lambda e: e.dma_start(wkvb_sb[:, :], T["wkvb"].ap()), inc=(("qg", 16),))
    wld = p.n("qg")
    ct2f = sbl("ct2f", [128, S], F32)
    st2f = sbl("st2f", [128, S], F32)
    p.em("sync", lambda e: e.dma_start(ct2f[:, :], T["cs2c"].ap()), inc=(("qs", 16),))
    p.em("sync", lambda e: e.dma_start(st2f[:, :], T["cs2s"].ap()), inc=(("qs", 16),))
    tabld = p.n("qs")

    z1f = sbl("z1f", [128, NCH * S], BF16)
    xnc = [sbl(f"xnc{b}", [128, NCH * TCH], BF16) for b in range(2)]
    sqz = [sbl(f"sqz{b}", [128, TCH], BF16) for b in range(2)]
    rms1 = sbl("rms1", [1, TCH], F32)
    rinv1 = sbl("rinv1", [1, TCH], F32)
    rb1 = [sbl(f"rb1_{b}", [128, TCH], F32) for b in range(2)]
    DP = 4
    qraw = [sbl(f"qraw2{b}", [128, TCH], BF16) for b in range(DP)]
    qc_t = [sbl(f"qc_t2{b}", [128, TCH], F32) for b in range(DP)]
    qr_t = [sbl(f"qr_t2{b}", [128, TCH], F32) for b in range(DP)]
    qacc = [sbl(f"qacc2{b}", [128, S], BF16) for b in range(2)]
    sq_t = [sbl(f"sq_t2{b}", [128, TCH], BF16) for b in range(2)]
    kvl = [sbl(f"kvl{b}", [128, TCH], BF16) for b in range(2)]
    krr = sbl("krr", [128, TCH], BF16)
    kmb = [sbl(f"kmb{b}", [128, TCH], BF16) for b in range(2)]
    ktb = [sbl(f"ktb{b}", [128, H * 33], BF16) for b in range(2)]
    nmax = sbl("nmax2w", [1, 2 * NCH * NTC], F32)
    nbred = sbl("nbred2", [1, 2 * NCH], F32)
    nbsq = sbl("nbsq2", [1, 2 * NCH], F32)

    pj = [psl(f"pj2{b}", [128, TCH], F32) for b in range(3)]
    rot = [psl(f"rot2{b}", [128, TCH], F32) for b in range(3)]
    statb = psl("stat2", [128, TCH], F32)
    ktp = psl("ktp", [128, TCH], F32)

    for b in range(2):
        p.em("vector", lambda e, _b=b: e.memset(
            ktb[_b][:, :].rearrange("p (h c) -> p h c", c=33)[:, :, 32:33], 1.0),
            inc=(("dve", 1),))

    z1_done = [("dve", 0)] * NTC
    ldc = [0] * NCH
    statmm = [0, 0]
    sqrt_act = 0
    recip_dve = 0
    scrout = [0, 0]
    bcin = [0, 0]
    for tc in range(NTC):
        b = tc % 2
        for c in range(NCH):
            wx = []
            if c == 0 and tc >= 2:
                wx = [z1_done[tc - 2]]
            p.em("sync", lambda e, _c=c, _tc=tc, _b=b: e.dma_start(
                xnc[_b][:, _c * TCH:(_c + 1) * TCH],
                T["xn_s"].ap()[_c][:, _tc * TCH:(_tc + 1) * TCH]),
                wait=wx, inc=(("qs", 16),))
            ldc[c] = p.n("qs")
        for c in range(NCH):
            cb = c % 2
            p.em("scalar", lambda e, _c=c, _cb=cb, _b=b: e.activation(
                sqz[_cb][:, :], xnc[_b][:, _c * TCH:(_c + 1) * TCH], AF.Square),
                wait=(("qs", ldc[_c if False else c]), ("pe", statmm[cb])),
                inc=(("act", 1),))
            sq_a = p.n("act")
            wmm = [("act", sq_a)]
            if c == 0:
                wmm.append(("act", sqrt_act))
            p.em("tensor", lambda e, _c=c, _cb=cb: e.matmul(
                statb[0:1, :], P["ones_b"][:, 0:1], sqz[_cb][:, :],
                start=(_c == 0), stop=(_c == NCH - 1)),
                wait=wmm, inc=(("pe", 1),))
            statmm[cb] = p.n("pe")
        p.em("scalar", lambda e: e.activation(rms1[0:1, :], statb[0:1, :], AF.Sqrt,
                                              bias=P["epsv"][0:1, 0:1], scale=1.0 / D),
             wait=(("pe", statmm[(NCH - 1) % 2]), ("dve", recip_dve)), inc=(("act", 1),))
        sqrt_act = p.n("act")
        p.em("vector", lambda e: e.reciprocal(rinv1[0:1, :], rms1[0:1, :]),
             wait=[("act", sqrt_act)] + ([("qg", scrout[(tc - 1) % 2])] if tc >= 1 else []),
             inc=(("dve", 1),))
        recip_dve = p.n("dve")
        p.em("gpsimd", lambda e, _s=tc % 2: e.dma_start(
            T["rb1_scr"].ap()[_s:_s + 1, :], rinv1[0:1, :]),
            wait=[("dve", recip_dve)] + ([("qg", bcin[tc % 2])] if tc >= 2 else []),
            inc=(("qg", 16),))
        scrout[tc % 2] = p.n("qg")
        wbc = [("qg", scrout[tc % 2])]
        if tc >= 2:
            wbc.append(z1_done[tc - 2])
        p.em("gpsimd", lambda e, _b=b, _s=tc % 2: e.dma_start(
            rb1[_b][:, :], bass.AP(T["rb1_scr"], _s * TCH, [[0, 128], [1, TCH]])),
            wait=wbc, inc=(("qg", 16),))
        bcin[tc % 2] = p.n("qg")
        for c in range(NCH):
            wz = [("qg", bcin[tc % 2])] if c == 0 else []
            p.em("vector", lambda e, _c=c, _b=b, _tc=tc: e.scalar_tensor_tensor(
                z1f[:, _c * S + _tc * TCH:_c * S + (_tc + 1) * TCH],
                xnc[_b][:, _c * TCH:(_c + 1) * TCH],
                P["wln1"][:, _c:_c + 1], rb1[_b][:, :], op0=ALU.mult, op1=ALU.mult),
                wait=wz, inc=(("dve", 1),))
        z1_done[tc] = ("dve", p.n("dve"))

    pj_evac = [0, 0, 0]
    raw_used_tt = [0] * DP
    rot_used = [0, 0, 0]
    qadd_gp = [("gp", 0)] * DP
    qacc_st = [0, 0]
    qacc_sq = [0, 0]
    prev_red = 0
    nrm_pe = [0, 0]
    for o in range(NCH):
        ab = o % 2
        for tc in range(NTC):
            g = o * NTC + tc
            ob = g % DP
            pb = g % 3
            wproj = [("qg", wld), ("act", pj_evac[pb])]
            if o == 0:
                wproj.append(z1_done[tc])
            for c in range(NCH):
                p.em("tensor", lambda e, _c=c, _o=o, _pb=pb, _tc=tc: e.matmul(
                    pj[_pb][:, :], wq2_sb[:, _c * D + _o * 128:_c * D + _o * 128 + 128],
                    z1f[:, _c * S + _tc * TCH:_c * S + (_tc + 1) * TCH],
                    start=(_c == 0), stop=(_c == NCH - 1)),
                    wait=wproj if c == 0 else (),
                    inc=(("pe", 1),) if c == NCH - 1 else ())
            pe_pj = p.n("pe")
            p.em("scalar", lambda e, _ob=ob, _pb=pb: e.copy(
                qraw[_ob][:, :], pj[_pb][:, :]),
                wait=(("pe", pe_pj), ("dve", raw_used_tt[ob])), inc=(("act", 1),))
            pj_evac[pb] = p.n("act")
            act_raw = p.n("act")
            rb = g % 3
            p.em("tensor", lambda e, _ob=ob, _rb=rb: e.matmul(
                rot[_rb][:, :], P["Rm"][:, :], qraw[_ob][:, :], start=True, stop=True),
                wait=(("act", act_raw), ("dve", rot_used[rb])), inc=(("pe", 1),))
            pe_rot = p.n("pe")
            wtt = [("act", act_raw), qadd_gp[ob]]
            if g == 0:
                wtt.append(("qs", tabld))
            p.em("vector", lambda e, _ob=ob, _tc=tc: e.tensor_tensor(
                qc_t[_ob][:, :], qraw[_ob][:, :],
                ct2f[:, _tc * TCH:(_tc + 1) * TCH], op=ALU.mult),
                wait=wtt, inc=(("dve", 1),))
            raw_used_tt[ob] = p.n("dve")
            p.em("vector", lambda e, _ob=ob, _rb=rb, _tc=tc: e.tensor_tensor(
                qr_t[_ob][:, :], rot[_rb][:, :],
                st2f[:, _tc * TCH:(_tc + 1) * TCH], op=ALU.mult),
                wait=(("pe", pe_rot), qadd_gp[ob]), inc=(("dve", 1),))
            rot_used[rb] = p.n("dve")
            dv2 = p.n("dve")
            wadd = [("dve", dv2)]
            if tc == 0:
                wadd += [("qc", qacc_st[ab]), ("act", qacc_sq[ab])]
            p.em("gpsimd", lambda e, _ob=ob, _ab=ab, _tc=tc: e.tensor_tensor(
                qacc[_ab][:, _tc * TCH:(_tc + 1) * TCH],
                qc_t[_ob][:, :], qr_t[_ob][:, :], op=ALU.add),
                wait=wadd, inc=(("gp", 1),))
            qadd_gp[ob] = ("gp", p.n("gp"))
            gp_add = p.n("gp")
            sb = tc % 2
            p.em("scalar", lambda e, _ab=ab, _tc=tc, _sb=sb: e.activation(
                sq_t[_sb][:, :], qacc[_ab][:, _tc * TCH:(_tc + 1) * TCH], AF.Square),
                wait=(("gp", gp_add), ("pe", nrm_pe[sb])), inc=(("act", 1),))
            qacc_sq[ab] = p.n("act")
            p.em("tensor", lambda e, _sb=sb: e.matmul(
                ktp[0:1, :], P["ones_b"][:, 0:1], sq_t[_sb][:, :],
                start=True, stop=True),
                wait=(("act", p.n("act")), ("dve", prev_red)), inc=(("pe", 1),))
            nrm_pe[sb] = p.n("pe")
            col = o * NTC + tc
            p.em("vector", lambda e, _col=col: e.tensor_reduce(
                nmax[0:1, _col:_col + 1], ktp[0:1, :], axis=AX.X, op=ALU.max),
                wait=(("pe", nrm_pe[sb]),), inc=(("dve", 1),))
            prev_red = p.n("dve")
        gp_all = p.n("gp")
        p.em("scalar", lambda e, _o=o, _ab=ab: e.dma_start(
            T["qm_s"].ap()[_o], qacc[_ab][:, :]),
            wait=(("gp", gp_all), ("act", qacc_sq[ab])), inc=(("qc", 16),))
        qacc_st[ab] = p.n("qc")
    q_red_last = prev_red
    q_nrm_last = max(nrm_pe)

    kvl_act = 0
    krr_tt = 0
    kvp_bank = [0, 0]
    kmb_st = [0, 0]
    kmb_sq = [0, 0]
    kvp_ev = 0
    ktb_st = [0, 0]
    ktp_copy = 0
    first_ktok = True
    first_sq = True
    for tc in range(NTC):
        kb = tc % 2
        pb = tc % 3
        for c in range(NCH):
            p.em("tensor", lambda e, _c=c, _pb=pb, _tc=tc: e.matmul(
                pj[_pb][:, :], wkva_sb[:, _c * 160:_c * 160 + 128],
                z1f[:, _c * S + _tc * TCH:_c * S + (_tc + 1) * TCH],
                start=(_c == 0), stop=(_c == NCH - 1)),
                wait=(("act", pj_evac[pb]),) if c == 0 else (),
                inc=(("pe", 1),) if c == NCH - 1 else ())
        pe_kvl = p.n("pe")
        p.em("scalar", lambda e, _kb=kb, _pb=pb: e.copy(kvl[_kb][:, :], pj[_pb][:, :]),
             wait=(("pe", pe_kvl),), inc=(("act", 1),))
        pj_evac[pb] = p.n("act")
        kvl_act = p.n("act")
        for c in range(NCH):
            p.em("tensor", lambda e, _c=c, _tc=tc: e.matmul(
                rot[0][32:64, :], wkva_sb[:, _c * 160 + 128:_c * 160 + 160],
                z1f[:, _c * S + _tc * TCH:_c * S + (_tc + 1) * TCH],
                start=(_c == 0), stop=(_c == NCH - 1), tile_position=(0, 32)),
                wait=(("dve", rot_used[0]),) if c == 0 else (),
                inc=(("pe", 1),) if c == NCH - 1 else ())
        pe_kr = p.n("pe")
        p.em("scalar", lambda e: e.copy(krr[32:64, :], rot[0][32:64, :]),
             wait=(("pe", pe_kr), ("dve", krr_tt)), inc=(("act", 1),))
        act_krr = p.n("act")
        p.em("tensor", lambda e: e.matmul(
            rot[1][32:64, :], P["Rm"][32:64, 32:64], krr[32:64, :],
            start=True, stop=True, tile_position=(32, 32)),
            wait=(("act", act_krr), ("dve", rot_used[1])), inc=(("pe", 1),))
        pe_krot = p.n("pe")
        p.em("vector", lambda e, _tc=tc: e.tensor_tensor(
            qc_t[0][32:64, :], krr[32:64, :],
            ct2f[32:64, _tc * TCH:(_tc + 1) * TCH], op=ALU.mult),
            wait=(("act", act_krr), qadd_gp[0]), inc=(("dve", 1),))
        krr_tt = p.n("dve")
        p.em("vector", lambda e, _tc=tc: e.tensor_tensor(
            qr_t[0][32:64, :], rot[1][32:64, :],
            st2f[32:64, _tc * TCH:(_tc + 1) * TCH], op=ALU.mult),
            wait=(("pe", pe_krot), qadd_gp[0]), inc=(("dve", 1),))
        rot_used[0] = p.n("dve")
        rot_used[1] = p.n("dve")
        dvkr = p.n("dve")
        p.em("gpsimd", lambda e: e.tensor_tensor(
            kmb[0][32:64, :], qc_t[0][32:64, :], qr_t[0][32:64, :], op=ALU.add),
            wait=(("dve", dvkr), ("qs", kmb_st[0]), ("act", kvp_ev)), inc=(("gp", 1),))
        qadd_gp[0] = ("gp", p.n("gp"))
        p.em("gpsimd", lambda e: e.tensor_copy(kmb[1][32:64, :], kmb[0][32:64, :]),
             wait=(("gp", p.n("gp")), ("qs", kmb_st[1])), inc=(("gp", 1),))
        gp_rope = p.n("gp")
        pjB = pj[(tc + 1) % 3]
        for c in range(NCH):
            cb = c % 2
            bank = rot[2] if cb == 0 else pjB
            wkv = [("act", kvp_bank[cb])]
            if c == 0:
                wkv += [("act", kvl_act), ("dve", rot_used[2])]
            if c == 1:
                wkv.append(("act", pj_evac[(tc + 1) % 3]))
            p.em("tensor", lambda e, _c=c, _kb=kb, _bk=bank: e.matmul(
                _bk[0:64, :], wkvb_sb[:, _c * 64:(_c + 1) * 64], kvl[_kb][:, :],
                start=True, stop=True),
                wait=wkv, inc=(("pe", 1),))
            pe_kv = p.n("pe")
            rot_used[2] = 0
            wcp = [("pe", pe_kv), ("qs", kmb_st[cb]), ("gp", kmb_sq[cb])]
            if c < 2:
                wcp.append(("gp", gp_rope))
            p.em("scalar", lambda e, _cb=cb, _bk=bank: e.copy(
                kmb[_cb][0:32, :], _bk[0:32, :]),
                 wait=wcp, inc=(("act", 1),))
            p.em("scalar", lambda e, _cb=cb, _bk=bank: e.copy(
                kmb[_cb][64:96, :], _bk[32:64, :]),
                 inc=(("act", 1),))
            kvp_ev = p.n("act")
            kvp_bank[cb] = kvp_ev
            wsq = [("act", kvp_ev)]
            if first_sq:
                wsq.append(("pe", q_nrm_last))
                first_sq = False
            p.em("gpsimd", lambda e, _cb=cb: e.tensor_tensor(
                sq_t[_cb][0:96, :], kmb[_cb][0:96, :], kmb[_cb][0:96, :], op=ALU.mult),
                wait=wsq, inc=(("gp", 1),))
            kmb_sq[cb] = p.n("gp")
            p.em("tensor", lambda e, _cb=cb: e.matmul(
                statb[0:1, :], P["ones_b"][0:96, 0:1], sq_t[_cb][0:96, :],
                start=True, stop=True),
                wait=(("gp", kmb_sq[cb]), ("dve", prev_red)), inc=(("pe", 1),))
            pe_nrm = p.n("pe")
            col = (NCH + c) * NTC + tc
            p.em("vector", lambda e, _col=col: e.tensor_reduce(
                nmax[0:1, _col:_col + 1], statb[0:1, :], axis=AX.X, op=ALU.max),
                wait=(("pe", pe_nrm),), inc=(("dve", 1),))
            prev_red = p.n("dve")
            p.em("sync", lambda e, _c=c, _cb=cb, _tc=tc: e.dma_start(
                T["km_s"].ap()[_c][:, _tc * TCH:(_tc + 1) * TCH], kmb[_cb][0:96, :]),
                wait=(("act", kvp_ev), ("gp", gp_rope)), inc=(("qs", 16),))
            kmb_st[cb] = p.n("qs")
        pj_evac[(tc + 1) % 3] = kvp_ev
        for i4 in range(4):
            ib = i4 % 2
            wkt = [("act", ktp_copy)]
            if first_ktok:
                wkt.append(("dve", q_red_last))
                first_ktok = False
            if i4 == 0:
                wkt.append(("act", kvl_act))
            p.em("tensor", lambda e, _i=i4, _kb=kb: e.matmul(
                ktp[:, :], kvl[_kb][:, _i * 128:(_i + 1) * 128], wkvb_sb[:, :],
                start=True, stop=True),
                wait=wkt, inc=(("pe", 1),))
            pe_kt = p.n("pe")
            p.em("scalar", lambda e, _ib=ib: e.copy(
                ktb[_ib][:, :].rearrange("p (h c) -> p h c", c=33)[:, :, 0:32],
                ktp[:, :].rearrange("p (h c) -> p h c", c=32)),
                wait=(("pe", pe_kt), ("qs", ktb_st[_ib if False else ib])), inc=(("act", 1),))
            ktp_copy = p.n("act")
            p.em("sync", lambda e, _ib=ib, _t=tc * 4 + i4: e.dma_start(
                T["kt_s"].ap()[_t * 128:(_t + 1) * 128, :], ktb[_ib][:, :]),
                wait=(("act", ktp_copy),), inc=(("qs", 16),))
            ktb_st[ib] = p.n("qs")

    finalize_bounds(p, P, nmax, nbred, nbsq, statb, "negB2", 1)
    p.close()
    les.close()


def ph2b(k, T, P):
    nc = k.nc
    p = Ph(k)
    les = ExitStack()
    sbl = lambda n, shp, dt: les.enter_context(nc.sbuf_tensor(n, shp, dt))
    psl = lambda n, shp, dt: les.enter_context(nc.psum_tensor(n, shp, dt))

    wo2_sb = sbl("wo2_sb", [96, NCH * D], BF16)
    p.em("gpsimd", lambda e: e.dma_start(
        wo2_sb[:, :].rearrange("p (r d) -> p r d", d=D),
        T["wo_m"].ap().rearrange("r p d -> p r d")), inc=(("qg", 16),))
    wld = p.n("qg")

    km_sb = sbl("km_sb", [128, NCH * S], BF16)
    kt_sb = sbl("kt_sb", [128, NT * H * 33], BF16)
    qp2 = [sbl(f"qp2{b}", [128, TCH], BF16) for b in range(2)]
    xo = [sbl(f"xo{b}", [128, TCH], BF16) for b in range(2)]
    pt = [sbl(f"pt2{b}", [128, 1024], BF16) for b in range(3)]
    rcb = sbl("rcb2", [128, TCH], F32)
    yn2 = sbl("yn2", [96, NCH * TCH], BF16)
    ab2 = [sbl(f"ab2{b}", [128, TCH], F32) for b in range(2)]

    sAB = psl("sAB2", [128, 2048], F32)
    yA = [psl(f"yA2{b}", [128, TCH], F32) for b in range(2)]
    yB = [psl(f"yB2{b}", [128, TCH], F32) for b in range(2)]

    p.em("vector", lambda e: e.memset(yn2[32:64, :], 0.0), inc=(("dve", 1),))

    qp_used = [0, 0]
    yn_used = 0
    exp_last = 0
    exp_kt = {}
    av_kt = {}
    rb_read = [0, 0]
    rb_slot = [0, 0, 0, 0]
    rc_out = 0
    wo_bank = [("dve", 0)] * 4
    yn_tt = [0] * NCH
    wo_dve_all = 0
    xo_used = [0, 0]
    ab_st = [0, 0]

    prev_ld = p.n("qs")
    for qc in range(NTC):
        for c in range(NCH):
            p.em("sync", lambda e, _c=c, _qc=qc: e.dma_start(
                km_sb[0:96, _c * S + _qc * TCH:_c * S + (_qc + 1) * TCH],
                T["km_s"].ap()[_c][:, _qc * TCH:(_qc + 1) * TCH]),
                wait=(("qs", prev_ld),) if c == 0 else (), inc=(("qs", 16),))
            p.em("sync", lambda e, _c=c, _qc=qc: e.dma_start(
                km_sb[96:128, _c * S + _qc * TCH:_c * S + (_qc + 1) * TCH],
                T["km_s"].ap()[_c][32:64, _qc * TCH:(_qc + 1) * TCH]), inc=(("qs", 16),))
        p.em("sync", lambda e, _qc=qc: e.dma_start(
            kt_sb[:, _qc * 4 * H * 33:(_qc + 1) * 4 * H * 33].rearrange(
                "p (k c) -> p k c", c=H * 33),
            T["kt_s"].ap()[_qc * TCH:(_qc + 1) * TCH, :].rearrange(
                "(k p) c -> p k c", p=128)), inc=(("qs", 16),))
        ld = p.n("qs")
        prev_ld = ld

        for c in range(NCH):
            hA, hB = 2 * c, 2 * c + 1
            cb_ = c % 2
            p.em("sync", lambda e, _c=c, _qc=qc, _b=c % 2: e.dma_start(
                qp2[_b][:, :], T["qm_s"].ap()[_c][:, _qc * TCH:(_qc + 1) * TCH]),
                wait=(("pe", qp_used[c % 2]),), inc=(("qs", 16),))
            ld_q = p.n("qs")
            nkt = 4 * qc + 4

            def _scores(kt):
                r = kt % 2
                rp = kt % 3
                edge = kt >= 4 * qc
                jj = kt - 4 * qc
                qoff = 128 * jj if edge else 0
                N = 512 - qoff
                wsc = [("act", exp_kt.get(kt - 2, exp_last))]
                if kt == 0:
                    wsc += [("qs", ld), ("qs", ld_q), ("qg", wld)]
                if kt < 2:
                    wsc.append(("dve", wo_dve_all))
                for hh, (rlo, sbase) in enumerate(((0, 0), (64, 512))):
                    p.em("tensor", lambda e, _c=c, _kt=kt, _rlo=rlo, _r=r, _b=cb_,
                         _sb=sbase, _n=N, _qo=qoff, _e=edge: e.matmul(
                        sAB[:, _r * 1024 + _sb:_r * 1024 + _sb + _n],
                        km_sb[_rlo:_rlo + 64, _c * S + _kt * 128:_c * S + (_kt + 1) * 128],
                        qp2[_b][_rlo:_rlo + 64, _qo:TCH],
                        start=True, stop=not _e,
                        tile_position=(_rlo, 0)),
                        wait=wsc if hh == 0 else (),
                        inc=(("pe", 1),) if (hh == 1 and not edge) else ())
                    if edge:
                        p.em("tensor", lambda e, _r=r, _sb=sbase: e.matmul(
                            sAB[:, _r * 1024 + _sb:_r * 1024 + _sb + 128],
                            P["negI"][:, :], P["U01"][:, :],
                            start=False, stop=True),
                            inc=(("pe", 1),) if hh == 1 else ())
                pe_sc = p.n("pe")
                wexp = [("pe", pe_sc)]
                if kt >= 3:
                    wexp.append(("pe", av_kt[kt - 3]))
                p.em("scalar", lambda e, _r=r, _rp=rp, _X=N, _c=c: e.activation(
                    pt[_rp][:, :].rearrange("p (g x) -> p g x", g=2)[:, :, 0:_X],
                    sAB[:, _r * 1024:(_r + 1) * 1024].rearrange(
                        "p (g x) -> p g x", g=2)[:, :, 0:_X],
                    AF.Exp, bias=P["negB2"][:, _c:_c + 1], scale=0.125),
                    wait=wexp, inc=(("act", 1),))
                exp_kt[kt] = p.n("act")

            def _av(kt):
                rp = kt % 3
                edge = kt >= 4 * qc
                jj = kt - 4 * qc
                qoff = 128 * jj if edge else 0
                N = 512 - qoff
                wav = [("act", exp_kt[kt])]
                if kt == 0:
                    wav += [("dve", rb_read[cb_])]
                for hh, (yps, ybase, ytp, sbase, h) in enumerate(((yA[cb_], 0, (0, 0), 0, hA), (yB[cb_], 64, (0, 64), 512, hB))):
                    p.em("tensor", lambda e, _y=yps, _rp=rp, _sb=sbase,
                         _n=N, _qo=qoff, _kt=kt, _h=h, _yb=ybase, _yt=ytp,
                         _st=(kt == 0), _sp=(kt == nkt - 1): e.matmul(
                        _y[_yb:_yb + 33, _qo:_qo + _n],
                        kt_sb[:, _kt * H * 33 + _h * 33:_kt * H * 33 + (_h + 1) * 33],
                        pt[_rp][:, _sb:_sb + _n],
                        start=_st, stop=_sp, tile_position=_yt),
                        wait=wav if (hh == 0 or kt == 0) else (),
                        inc=(("pe", 1),) if hh == 1 else ())
                av_kt[kt] = p.n("pe")

            for kt in range(nkt):
                _scores(kt)
                if kt >= 2:
                    _av(kt - 2)
            if nkt >= 2:
                _av(nkt - 2)
            _av(nkt - 1)
            exp_last = exp_kt[nkt - 1]
            qp_used[cb_] = p.n("pe")
            pe_av_all = p.n("pe")
            for hh2, (yps, h, dlo, klo) in enumerate(
                    ((yA[cb_], hA, 32, 0), (yB[cb_], hB, 96, 64))):
                slot = cb_ * 2 + hh2
                p.em("vector", lambda e, _y=yps, _d=dlo: e.reciprocal(
                    rcb[_d:_d + 1, :], _y[_d:_d + 1, :]),
                    wait=(("pe", pe_av_all), ("qg", rc_out)), inc=(("dve", 1),))
                dv_rc = p.n("dve")
                p.em("gpsimd", lambda e, _d=dlo, _s=slot: e.dma_start(
                    T["rb2_scr"].ap()[_s:_s + 1, :], rcb[_d:_d + 1, :]),
                    wait=(("dve", dv_rc), ("qg", rb_slot[slot])), inc=(("qg", 16),))
                rc_out = p.n("qg")
                p.em("gpsimd", lambda e, _k=klo, _s=slot: e.dma_start(
                    rcb[_k:_k + 32, :], bass.AP(T["rb2_scr"], _s * TCH, [[0, 32], [1, TCH]])),
                    wait=(("dve", rb_read[cb_]), ("qg", rc_out)), inc=(("qg", 16),))
                rb_in = p.n("qg")
                rb_slot[slot] = rb_in
                p.em("vector", lambda e, _y=yps, _c=c, _k=klo: e.tensor_tensor(
                    yn2[_k:_k + 32, _c * TCH:(_c + 1) * TCH], _y[_k:_k + 32, :],
                    rcb[_k:_k + 32, :], op=ALU.mult),
                    wait=(("qg", rb_in), ("pe", yn_used)), inc=(("dve", 1),))
                rb_read[cb_] = p.n("dve")
                if hh2 == 1:
                    yn_tt[c] = p.n("dve")
        for o in range(NCH):
            r4 = o % 4
            ob = o % 2
            p.em("sync", lambda e, _o=o, _ob=ob, _qc=qc: e.dma_start(
                xo[_ob][:, :], T["xn_s"].ap()[_o][:, _qc * TCH:(_qc + 1) * TCH]),
                wait=(("dve", xo_used[ob]),), inc=(("qs", 16),))
            ld_x = p.n("qs")
            wwo = [wo_bank[r4], ("act", exp_last), ("dve", yn_tt[4])]
            for r in range(NCH):
                wh = wwo if r == 0 else \
                    ([("dve", yn_tt[r])] if r >= 5 else ())
                p.em("tensor", lambda e, _o=o, _r=r, _r4=r4: e.matmul(
                    sAB[:, _r4 * 512:(_r4 + 1) * 512],
                    wo2_sb[0:96, _r * D + _o * 128:_r * D + _o * 128 + 128],
                    yn2[0:96, _r * TCH:(_r + 1) * TCH],
                    start=(_r == 0), stop=(_r == NCH - 1)),
                    wait=wh, inc=(("pe", 1),) if r == NCH - 1 else ())
            pe_wo = p.n("pe")
            p.em("vector", lambda e, _o=o, _ob=ob, _r4=r4: e.tensor_tensor(
                ab2[_ob][:, :], sAB[:, _r4 * 512:(_r4 + 1) * 512], xo[_ob][:, :],
                op=ALU.add),
                wait=(("pe", pe_wo), ("qc", ab_st[ob]), ("qs", ld_x)), inc=(("dve", 1),))
            wo_bank[r4] = ("dve", p.n("dve"))
            xo_used[ob] = p.n("dve")
            p.em("scalar", lambda e, _o=o, _ob=ob, _qc=qc: e.dma_start(
                T["a_s"].ap()[_o][:, _qc * TCH:(_qc + 1) * TCH], ab2[_ob][:, :]),
                wait=(("dve", p.n("dve")),), inc=(("qc", 16),))
            ab_st[ob] = p.n("qc")
        wo_dve_all = p.n("dve")
        yn_used = p.n("pe")
    p.close()
    les.close()


def ph3(k, T, P):
    nc = k.nc
    p = Ph(k)
    les = ExitStack()
    sbl = lambda n, shp, dt: les.enter_context(nc.sbuf_tensor(n, shp, dt))
    psl = lambda n, shp, dt: les.enter_context(nc.psum_tensor(n, shp, dt))
    CH3 = 1024
    NCHK = S // CH3
    NE = E + 1

    wgate_sb = sbl("wgate_sb", [128, NCH * E], BF16)
    oh_sb = sbl("oh_sb", [E, E * 128], F32)
    p.em("gpsimd", lambda e: e.dma_start(
        wgate_sb[:, :].rearrange("p (c d) -> p c d", d=E),
        T["wgate"].ap().rearrange("c p d -> p c d")), inc=(("qg", 16),))
    p.em("gpsimd", lambda e: e.dma_start(oh_sb[0:E, :], T["onehot"].ap()),
         inc=(("qg", 16),))
    wgate_ld = p.n("qg")

    h_all = sbl("h_all", [128, NE * 4 * CH3], BF16)
    z2 = sbl("z2", [128, NCH * CH3], BF16)
    wg_sb = [sbl(f"wg_sb{b}", [128, NCH * II], BF16) for b in range(2)]
    wu_sb = [sbl(f"wu_sb{b}", [128, NCH * II], BF16) for b in range(2)]
    wd_sb = [sbl(f"wd_sb{b}", [128, 4 * D], BF16) for b in range(2)]
    rb2 = sbl("rb2", [128, CH3], F32)
    cbb = sbl("cbb", [128, CH3], F32)
    comb_f = sbl("comb_f", [8, CH3], F32)
    a1 = [sbl(f"a1_{b}", [128, CH3], F32) for b in range(2)]
    sqa = [sbl(f"sqa{b}", [128, TCH], BF16) for b in range(2)]
    rms3 = sbl("rms3", [1, TCH], F32)
    rinv3 = sbl("rinv3", [1, TCH], F32)
    sg = [sbl(f"sg{b}", [128, TCH], BF16) for b in range(2)]
    us = [sbl(f"us{b}", [128, TCH], BF16) for b in range(2)]
    lg = sbl("lg", [128, E], F32)
    lg2 = sbl("lg2", [128, E], F32)
    mx = sbl("mx3", [128, 1], F32)
    eg = sbl("eg", [128, E], F32)
    srt = sbl("srt", [128, E], F32)
    msk = sbl("msk", [128, E], F32)
    wun = sbl("wun", [128, E], F32)
    s12 = sbl("s12", [128, 1], F32)
    rs12 = sbl("rs12", [128, 1], F32)
    combt = sbl("combt", [128, E], F32)
    a3 = [sbl(f"a3_{b}", [128, 256], F32) for b in range(2)]
    A3 = [sbl(f"A3_{b}", [128, 256], F32) for b in range(2)]
    s1b = [sbl(f"s1b{b}", [128, 256], F32) for b in range(2)]
    s2b = [sbl(f"s2b{b}", [128, 256], F32) for b in range(2)]
    x3 = [sbl(f"x3_{b}", [128, D], F32) for b in range(2)]
    out_sb = [sbl(f"out_sb{b}", [128, D], F32) for b in range(2)]

    gu_g = psl("gu_g", [128, TCH], F32)
    gu_u = psl("gu_u", [128, TCH], F32)
    moe_ps = psl("moe_ps", [128, 2048], F32)
    tp3 = psl("tp3", [128, 256], F32)
    stat3 = psl("stat3", [128, TCH], F32)

    a1_act = [0, 0]
    a1_dve = [("dve", 0), ("dve", 0)]
    sqa_used = [0, 0]
    stat3_free = ("act", 0)
    z2_used = 0
    h_used = 0
    rb2_used = ("dve", 0)
    gug_sil = 0
    uu_tt = 0
    cbb_rd = 0
    silu_rd = [0, 0]
    uu_rd = [0, 0]
    wld_g = [0, 0]
    wldd = [0, 0]
    wst_g = [0, 0]
    wst_d = [0, 0]
    gate_dve = 0
    tpg_evac = 0
    lg_evac = 0
    cb_out_prev = 0
    cbb_in = 0
    moe_evac = 0
    a3_used = [0, 0]
    A3_used = [0, 0]
    s1_used = [0, 0]
    s2_used = [0, 0]
    x3_used = [0, 0]
    out_st = [0, 0]
    tp3_used = 0
    prev_batch = p.n("qs")

    for ch in range(NCHK):
        for o in range(NCH):
            ob = o % 2
            w = [("act", a1_act[ob]), a1_dve[ob]]
            if o == 0:
                w.append(("qs", prev_batch))
            p.em("sync", lambda e, _o=o, _ob=ob, _ch=ch: e.dma_start(
                a1[_ob][:, :], T["a_s"].ap()[_o][:, _ch * CH3:(_ch + 1) * CH3]),
                wait=w, inc=(("qs", 16),))
            ld = p.n("qs")
            for sub in range(2):
                bank = gu_g if sub == 0 else gu_u
                bank_war = ("act", gug_sil) if sub == 0 else ("dve", uu_tt)
                p.em("scalar", lambda e, _ob=ob, _sub=sub: e.activation(
                    sqa[_sub][:, :], a1[_ob][:, _sub * TCH:(_sub + 1) * TCH], AF.Square),
                    wait=(("qs", ld), ("pe", sqa_used[sub])), inc=(("act", 1),))
                sq_act = p.n("act")
                wmm = [("act", sq_act)]
                if o == 0:
                    wmm += [stat3_free, bank_war]
                p.em("tensor", lambda e, _o=o, _sub=sub, _bk=bank: e.matmul(
                    _bk[0:1, :], P["ones_b"][:, 0:1], sqa[_sub][:, :],
                    start=(_o == 0), stop=(_o == NCH - 1)),
                    wait=wmm, inc=(("pe", 1),))
                sqa_used[sub] = p.n("pe")
            a1_act[ob] = p.n("act")
        for sub in range(2):
            bank = gu_g if sub == 0 else gu_u
            p.em("scalar", lambda e, _bk=bank: e.activation(
                rms3[0:1, :], _bk[0:1, :], AF.Sqrt, bias=P["epsv"][0:1, 0:1],
                scale=1.0 / D), wait=(("pe", p.n("pe")), ("dve", p.n("dve"))),
                inc=(("act", 1),))
            p.em("vector", lambda e: e.reciprocal(rinv3[0:1, :], rms3[0:1, :]),
                 wait=(("act", p.n("act")),), inc=(("dve", 1),))
            p.em("tensor", lambda e: e.matmul(
                stat3[:, :], P["ones_f"][0:1, :], rinv3[0:1, :], start=True, stop=True),
                wait=(("dve", p.n("dve")), stat3_free), inc=(("pe", 1),))
            p.em("scalar", lambda e, _sub=sub: e.copy(
                rb2[:, _sub * TCH:(_sub + 1) * TCH], stat3[:, :]),
                wait=(("pe", p.n("pe")), rb2_used), inc=(("act", 1),))
            stat3_free = ("act", p.n("act"))
        rb2_act = p.n("act")
        gug_sil = rb2_act
        for o in range(NCH):
            ob = o % 2
            w = [("act", a1_act[ob]), a1_dve[ob]]
            if o == 0:
                w.append(("qs", p.n("qs")))
            p.em("sync", lambda e, _o=o, _ob=ob, _ch=ch: e.dma_start(
                a1[_ob][:, :], T["a_s"].ap()[_o][:, _ch * CH3:(_ch + 1) * CH3]),
                wait=w, inc=(("qs", 16),))
            ld = p.n("qs")
            p.em("vector", lambda e, _o=o, _ob=ob: e.scalar_tensor_tensor(
                z2[:, _o * CH3:(_o + 1) * CH3], a1[_ob][:, :],
                P["wln2"][:, _o:_o + 1], rb2[:, :], op0=ALU.mult, op1=ALU.mult),
                wait=(("qs", ld), ("act", rb2_act), ("pe", z2_used)),
                inc=(("dve", 1),))
            a1_dve[ob] = ("dve", p.n("dve"))
            rb2_used = ("dve", p.n("dve"))
        z2_dve = ("dve", p.n("dve"))
        prev_batch = p.n("qs")
        for tt in range(CH3 // 128):
            wlg = [z2_dve, ("qg", wgate_ld), ("act", lg_evac), stat3_free]
            for o in range(NCH):
                p.em("tensor", lambda e, _o=o, _tt=tt: e.matmul(
                    stat3[0:128, 0:E], z2[:, _o * CH3 + _tt * 128:_o * CH3 + (_tt + 1) * 128],
                    wgate_sb[:, _o * E:(_o + 1) * E],
                    start=(_o == 0), stop=(_o == NCH - 1)),
                    wait=wlg if o == 0 else (), inc=(("pe", 1),) if o == NCH - 1 else ())
            p.em("scalar", lambda e: e.copy(lg[:, :], stat3[0:128, 0:E]),
                 wait=(("pe", p.n("pe")), ("dve", gate_dve)), inc=(("act", 1),))
            lg_evac = p.n("act")
            gate_act = p.n("act")

            def gch(build):
                p.em("vector", build, wait=(("dve", p.n("dve")), ("act", gate_act)),
                     inc=(("dve", 1),))
            gch(lambda e: e.tensor_tensor(lg2[:, :], lg[:, :], P["ebb"][:, 0:E], op=ALU.add))
            gch(lambda e: e.tensor_reduce(mx[:, 0:1], lg2[:, :], axis=AX.X, op=ALU.max,
                                          negate=True))
            p.em("scalar", lambda e: e.activation(eg[:, :], lg2[:, :], AF.Exp,
                                                  bias=mx[:, 0:1]),
                 wait=(("dve", p.n("dve")),), inc=(("act", 1),))
            gate_act = p.n("act")
            gch(lambda e: e.max(srt[:, :], eg[:, :]))
            gch(lambda e: e.tensor_scalar(msk[:, :], eg[:, :], srt[:, 1:2], None,
                                          op0=ALU.is_ge))
            gch(lambda e: e.tensor_tensor(wun[:, :], eg[:, :], msk[:, :], op=ALU.mult))
            gch(lambda e: e.tensor_scalar(s12[:, 0:1], srt[:, 1:2], 1.0, None, op0=ALU.add))
            gch(lambda e: e.reciprocal(rs12[:, 0:1], s12[:, 0:1]))
            gch(lambda e: e.tensor_scalar(combt[:, :], wun[:, :], rs12[:, 0:1], None,
                                          op0=ALU.mult))
            gate_dve = p.n("dve")
            p.em("tensor", lambda e: e.transpose(
                tp3[0:E, 0:128], combt[:, :], P["idn_f"][:, :]),
                wait=(("dve", gate_dve), ("dve", tp3_used)), inc=(("pe", 1),))
            p.em("scalar", lambda e, _tt=tt: e.copy(
                comb_f[0:E, _tt * 128:(_tt + 1) * 128], tp3[0:E, 0:128]),
                wait=(("pe", p.n("pe")),), inc=(("act", 1),))
            tpg_evac = p.n("act")
        stat3_free = ("act", tpg_evac)
        uu_tt_p = uu_tt
        for e_i in range(NE):
            eb = e_i % 2
            wldw = [("pe", wld_g[eb]), ("qc", wst_g[eb])]
            if ch == 0:
                gsrc = T["wg"].ap()[e_i] if e_i < E else T["swg"].ap()
                usrc = T["wu"].ap()[e_i] if e_i < E else T["swu"].ap()
                p.em("gpsimd", lambda e, _b=eb, _s=gsrc: e.dma_start(
                    wg_sb[_b][:, :].rearrange("p (c d) -> p c d", d=II),
                    _s.rearrange("c p d -> p c d")),
                    wait=wldw, inc=(("qg", 16),))
                p.em("gpsimd", lambda e, _b=eb, _s=usrc: e.dma_start(
                    wu_sb[_b][:, :].rearrange("p (c d) -> p c d", d=II),
                    _s.rearrange("c p d -> p c d")), inc=(("qg", 16),))
                wld = p.n("qg")
                p.em("scalar", lambda e, _b=eb, _e=e_i: e.dma_start(
                    T["wgb_s"].ap()[_e], wg_sb[_b][:, :]),
                    wait=(("qg", wld),), inc=(("qc", 16),))
                p.em("scalar", lambda e, _b=eb, _e=e_i: e.dma_start(
                    T["wub_s"].ap()[_e], wu_sb[_b][:, :]), inc=(("qc", 16),))
                wst_g[eb] = p.n("qc")
            else:
                p.em("gpsimd", lambda e, _b=eb, _e=e_i: e.dma_start(
                    wg_sb[_b][:, :], T["wgb_s"].ap()[_e]),
                    wait=wldw, inc=(("qg", 16),))
                p.em("gpsimd", lambda e, _b=eb, _e=e_i: e.dma_start(
                    wu_sb[_b][:, :], T["wub_s"].ap()[_e]), inc=(("qg", 16),))
                wld = p.n("qg")
            if e_i == 0:
                p.em("gpsimd", lambda e: e.dma_start(T["cb_scr"].ap(), comb_f[0:E, :]),
                     wait=(("act", tpg_evac), ("qg", cb_out_prev)), inc=(("qg", 16),))
                cb_out = p.n("qg")
            if e_i < E:
                p.em("gpsimd", lambda e, _e=e_i: e.dma_start(
                    cbb[:, :], bass.AP(T["cb_scr"], _e * 1024, [[0, 128], [1, 1024]])),
                    wait=(("qg", cb_out), ("dve", cbb_rd)), inc=(("qg", 16),))
                cbb_in = p.n("qg")
                if e_i == E - 1:
                    cb_out_prev = cbb_in
            cbb_act = p.n("act")
            for sub in range(2):
                for ich in range(4):
                    gb = ich % 2
                    wg0 = [("qg", wld), z2_dve, ("act", gug_sil)]
                    for c in range(NCH):
                        p.em("tensor", lambda e, _c=c, _b=eb, _i=ich, _s=sub: e.matmul(
                            gu_g[:, :], wg_sb[_b][:, _c * II + _i * 128:_c * II + (_i + 1) * 128],
                            z2[:, _c * CH3 + _s * TCH:_c * CH3 + (_s + 1) * TCH],
                            start=(_c == 0), stop=(_c == NCH - 1)),
                            wait=wg0 if c == 0 else (),
                            inc=(("pe", 1),) if c == NCH - 1 else ())
                    pe_g = p.n("pe")
                    p.em("scalar", lambda e, _gb=gb: e.activation(
                        sg[_gb][:, :], gu_g[:, :], AF.Silu),
                        wait=(("pe", pe_g), ("dve", silu_rd[gb])), inc=(("act", 1),))
                    gug_sil = p.n("act")
                    sil_act = p.n("act")
                    wu0 = [("dve", uu_tt), ("act", rb2_act)]
                    for c in range(NCH):
                        p.em("tensor", lambda e, _c=c, _b=eb, _i=ich, _s=sub: e.matmul(
                            gu_u[:, :], wu_sb[_b][:, _c * II + _i * 128:_c * II + (_i + 1) * 128],
                            z2[:, _c * CH3 + _s * TCH:_c * CH3 + (_s + 1) * TCH],
                            start=(_c == 0), stop=(_c == NCH - 1)),
                            wait=wu0 if c == 0 else (),
                            inc=(("pe", 1),) if c == NCH - 1 else ())
                    pe_u = p.n("pe")
                    if e_i < E:
                        p.em("vector", lambda e, _gb=gb, _s=sub: e.tensor_tensor(
                            us[_gb][:, :], gu_u[:, :], cbb[:, _s * TCH:(_s + 1) * TCH],
                            op=ALU.mult),
                            wait=(("pe", pe_u), ("qg", cbb_in), ("dve", uu_rd[gb])),
                            inc=(("dve", 1),))
                    else:
                        p.em("vector", lambda e, _gb=gb: e.tensor_copy(
                            us[_gb][:, :], gu_u[:, :]),
                            wait=(("pe", pe_u), ("dve", uu_rd[gb])), inc=(("dve", 1),))
                    uu_tt = p.n("dve")
                    cbb_rd = p.n("dve")
                    p.em("vector", lambda e, _gb=gb, _e=e_i, _i=ich, _s=sub: e.tensor_tensor(
                        h_all[:, (_e * 4 + _i) * CH3 + _s * TCH:(_e * 4 + _i) * CH3 + (_s + 1) * TCH],
                        sg[_gb][:, :], us[_gb][:, :], op=ALU.mult),
                        wait=(("act", sil_act), ("dve", uu_tt), ("pe", h_used)),
                        inc=(("dve", 1),))
                    silu_rd[gb] = p.n("dve")
                    uu_rd[gb] = p.n("dve")
            wld_g[eb] = p.n("pe")
        z2_used = p.n("pe")
        h_dve = p.n("dve")
        for s4 in range(4):
            for e_i in range(NE):
                eb = e_i % 2
                wldw = [("pe", wldd[eb]), ("qc", wst_d[eb])]
                if ch == 0 and s4 == 0:
                    dsrc = T["wd"].ap()[e_i] if e_i < E else T["swd"].ap()
                    p.em("gpsimd", lambda e, _b=eb, _s=dsrc: e.dma_start(
                        wd_sb[_b][:, :].rearrange("p (c d) -> p c d", d=D),
                        _s.rearrange("c p d -> p c d")),
                        wait=wldw, inc=(("qg", 16),))
                    wld = p.n("qg")
                    p.em("scalar", lambda e, _b=eb, _e=e_i: e.dma_start(
                        T["wdb_s"].ap()[_e], wd_sb[_b][:, :]),
                        wait=(("qg", wld),), inc=(("qc", 16),))
                    wst_d[eb] = p.n("qc")
                else:
                    p.em("gpsimd", lambda e, _b=eb, _e=e_i: e.dma_start(
                        wd_sb[_b][:, :], T["wdb_s"].ap()[_e]),
                        wait=wldw, inc=(("qg", 16),))
                    wld = p.n("qg")
                wv0 = [("qg", wld), ("dve", h_dve)]
                if e_i == 0:
                    wv0.append(("dve", moe_evac))
                for o in range(NCH):
                    for ich in range(4):
                        st_flag = (e_i == 0 and ich == 0 and o % 2 == 0)
                        sp_flag = (e_i == NE - 1 and ich == 3 and o % 2 == 1)
                        p.em("tensor", lambda e, _o=o, _i=ich, _b=eb, _e=e_i, _s4=s4,
                             _st=st_flag, _sp=sp_flag: e.matmul(
                            moe_ps[:, _o * 256:(_o + 1) * 256],
                            wd_sb[_b][:, _i * D + _o * 128:_i * D + _o * 128 + 128],
                            h_all[:, (_e * 4 + _i) * CH3 + _s4 * 256:(_e * 4 + _i) * CH3 + (_s4 + 1) * 256],
                            start=_st, stop=_sp),
                            wait=wv0 if (o == 0 and ich == 0) else (),
                            inc=(("pe", 1),) if (o == NCH - 1 and ich == 3) else ())
                wldd[eb] = p.n("pe")
            pe_moe = p.n("pe")
            for jj in range(2):
                t_tile = ch * 8 + s4 * 2 + jj
                w = [("dve", x3_used[jj])]
                if jj == 0:
                    w.append(("qs", prev_batch))
                p.em("sync", lambda e, _t=t_tile, _j=jj: e.dma_start(
                    x3[_j][:, :], T["x"].ap()[_t * 128:(_t + 1) * 128, :]),
                    wait=w, inc=(("qs", 16),))
            for o in range(NCH):
                ob = o % 2
                p.em("sync", lambda e, _o=o, _ob=ob, _ch=ch, _s4=s4: e.dma_start(
                    a3[_ob][:, :],
                    T["a_s"].ap()[_o][:, _ch * CH3 + _s4 * 256:_ch * CH3 + (_s4 + 1) * 256]),
                    wait=(("dve", a3_used[ob]),), inc=(("qs", 16),))
                p.em("sync", lambda e, _o=o, _ob=ob, _ch=ch, _s4=s4: e.dma_start(
                    A3[_ob][:, :],
                    T["A_s"].ap()[_o][:, _ch * CH3 + _s4 * 256:_ch * CH3 + (_s4 + 1) * 256]),
                    wait=(("dve", A3_used[ob]),), inc=(("qs", 16),))
                ld3 = p.n("qs")
                p.em("vector", lambda e, _o=o, _ob=ob: e.tensor_tensor(
                    s1b[_ob][:, :], moe_ps[:, _o * 256:(_o + 1) * 256], a3[_ob][:, :],
                    op=ALU.add),
                    wait=(("qs", ld3), ("pe", pe_moe), ("dve", s1_used[ob])),
                    inc=(("dve", 1),))
                a3_used[ob] = p.n("dve")
                p.em("vector", lambda e, _ob=ob: e.tensor_tensor(
                    s2b[_ob][:, :], s1b[_ob][:, :], A3[_ob][:, :], op=ALU.add),
                    wait=(("dve", p.n("dve")), ("pe", s2_used[ob])), inc=(("dve", 1),))
                A3_used[ob] = p.n("dve")
                s1_used[ob] = p.n("dve")
                dv_s2 = p.n("dve")
                if o == NCH - 1:
                    moe_evac = dv_s2
                for jj in range(2):
                    p.em("tensor", lambda e, _ob=ob, _j=jj: e.transpose(
                        tp3[:, _j * 128:(_j + 1) * 128],
                        s2b[_ob][:, _j * 128:(_j + 1) * 128], P["idn_f"][:, :]),
                        wait=(("dve", dv_s2), ("dve", tp3_used), ("act", tpg_evac)),
                        inc=(("pe", 1),))
                    s2_used[ob] = p.n("pe")
                    pe_tp = p.n("pe")
                    p.em("vector", lambda e, _j=jj, _o=o: e.tensor_tensor(
                        out_sb[_j][:, _o * 128:(_o + 1) * 128],
                        tp3[:, _j * 128:(_j + 1) * 128],
                        x3[_j][:, _o * 128:(_o + 1) * 128], op=ALU.add),
                        wait=(("pe", pe_tp), ("qs", p.n("qs")), ("qc", out_st[jj])),
                        inc=(("dve", 1),))
                    tp3_used = p.n("dve")
                    x3_used[jj] = p.n("dve")
            for jj in range(2):
                t_tile = ch * 8 + s4 * 2 + jj
                p.em("scalar", lambda e, _t=t_tile, _j=jj: e.dma_start(
                    T["out"].ap()[_t * 128:(_t + 1) * 128, :], out_sb[_j][:, :]),
                    wait=(("dve", tp3_used),), inc=(("qc", 16),))
                out_st[jj] = p.n("qc")
            prev_batch = p.n("qs")
        h_used = p.n("pe")
    p.close()
    les.close()



def layout_inputs(ins):
    g = lambda n: np.asarray(ins[n])
    d = {}
    d["wq_a"] = g("L0_attn_Wq").reshape(NCH, 128, D)
    d["wk_a"] = g("L0_attn_Wk").reshape(NCH, 128, D)
    d["wv_a"] = g("L0_attn_Wv").reshape(NCH, 128, D)
    d["wo_a"] = g("L0_attn_Wo").reshape(H, HD, D)
    d["bq_a"] = g("L0_attn_bq").reshape(NCH, 128, 1)
    d["bk_a"] = g("L0_attn_bk").reshape(NCH, 128, 1)
    d["bv_a"] = g("L0_attn_bv").reshape(1, D)
    d["bo_a"] = g("L0_attn_bo").reshape(NCH, 128, 1)
    d["wln1"] = g("L0_ffn_w_ln1").reshape(NCH, 128, 1)
    d["wln2"] = g("L0_ffn_w_ln2").reshape(NCH, 128, 1)
    d["wq_m"] = g("L0_ffn_Wq").reshape(NCH, 128, D)
    d["wkva"] = g("L0_ffn_Wkva").reshape(NCH, 128, RLAT + RP)
    d["wkvb"] = g("L0_ffn_Wkvb")
    wo2 = g("L0_ffn_Wo").reshape(H, HD, D)[:, 0:32, :]
    wo_m = np.zeros((NCH, 96, D), np.float32)
    wo_m[:, 0:32] = wo2[0::2]
    wo_m[:, 64:96] = wo2[1::2]
    d["wo_m"] = wo_m
    d["wgate"] = g("L0_ffn_Wgate").reshape(NCH, 128, E)
    d["ebias"] = g("L0_ffn_expert_bias").reshape(1, E)
    d["wg"] = g("L0_ffn_Wg").reshape(E, NCH, 128, II)
    d["wu"] = g("L0_ffn_Wu").reshape(E, NCH, 128, II)
    d["wd"] = g("L0_ffn_Wd").reshape(E, 4, 128, D)
    d["swg"] = g("L0_ffn_sWg").reshape(NCH, 128, II)
    d["swu"] = g("L0_ffn_sWu").reshape(NCH, 128, II)
    d["swd"] = g("L0_ffn_sWd").reshape(4, 128, D)
    return d


def kernel(**inputs):
    nc = build()
    shared = layout_inputs(inputs)
    shared.update(host_constants())
    x_full = np.asarray(inputs["L0_attn_x"])
    in_maps = []
    for core in range(8):
        m = dict(shared)
        m["x"] = np.ascontiguousarray(x_full[core])
        in_maps.append(m)
    res = run_bass_kernel_spmd(nc, in_maps, core_ids=list(range(8)))
    return np.stack([r["out"] for r in res.results], axis=0)


# version 24 (final)
# speedup vs baseline: 1.2062x; 1.0071x over previous
import os
import numpy as np
import ml_dtypes
import concourse.bass as bass
import concourse.mybir as mybir
from concourse.bass_utils import run_bass_kernel_spmd
from concourse.alu_op_type import AluOpType as ALU
from contextlib import ExitStack

F32 = mybir.dt.float32
BF16 = mybir.dt.bfloat16
AF = mybir.ActivationFunctionType
AX = mybir.AxisListType
bfdt = ml_dtypes.bfloat16

S = 4096
D = 1024
NCH = 8
TCH = 512
NTC = S // TCH
NT = S // 128
H = 16
HD = 64
RLAT = 128
RP = 32
E = 8
II = 512
EPS = 1e-5
NEGBIG = -1e30

ENGS = ("sync", "tensor", "vector", "scalar", "gpsimd")


NSLOT = 4
QUEUES = ("qs", "qg", "qc")


class K:
    def __init__(self, nc, sems):
        self.nc = nc
        self.sems = sems
        self.cnt = {s: 0 for s in sems}
        self.qn = {q: 0 for q in QUEUES}


def _slot_count(n, j):
    return 0 if n <= j else (n - j + NSLOT - 1) // NSLOT


class Ph:

    def __init__(self, k):
        self.k = k
        self.st = {e: [] for e in ENGS}

    def em(self, eng, build, wait=(), inc=()):
        sems = self.k.sems
        exp_wait = []
        for s, v in wait:
            if s in QUEUES:
                n = v // 16
                for j in range(NSLOT):
                    c = _slot_count(n, j)
                    if c > 0:
                        exp_wait.append((f"{s}{j}", 16 * c))
            else:
                exp_wait.append((s, v))
        exp_inc = []
        for s, a in inc:
            if s in QUEUES:
                if os.environ.get("KSERIAL") and self.k.qn[s] > 0:
                    n = self.k.qn[s]
                    for j in range(NSLOT):
                        cc = _slot_count(n, j)
                        if cc > 0:
                            exp_wait.append((f"{s}{j}", 16 * cc))
                j = self.k.qn[s] % NSLOT
                self.k.qn[s] += 1
                exp_inc.append((f"{s}{j}", 16))
            else:
                self.k.cnt[s] += a
                exp_inc.append((s, a))
        w, ii = tuple(exp_wait), tuple(exp_inc)

        def fn(e):
            for s, v in w:
                if v > 0:
                    e.wait_ge(sems[s], v)
            r = build(e)
            for s, a in ii:
                r = r.then_inc(sems[s], a)

        self.st[eng].append(fn)

    def n(self, s):
        if s in QUEUES:
            return 16 * self.k.qn[s]
        return self.k.cnt[s]

    def close(self, drain_stores=True):
        if drain_stores:
            self.em("sync", lambda e: e.nop(),
                    wait=(("qc", self.n("qc")), ("qg", self.n("qg"))))
        with self.k.nc.Block() as block:
            for e in ENGS:
                lst = self.st[e]
                if lst:
                    getattr(block, e)(lambda eng, _l=tuple(lst): [f(eng) for f in _l])



def host_constants():
    c = {}
    pos = np.arange(S, dtype=np.float32)
    freq = np.arange(0, HD, 2, dtype=np.float32) / HD
    inv = (1.0 / 10000.0 ** freq).astype(np.float32)
    angrep = np.repeat(np.outer(pos, inv).astype(np.float32), 2, axis=-1)
    cos_o = np.cos(angrep).T.astype(np.float32)
    sin_o = np.sin(angrep).T.astype(np.float32)
    c["cos_o"] = np.concatenate([cos_o, cos_o], 0)
    c["sin_o"] = np.concatenate([sin_o, sin_o], 0)
    freqs2 = (1.0 / 100000.0 ** (np.arange(0, RP, 2, dtype=np.float32) / RP)).astype(np.float32)
    ang2rep = np.repeat(np.outer(pos, freqs2).astype(np.float32), 2, axis=-1)
    cos2 = np.cos(ang2rep).T.astype(np.float32)
    sin2 = np.sin(ang2rep).T.astype(np.float32)
    ones32 = np.ones((32, S), np.float32)
    zero32 = np.zeros((32, S), np.float32)
    c["cs2c"] = np.concatenate([ones32, cos2, ones32, cos2], 0)
    c["cs2s"] = np.concatenate([zero32, sin2, zero32, sin2], 0)
    r64 = np.zeros((HD, HD), np.float32)
    for i in range(0, HD, 2):
        r64[i, i + 1] = -1.0
        r64[i + 1, i] = 1.0
    Ro = np.zeros((128, 128), np.float32)
    Ro[0:64, 0:64] = r64
    Ro[64:128, 64:128] = r64
    c["Ro"] = np.ascontiguousarray(Ro.T).astype(bfdt)
    r32 = np.zeros((RP, RP), np.float32)
    for i in range(0, RP, 2):
        r32[i, i + 1] = -1.0
        r32[i + 1, i] = 1.0
    Rm = np.zeros((128, 128), np.float32)
    Rm[32:64, 32:64] = r32
    Rm[96:128, 96:128] = r32
    c["Rm"] = np.ascontiguousarray(Rm.T).astype(bfdt)
    c["idn_b"] = np.eye(128, dtype=np.float32).astype(bfdt)
    c["idn_f"] = np.eye(128, dtype=np.float32)
    c["ones_f"] = np.ones((128, 128), np.float32)
    c["ones_b"] = np.ones((128, 128), np.float32).astype(bfdt)
    u = np.zeros((128, 128), np.float32)
    for kk in range(128):
        u[kk, :kk] = 1.0
    c["U01"] = u.astype(bfdt)
    c["negI"] = (NEGBIG * np.eye(128, dtype=np.float32)).astype(bfdt)
    ind = np.zeros((128, 2), np.float32)
    ind[0:64, 0] = 1.0
    ind[64:128, 1] = 1.0
    c["indAB"] = ind.astype(bfdt)
    ind2 = np.zeros((96, 2), np.float32)
    ind2[0:64, 0] = 1.0
    ind2[32:96, 1] = 1.0
    c["indMk"] = ind2.astype(bfdt)
    oh = np.zeros((E, E * 128), np.float32)
    for e in range(E):
        oh[e, e * 128:(e + 1) * 128] = 1.0
    c["onehot"] = oh
    return c


CONST_SPECS = {
    "cos_o": ([128, S], F32), "sin_o": ([128, S], F32),
    "cs2c": ([128, S], F32), "cs2s": ([128, S], F32),
    "Ro": ([128, 128], BF16), "Rm": ([128, 128], BF16),
    "idn_b": ([128, 128], BF16), "idn_f": ([128, 128], F32),
    "ones_f": ([128, 128], F32), "ones_b": ([128, 128], BF16),
    "U01": ([128, 128], BF16), "negI": ([128, 128], BF16),
    "indAB": ([128, 2], BF16), "indMk": ([96, 2], BF16),
    "onehot": ([E, E * 128], F32),
}

INPUT_SPECS = {
    "x": ([S, D], F32),
    "wq_a": ([NCH, 128, D], F32), "wk_a": ([NCH, 128, D], F32),
    "wv_a": ([NCH, 128, D], F32), "wo_a": ([H, HD, D], F32),
    "bq_a": ([NCH, 128, 1], F32), "bk_a": ([NCH, 128, 1], F32),
    "bv_a": ([1, D], F32), "bo_a": ([NCH, 128, 1], F32),
    "wln1": ([NCH, 128, 1], F32), "wln2": ([NCH, 128, 1], F32),
    "wq_m": ([NCH, 128, D], F32), "wkva": ([NCH, 128, RLAT + RP], F32),
    "wkvb": ([RLAT, H * 32], F32), "wo_m": ([NCH, 96, D], F32),
    "wgate": ([NCH, 128, E], F32), "ebias": ([1, E], F32),
    "wg": ([E, NCH, 128, II], F32), "wu": ([E, NCH, 128, II], F32),
    "wd": ([E, 4, 128, D], F32),
    "swg": ([NCH, 128, II], F32), "swu": ([NCH, 128, II], F32),
    "swd": ([4, 128, D], F32),
}

SCRATCH_SPECS = {
    "xn_s": ([NCH, 128, S], BF16),
    "qo_s": ([NCH, 128, S], BF16), "ko_s": ([NCH, 128, S], BF16),
    "vo_s": ([S, H * 65], BF16), "A_s": ([NCH, 128, S], F32),
    "qm_s": ([NCH, 128, S], BF16), "km_s": ([NCH, 96, S], BF16),
    "kt_s": ([S, H * 33], BF16), "a_s": ([NCH, 128, S], F32),
    "wgb_s": ([E + 1, 128, NCH * II], BF16), "wub_s": ([E + 1, 128, NCH * II], BF16),
    "wdb_s": ([E + 1, 128, 4 * D], BF16),
    "rb1_scr": ([4, TCH], F32), "rb2_scr": ([4, TCH], F32),
    "cb_scr": ([8, 1024], F32),
}



def build(upto=99, dbg=()):
    nc = bass.Bass("TRN2", target_bir_lowering=False)
    es = ExitStack()

    T = {}
    for n, (shp, dt) in INPUT_SPECS.items():
        T[n] = nc.dram_tensor(n, shp, dt, kind="ExternalInput")
    for n, (shp, dt) in CONST_SPECS.items():
        T[n] = nc.dram_tensor(n, shp, dt, kind="ExternalInput")
    for n, (shp, dt) in SCRATCH_SPECS.items():
        kind = "ExternalOutput" if n in dbg else "Internal"
        T[n] = nc.dram_tensor(n, shp, dt, kind=kind)
    T["out"] = nc.dram_tensor("out", [S, D], F32, kind="ExternalOutput")
    if "nb_dump" in dbg:
        T["nb_dump"] = nc.dram_tensor("nb_dump", [128, 2 * NCH], F32, kind="ExternalOutput")

    sems = {}
    for s in ("pe", "act", "dve", "gp"):
        sems[s] = es.enter_context(nc.semaphore(s))
    for q in QUEUES:
        for j in range(NSLOT):
            sems[f"{q}{j}"] = es.enter_context(nc.semaphore(f"{q}{j}"))
    k = K(nc, sems)

    P = {}
    for n in ("Ro", "Rm", "idn_b", "idn_f", "ones_f", "ones_b", "U01", "negI",
              "indAB", "indMk"):
        shp, dt = CONST_SPECS[n]
        P[n] = es.enter_context(nc.sbuf_tensor(n + "_c", shp, dt))
    for n in ("bq_a", "bk_a", "bo_a", "wln1", "wln2"):
        P[n] = es.enter_context(nc.sbuf_tensor(n + "_v", [128, NCH], F32))
    P["bvb"] = es.enter_context(nc.sbuf_tensor("bvb", [128, D], F32))
    P["ebb"] = es.enter_context(nc.sbuf_tensor("ebb", [128, 4 * E], F32))
    P["negB1"] = es.enter_context(nc.sbuf_tensor("negB1", [128, NCH], F32))
    P["negB2"] = es.enter_context(nc.sbuf_tensor("negB2", [128, NCH], F32))
    P["nbrow"] = es.enter_context(nc.sbuf_tensor("nbrow", [1, 2 * NCH], F32))
    P["epsv"] = es.enter_context(nc.sbuf_tensor("epsv", [128, 1], F32))

    on = (lambda i: i in upto) if isinstance(upto, (set, frozenset, tuple, list)) \
        else (lambda i: upto >= i)
    ph_init(k, T, P)
    xn_es = ExitStack()
    xn_f = xn_es.enter_context(nc.sbuf_tensor("xn_f", [128, NCH * S], BF16))
    if on(1):
        ph0(k, T, P, xn_f)
    if on(2):
        ph1a(k, T, P, xn_f)
    xn_es.close()
    if on(3):
        ph1b(k, T, P)
    if on(4):
        ph2a(k, T, P)
    if on(5):
        ph2b(k, T, P)
    if on(6):
        ph3(k, T, P)
    if "nb_dump" in dbg:
        p = Ph(k)
        if True:
            p.em("sync", lambda e: e.dma_start(T["nb_dump"].ap()[:, 0:NCH], P["negB1"][:, :]),
                 inc=(("qs", 16),))
            p.em("sync", lambda e: e.dma_start(T["nb_dump"].ap()[:, NCH:2 * NCH], P["negB2"][:, :]),
                 inc=(("qs", 16),))
        p.em("sync", lambda e: e.nop(), wait=(("qs", p.n("qs")),))
        p.close()
    ph_fini(k)
    es.close()
    return nc


def ph_fini(k):
    p = Ph(k)
    p.em("scalar", lambda e: e.nop(),
         wait=(("qs", p.n("qs")), ("qg", p.n("qg")), ("qc", p.n("qc"))))
    p.close()



def ph_init(k, T, P):
    p = Ph(k)
    for n in ("Ro", "Rm", "idn_b", "idn_f", "ones_f", "ones_b", "U01", "negI",
              "indAB", "indMk"):
        shp = CONST_SPECS[n][0]
        p.em("sync", lambda e, _n=n, _s=shp: e.dma_start(
            P[_n][0:_s[0], 0:_s[1]], T[_n].ap()), inc=(("qs", 16),))
    for n in ("bq_a", "bk_a", "bo_a", "wln1", "wln2"):
        for c in range(NCH):
            p.em("sync", lambda e, _n=n, _c=c: e.dma_start(
                P[_n][:, _c:_c + 1], T[_n].ap()[_c]), inc=(("qs", 16),))
    p.em("vector", lambda e: e.memset(P["epsv"][:, 0:1], EPS), inc=(("dve", 1),))
    les = ExitStack()
    bvrow = les.enter_context(k.nc.sbuf_tensor("bvrow", [1, D], F32))
    ebrow = les.enter_context(k.nc.sbuf_tensor("ebrow", [1, 4 * E], F32))
    p.em("sync", lambda e: e.dma_start(bvrow[0:1, :], T["bv_a"].ap()), inc=(("qs", 16),))
    p.em("sync", lambda e: e.dma_start(ebrow[0:1, 0:E], T["ebias"].ap()), inc=(("qs", 16),))
    ld = p.n("qs")
    for j in range(1, 4):
        p.em("vector", lambda e, _j=j: e.tensor_copy(
            ebrow[0:1, _j * E:(_j + 1) * E], ebrow[0:1, 0:E]),
            wait=(("qs", ld), ("dve", p.n("dve"))), inc=(("dve", 1),))
    dv = p.n("dve")
    pb = les.enter_context(k.nc.psum_tensor("pb_init", [128, D], F32))
    pb2 = les.enter_context(k.nc.psum_tensor("pb2_init", [128, 4 * E], F32))
    for half in range(2):
        p.em("tensor", lambda e, _h=half: e.matmul(
            pb[:, _h * 512:(_h + 1) * 512], P["ones_f"][0:1, :],
            bvrow[0:1, _h * 512:(_h + 1) * 512], start=True, stop=True),
            wait=(("qs", ld),), inc=(("pe", 1),))
    p.em("tensor", lambda e: e.matmul(pb2[:, :], P["ones_f"][0:1, :], ebrow[0:1, :],
                                      start=True, stop=True),
         wait=(("dve", dv),), inc=(("pe", 1),))
    pe = p.n("pe")
    p.em("scalar", lambda e: e.copy(P["bvb"][:, 0:512], pb[:, 0:512]),
         wait=(("pe", pe),), inc=(("act", 1),))
    p.em("scalar", lambda e: e.copy(P["bvb"][:, 512:1024], pb[:, 512:1024]), inc=(("act", 1),))
    p.em("scalar", lambda e: e.copy(P["ebb"][:, :], pb2[:, :]), inc=(("act", 1),))
    p.close()
    les.close()



def ph0(k, T, P, xn_f):
    nc = k.nc
    p = Ph(k)
    les = ExitStack()
    sbl = lambda n, shp, dt: les.enter_context(nc.sbuf_tensor(n, shp, dt))
    psl = lambda n, shp, dt: les.enter_context(nc.psum_tensor(n, shp, dt))
    DP = 3
    xtok = [sbl(f"xtok{b}", [128, D], F32) for b in range(DP)]
    xntok = [sbl(f"xntok{b}", [128, D], BF16) for b in range(DP)]
    stats = [sbl(f"stats0{b}", [128, 12], F32) for b in range(2)]
    mv = [sbl(f"mv0{b}", [128, 2], F32) for b in range(2)]
    sqv = [sbl(f"sqv0{b}", [128, 1], F32) for b in range(2)]
    rinv = [sbl(f"rinv0{b}", [128, 1], F32) for b in range(2)]
    mneg = [sbl(f"mneg0{b}", [128, 1], F32) for b in range(2)]
    tp = [psl(f"tp0_{b}", [128, 128], BF16) for b in range(4)]

    xn_act = [0] * NT
    tr_done = [0] * NT
    ev_done = [0] * NT
    ag_done = [0, 0]
    sq_done = [0, 0]
    ri_used = [0, 0]
    tp_ev = [0] * 4

    for i in range(NT):
        b = i % DP
        sb_ = i % 2
        w = [("act", xn_act[i - DP])] if i >= DP else []
        p.em("sync", lambda e, _b=b, _i=i: e.dma_start(
            xtok[_b][:, :], T["x"].ap()[_i * 128:(_i + 1) * 128, :]),
            wait=w, inc=(("qs", 16),))
        ld = p.n("qs")
        wstats = [("qs", ld), ("dve", ag_done[sb_])]
        p.em("vector", lambda e, _b=b, _s=sb_: e.bn_stats(stats[_s][:, 0:6], xtok[_b][:, 0:512]),
             wait=wstats, inc=(("dve", 1),))
        p.em("vector", lambda e, _b=b, _s=sb_: e.bn_stats(stats[_s][:, 6:12], xtok[_b][:, 512:1024]),
             inc=(("dve", 1),))
        bs = p.n("dve")
        p.em("vector", lambda e, _s=sb_: e.bn_aggr(mv[_s][:, :], stats[_s][:, :]),
             wait=(("dve", bs), ("act", sq_done[sb_])), inc=(("dve", 1),))
        ag_done[sb_] = p.n("dve")
        p.em("scalar", lambda e, _s=sb_: e.activation(sqv[_s][:, 0:1], mv[_s][:, 1:2], AF.Sqrt,
                                                      bias=P["epsv"][:, 0:1]),
             wait=(("dve", ag_done[sb_]), ("dve", ri_used[sb_])), inc=(("act", 1),))
        sq_done[sb_] = p.n("act")
        p.em("vector", lambda e, _s=sb_: e.reciprocal(rinv[_s][:, 0:1], sqv[_s][:, 0:1]),
             wait=(("act", sq_done[sb_]), ("act", ri_used[sb_])), inc=(("dve", 1),))
        p.em("vector", lambda e, _s=sb_: e.tensor_scalar(mneg[_s][:, 0:1], mv[_s][:, 0:1],
                                                         rinv[_s][:, 0:1], -1.0,
                                                         op0=ALU.mult, op1=ALU.mult),
             wait=(("dve", p.n("dve")),), inc=(("dve", 1),))
        mn = p.n("dve")
        wxn = [("dve", mn)]
        if i >= DP:
            wxn.append(("pe", tr_done[i - DP]))
        p.em("scalar", lambda e, _b=b, _s=sb_: e.activation(
            xntok[_b][:, :], xtok[_b][:, :], AF.Identity,
            bias=mneg[_s][:, 0:1], scale=rinv[_s][:, 0:1]), wait=wxn, inc=(("act", 1),))
        xn_act[i] = p.n("act")
        ri_used[sb_] = xn_act[i]
        for c in range(NCH):
            tb = c % 4
            wtr = [("act", xn_act[i]), ("dve", tp_ev[tb])]
            p.em("tensor", lambda e, _b=b, _c=c, _tb=tb: e.transpose(
                tp[_tb][:, :], xntok[_b][:, _c * 128:(_c + 1) * 128], P["idn_b"][:, :]),
                wait=wtr, inc=(("pe", 1),))
            p.em("vector", lambda e, _c=c, _i=i, _tb=tb: e.tensor_copy(
                xn_f[:, _c * S + _i * 128: _c * S + (_i + 1) * 128], tp[_tb][:, :]),
                wait=(("pe", p.n("pe")),), inc=(("dve", 1),))
            tp_ev[tb] = p.n("dve")
        tr_done[i] = p.n("pe")
        ev_done[i] = p.n("dve")
        p.em("gpsimd", lambda e, _i=i: e.dma_start(
            T["xn_s"].ap()[:, :, _i * 128:(_i + 1) * 128].rearrange("c p t -> p c t"),
            xn_f[:, :].rearrange("p (c s) -> p c s", s=S)[:, :, _i * 128:(_i + 1) * 128]),
            wait=(("dve", ev_done[i]),), inc=(("qg", 16),))
    p.close()
    les.close()


def ph1a(k, T, P, xn_f):
    nc = k.nc
    p = Ph(k)
    les = ExitStack()
    sbl = lambda n, shp, dt: les.enter_context(nc.sbuf_tensor(n, shp, dt))
    psl = lambda n, shp, dt: les.enter_context(nc.psum_tensor(n, shp, dt))

    wsb = {}
    for n, dn in (("wq", "wq_a"), ("wk", "wk_a"), ("wv", "wv_a")):
        wsb[n] = sbl(n + "_sb", [128, NCH * D], BF16)
        p.em("gpsimd", lambda e, _n=n, _d=dn: e.dma_start(
            wsb[_n][:, :].rearrange("p (c d) -> p c d", d=D),
            T[_d].ap().rearrange("c p d -> p c d")), inc=(("qg", 16),))
    wld = p.n("qg")
    ctf = sbl("ctf", [128, S], F32)
    stf = sbl("stf", [128, S], F32)
    p.em("sync", lambda e: e.dma_start(ctf[:, :], T["cos_o"].ap()), inc=(("qs", 16),))
    p.em("sync", lambda e: e.dma_start(stf[:, :], T["sin_o"].ap()), inc=(("qs", 16),))
    tabld = p.n("qs")

    DP = 4
    qraw = [sbl(f"qraw{b}", [128, TCH], BF16) for b in range(DP)]
    qc_t = [sbl(f"qc_t{b}", [128, TCH], F32) for b in range(DP)]
    qr_t = [sbl(f"qr_t{b}", [128, TCH], F32) for b in range(DP)]
    qacc = [sbl(f"qacc{b}", [128, S], BF16) for b in range(2)]
    sq_t = sbl("sq_t", [128, S], BF16)
    vo_b = [sbl(f"vo_b{b}", [128, H * 65], BF16) for b in range(2)]
    nmax = sbl("nmax1w", [1, 2 * NCH * NTC], F32)
    nbred = sbl("nbred", [1, 2 * NCH], F32)
    nbsq = sbl("nbsq", [1, 2 * NCH], F32)

    pj = [psl(f"pj{b}", [128, TCH], F32) for b in range(3)]
    rot = [psl(f"rot{b}", [128, TCH], F32) for b in range(3)]
    vp = psl("vp0", [128, TCH], F32)
    nb_sh = psl("nb_sh", [128, TCH], F32)

    for b in range(2):
        p.em("vector", lambda e, _b=b: e.memset(
            vo_b[_b][:, :].rearrange("p (h c) -> p h c", c=65)[:, :, 64:65], 1.0),
            inc=(("dve", 1),))

    pj_evac = [0, 0, 0]
    raw_used_tt = [0] * DP
    rot_used = [0, 0, 0]
    qadd_gp = [("gp", 0)] * DP
    qacc_st = [0, 0]
    qacc_sq = [0, 0]
    v_tt = 0
    vo_stq = [0, 0]
    prev_red = 0
    prev_nrm = 0
    sq_nrm_rd = 0

    def flush_o(go, ab, dst, o):
        nonlocal prev_red, prev_nrm, sq_nrm_rd
        gp_all = p.n("gp")
        p.em("scalar", lambda e, _dst=dst, _o=o, _ab=ab: e.dma_start(
            T[_dst].ap()[_o], qacc[_ab][:, :]),
            wait=(("gp", gp_all),), inc=(("qc", 16),))
        qacc_st[ab] = p.n("qc")
        p.em("scalar", lambda e, _ab=ab: e.activation(
            sq_t[:, :], qacc[_ab][:, :], AF.Square),
            wait=(("gp", gp_all), ("pe", sq_nrm_rd)), inc=(("act", 1),))
        qacc_sq[ab] = p.n("act")
        sq_act = p.n("act")
        for tc in range(NTC):
            p.em("tensor", lambda e, _tc=tc: e.matmul(
                nb_sh[0:1, :], P["ones_b"][:, 0:1],
                sq_t[:, _tc * TCH:(_tc + 1) * TCH], start=True, stop=True),
                wait=(("act", sq_act), ("dve", prev_red)) if tc == 0 else
                     (("dve", prev_red),), inc=(("pe", 1),))
            prev_nrm = p.n("pe")
            col = go * NTC + tc
            p.em("vector", lambda e, _col=col: e.tensor_reduce(
                nmax[0:1, _col:_col + 1], nb_sh[0:1, :], axis=AX.X, op=ALU.max),
                wait=(("pe", prev_nrm),), inc=(("dve", 1),))
            prev_red = p.n("dve")
        sq_nrm_rd = prev_nrm

    def emit_pending(pd):
        ob, pb, tc, act_raw, ab = pd
        p.em("tensor", lambda e, _ob=ob, _pb=pb: e.matmul(
            rot[_pb][:, :], P["Ro"][:, :], qraw[_ob][:, :], start=True, stop=True),
            wait=(("act", act_raw), ("dve", rot_used[_pb if False else pb])),
            inc=(("pe", 1),))
        pe_rot = p.n("pe")
        p.em("vector", lambda e, _ob=ob, _tc=tc: e.tensor_tensor(
            qc_t[_ob][:, :], qraw[_ob][:, :],
            ctf[:, _tc * TCH:(_tc + 1) * TCH], op=ALU.mult),
            wait=(("qs", tabld), ("act", act_raw), qadd_gp[ob]),
            inc=(("dve", 1),))
        raw_used_tt[ob] = p.n("dve")
        p.em("vector", lambda e, _ob=ob, _pb=pb, _tc=tc: e.tensor_tensor(
            qr_t[_ob][:, :], rot[_pb][:, :],
            stf[:, _tc * TCH:(_tc + 1) * TCH], op=ALU.mult),
            wait=(("pe", pe_rot), qadd_gp[ob]), inc=(("dve", 1),))
        rot_used[pb] = p.n("dve")
        dv2 = p.n("dve")
        wadd = [("dve", dv2)]
        if tc == 0:
            wadd += [("qc", qacc_st[ab]), ("act", qacc_sq[ab])]
        p.em("gpsimd", lambda e, _ob=ob, _ab=ab, _tc=tc: e.tensor_tensor(
            qacc[_ab][:, _tc * TCH:(_tc + 1) * TCH],
            qc_t[_ob][:, :], qr_t[_ob][:, :], op=ALU.add),
            wait=wadd, inc=(("gp", 1),))
        qadd_gp[ob] = ("gp", p.n("gp"))

    pending = None
    pend_fin = None
    for wi, (wn, bn, dst) in enumerate((("wq", "bq_a", "qo_s"), ("wk", "bk_a", "ko_s"))):
        for o in range(NCH):
            go = wi * NCH + o
            ab = go % 2
            for tc in range(NTC):
                g = tc
                ob = g % DP
                pb = g % 3
                wproj = [("qg", wld), ("act", pj_evac[pb])]
                for c in range(NCH):
                    p.em("tensor", lambda e, _c=c, _o=o, _pb=pb, _wn=wn, _tc=tc: e.matmul(
                        pj[_pb][:, :], wsb[_wn][:, _c * D + _o * 128:_c * D + _o * 128 + 128],
                        xn_f[:, _c * S + _tc * TCH:_c * S + (_tc + 1) * TCH],
                        start=(_c == 0), stop=(_c == NCH - 1)),
                        wait=wproj if c == 0 else (),
                        inc=(("pe", 1),) if c == NCH - 1 else ())
                pe_pj = p.n("pe")
                if pending is not None:
                    emit_pending(pending)
                    if pend_fin is not None:
                        flush_o(*pend_fin)
                        pend_fin = None
                p.em("scalar", lambda e, _ob=ob, _pb=pb, _bn=bn, _o=o: e.activation(
                    qraw[_ob][:, :], pj[_pb][:, :], AF.Identity,
                    bias=P[_bn][:, _o:_o + 1]),
                    wait=(("pe", pe_pj), ("dve", raw_used_tt[ob])), inc=(("act", 1),))
                pj_evac[pb] = p.n("act")
                pending = (ob, pb, tc, p.n("act"), ab)
                if tc == NTC - 1:
                    pend_fin = (go, ab, dst, o)
    emit_pending(pending)
    flush_o(*pend_fin)
    pending = None
    pend_fin = None
    for t in range(NT):
        vb = t % 2
        for half in range(2):
            wv0 = [("qg", wld), ("dve", v_tt)]
            for c in range(NCH):
                p.em("tensor", lambda e, _c=c, _t=t, _h=half: e.matmul(
                    vp[:, :], xn_f[:, _c * S + _t * 128:_c * S + (_t + 1) * 128],
                    wsb["wv"][:, _c * D + _h * TCH:_c * D + (_h + 1) * TCH],
                    start=(_c == 0), stop=(_c == NCH - 1)),
                    wait=wv0 if c == 0 else (),
                    inc=(("pe", 1),) if c == NCH - 1 else ())
            pe_v = p.n("pe")
            p.em("vector", lambda e, _vb=vb, _h=half: e.tensor_tensor(
                vo_b[_vb][:, :].rearrange("p (h c) -> p h c", c=65)[:, _h * 8:(_h + 1) * 8, 0:64],
                vp[:, :].rearrange("p (h c) -> p h c", c=64),
                P["bvb"][:, _h * TCH:(_h + 1) * TCH].rearrange("p (h c) -> p h c", c=64),
                op=ALU.add),
                wait=(("pe", pe_v), ("qc", vo_stq[vb])), inc=(("dve", 1),))
            v_tt = p.n("dve")
        p.em("scalar", lambda e, _vb=vb, _t=t: e.dma_start(
            T["vo_s"].ap()[_t * 128:(_t + 1) * 128, :], vo_b[_vb][:, :]),
            wait=(("dve", v_tt),), inc=(("qc", 16),))
        vo_stq[vb] = p.n("qc")

    finalize_bounds(p, P, nmax, nbred, nbsq, nb_sh, "negB1", 0)
    p.close()
    les.close()


def finalize_bounds(p, P, nmax, nbred, nbsq, nbp, dstname, rowoff):
    for col in range(2 * NCH):
        p.em("vector", lambda e, _c=col: e.tensor_reduce(
            nbred[0:1, _c:_c + 1], nmax[0:1, _c * NTC:(_c + 1) * NTC], axis=AX.X, op=ALU.max),
            wait=(("dve", p.n("dve")),) if col == 0 else (), inc=(("dve", 1),))
    p.em("scalar", lambda e: e.activation(nbsq[0:1, :], nbred[0:1, :], AF.Sqrt),
         wait=(("dve", p.n("dve")),), inc=(("act", 1),))
    a0 = p.n("act")
    for o in range(NCH):
        p.em("vector", lambda e, _o=o, _ro=rowoff: e.scalar_tensor_tensor(
            P["nbrow"][0:1, _ro * NCH + _o:_ro * NCH + _o + 1],
            nbsq[0:1, _o:_o + 1], -0.125, nbsq[0:1, NCH + _o:NCH + _o + 1],
            op0=ALU.mult, op1=ALU.mult),
            wait=(("act", a0),), inc=(("dve", 1),))
    dvp = p.n("dve")
    p.em("tensor", lambda e, _ro=rowoff: e.matmul(
        nbp[0:128, 0:NCH], P["ones_f"][0:1, :], P["nbrow"][0:1, _ro * NCH:(_ro + 1) * NCH],
        start=True, stop=True),
        wait=(("dve", dvp),), inc=(("pe", 1),))
    p.em("scalar", lambda e: e.copy(P[dstname][:, :], nbp[0:128, 0:NCH]),
         wait=(("pe", p.n("pe")),), inc=(("act", 1),))


def ph1b(k, T, P):
    nc = k.nc
    p = Ph(k)
    les = ExitStack()
    sbl = lambda n, shp, dt: les.enter_context(nc.sbuf_tensor(n, shp, dt))
    psl = lambda n, shp, dt: les.enter_context(nc.psum_tensor(n, shp, dt))

    wo_sb = sbl("wo_sb", [64, H * D], BF16)
    p.em("gpsimd", lambda e: e.dma_start(
        wo_sb[:, :].rearrange("p (h d) -> p h d", d=D),
        T["wo_a"].ap().rearrange("h p d -> p h d")), inc=(("qg", 16),))
    wld = p.n("qg")

    ko_sb = sbl("ko_sb", [128, NCH * S], BF16)
    vo_sb = sbl("vo_sb", [128, NT * H * 65], BF16)
    qp = [sbl(f"qp{b}", [128, TCH], BF16) for b in range(2)]
    pt = [sbl(f"pt{b}", [128, 1024], BF16) for b in range(3)]
    rcb = sbl("rcb", [128, TCH], F32)
    yn_sb = sbl("yn_sb", [64, H * TCH], BF16)
    Ab = [sbl(f"Ab{b}", [128, TCH], F32) for b in range(2)]

    sAB = psl("sAB", [128, 2048], F32)
    yA = [psl(f"yA{b}", [128, TCH], F32) for b in range(2)]
    yB = [psl(f"yB{b}", [128, TCH], F32) for b in range(2)]

    qp_used = [0, 0]
    yn_used = 0
    exp_last = 0
    exp_kt = {}
    av_kt = {}
    rb_read = [0, 0]
    rb_slot = [0, 0, 0, 0]
    rc_out = 0
    wo_bank = [("act", 0)] * 4
    yn_tt = [0] * NCH
    ab_st = [0, 0]

    prev_ld = p.n("qs")
    for qc in range(NTC):
        nk = 4 * qc + 4
        for c in range(NCH):
            p.em("sync", lambda e, _c=c, _qc=qc: e.dma_start(
                ko_sb[:, _c * S + _qc * TCH:_c * S + (_qc + 1) * TCH],
                T["ko_s"].ap()[_c][:, _qc * TCH:(_qc + 1) * TCH]),
                wait=(("qs", prev_ld),) if c == 0 else (), inc=(("qs", 16),))
        p.em("sync", lambda e, _qc=qc: e.dma_start(
            vo_sb[:, _qc * 4 * H * 65:(_qc + 1) * 4 * H * 65].rearrange(
                "p (k c) -> p k c", c=H * 65),
            T["vo_s"].ap()[_qc * TCH:(_qc + 1) * TCH, :].rearrange(
                "(k p) c -> p k c", p=128)), inc=(("qs", 16),))
        ld = p.n("qs")
        prev_ld = ld

        for c in range(NCH):
            hA, hB = 2 * c, 2 * c + 1
            cb_ = c % 2
            p.em("sync", lambda e, _c=c, _qc=qc, _b=c % 2: e.dma_start(
                qp[_b][:, :], T["qo_s"].ap()[_c][:, _qc * TCH:(_qc + 1) * TCH]),
                wait=(("pe", qp_used[c % 2]),), inc=(("qs", 16),))
            ld_q = p.n("qs")
            nkt = 4 * qc + 4

            def _scores(kt):
                r = kt % 2
                rp = kt % 3
                edge = kt >= 4 * qc
                jj = kt - 4 * qc
                qoff = 128 * jj if edge else 0
                N = 512 - qoff
                wsc = [("act", exp_kt.get(kt - 2, exp_last))]
                if kt == 0:
                    wsc += [("qs", ld), ("qs", ld_q), ("qg", wld)]
                for hh, (rlo, sbase) in enumerate(((0, 0), (64, 512))):
                    p.em("tensor", lambda e, _c=c, _kt=kt, _rlo=rlo, _r=r, _b=cb_,
                         _sb=sbase, _n=N, _qo=qoff, _e=edge: e.matmul(
                        sAB[:, _r * 1024 + _sb:_r * 1024 + _sb + _n],
                        ko_sb[_rlo:_rlo + 64, _c * S + _kt * 128:_c * S + (_kt + 1) * 128],
                        qp[_b][_rlo:_rlo + 64, _qo:TCH],
                        start=True, stop=not _e,
                        tile_position=(_rlo, 0)),
                        wait=wsc if hh == 0 else (),
                        inc=(("pe", 1),) if (hh == 1 and not edge) else ())
                    if edge:
                        p.em("tensor", lambda e, _r=r, _sb=sbase: e.matmul(
                            sAB[:, _r * 1024 + _sb:_r * 1024 + _sb + 128],
                            P["negI"][:, :], P["U01"][:, :],
                            start=False, stop=True),
                            inc=(("pe", 1),) if hh == 1 else ())
                pe_sc = p.n("pe")
                wexp = [("pe", pe_sc)]
                if kt >= 3:
                    wexp.append(("pe", av_kt[kt - 3]))
                p.em("scalar", lambda e, _r=r, _rp=rp, _X=N, _c=c: e.activation(
                    pt[_rp][:, :].rearrange("p (g x) -> p g x", g=2)[:, :, 0:_X],
                    sAB[:, _r * 1024:(_r + 1) * 1024].rearrange(
                        "p (g x) -> p g x", g=2)[:, :, 0:_X],
                    AF.Exp, bias=P["negB1"][:, _c:_c + 1], scale=0.125),
                    wait=wexp, inc=(("act", 1),))
                exp_kt[kt] = p.n("act")

            def _av(kt):
                rp = kt % 3
                edge = kt >= 4 * qc
                jj = kt - 4 * qc
                qoff = 128 * jj if edge else 0
                N = 512 - qoff
                wav = [("act", exp_kt[kt])]
                if kt == 0:
                    wav += [("dve", rb_read[cb_])]
                for hh, (yps, ybase, ytp, sbase, h) in enumerate(((yA[cb_], 0, (0, 0), 0, hA), (yB[cb_], 0, (0, 0), 512, hB))):
                    p.em("tensor", lambda e, _y=yps, _rp=rp, _sb=sbase,
                         _n=N, _qo=qoff, _kt=kt, _h=h, _yb=ybase, _yt=ytp,
                         _st=(kt == 0), _sp=(kt == nkt - 1): e.matmul(
                        _y[_yb:_yb + 65, _qo:_qo + _n],
                        vo_sb[:, _kt * H * 65 + _h * 65:_kt * H * 65 + (_h + 1) * 65],
                        pt[_rp][:, _sb:_sb + _n],
                        start=_st, stop=_sp, tile_position=_yt),
                        wait=wav if (hh == 0 or kt == 0) else (),
                        inc=(("pe", 1),) if hh == 1 else ())
                av_kt[kt] = p.n("pe")

            for kt in range(nkt):
                _scores(kt)
                if kt >= 2:
                    _av(kt - 2)
            if nkt >= 2:
                _av(nkt - 2)
            _av(nkt - 1)
            exp_last = exp_kt[nkt - 1]
            qp_used[cb_] = p.n("pe")
            pe_av_all = p.n("pe")
            for hh2, (yps, h) in enumerate(((yA[cb_], hA), (yB[cb_], hB))):
                slot = cb_ * 2 + hh2
                p.em("vector", lambda e, _y=yps: e.reciprocal(
                    rcb[64:65, :], _y[64:65, :]),
                    wait=(("pe", pe_av_all), ("qg", rc_out)), inc=(("dve", 1),))
                dv_rc = p.n("dve")
                p.em("gpsimd", lambda e, _s=slot: e.dma_start(
                    T["rb1_scr"].ap()[_s:_s + 1, :], rcb[64:65, :]),
                    wait=(("dve", dv_rc), ("qg", rb_slot[slot])), inc=(("qg", 16),))
                rc_out = p.n("qg")
                p.em("gpsimd", lambda e, _s=slot: e.dma_start(
                    rcb[0:64, :], bass.AP(T["rb1_scr"], _s * TCH, [[0, 64], [1, TCH]])),
                    wait=(("dve", rb_read[cb_]), ("qg", rc_out)), inc=(("qg", 16),))
                rb_in = p.n("qg")
                rb_slot[slot] = rb_in
                p.em("vector", lambda e, _y=yps, _h=h: e.tensor_tensor(
                    yn_sb[0:64, _h * TCH:(_h + 1) * TCH], _y[0:64, :], rcb[0:64, :],
                    op=ALU.mult),
                    wait=(("qg", rb_in), ("pe", yn_used)), inc=(("dve", 1),))
                rb_read[cb_] = p.n("dve")
                if hh2 == 1:
                    yn_tt[c] = p.n("dve")
        for o in range(NCH):
            r4 = o % 4
            ob = o % 2
            wwo = [wo_bank[r4], ("act", exp_last), ("dve", yn_tt[5])]
            for h in range(H):
                wh = wwo if h == 0 else \
                    ([("dve", yn_tt[h // 2])] if h // 2 >= 6 else ())
                p.em("tensor", lambda e, _o=o, _h=h, _r4=r4: e.matmul(
                    sAB[:, _r4 * 512:(_r4 + 1) * 512],
                    wo_sb[0:64, _h * D + _o * 128:_h * D + _o * 128 + 128],
                    yn_sb[0:64, _h * TCH:(_h + 1) * TCH],
                    start=(_h == 0), stop=(_h == H - 1)),
                    wait=wh, inc=(("pe", 1),) if h == H - 1 else ())
            pe_wo = p.n("pe")
            p.em("scalar", lambda e, _o=o, _ob=ob, _r4=r4: e.activation(
                Ab[_ob][:, :], sAB[:, _r4 * 512:(_r4 + 1) * 512], AF.Identity,
                bias=P["bo_a"][:, _o:_o + 1]),
                wait=(("pe", pe_wo), ("qc", ab_st[ob])), inc=(("act", 1),))
            wo_bank[r4] = ("act", p.n("act"))
            p.em("scalar", lambda e, _o=o, _ob=ob, _qc=qc: e.dma_start(
                T["A_s"].ap()[_o][:, _qc * TCH:(_qc + 1) * TCH], Ab[_ob][:, :]),
                wait=(("act", p.n("act")),), inc=(("qc", 16),))
            ab_st[ob] = p.n("qc")
        exp_last = p.n("act")
        yn_used = p.n("pe")
    p.close()
    les.close()


def ph2a(k, T, P):
    nc = k.nc
    p = Ph(k)
    les = ExitStack()
    sbl = lambda n, shp, dt: les.enter_context(nc.sbuf_tensor(n, shp, dt))
    psl = lambda n, shp, dt: les.enter_context(nc.psum_tensor(n, shp, dt))

    wq2_sb = sbl("wq2_sb", [128, NCH * D], BF16)
    wkva_sb = sbl("wkva_sb", [128, NCH * 160], BF16)
    wkvb_sb = sbl("wkvb_sb", [128, 512], BF16)
    p.em("gpsimd", lambda e: e.dma_start(
        wq2_sb[:, :].rearrange("p (c d) -> p c d", d=D),
        T["wq_m"].ap().rearrange("c p d -> p c d")), inc=(("qg", 16),))
    p.em("gpsimd", lambda e: e.dma_start(
        wkva_sb[:, :].rearrange("p (c d) -> p c d", d=160),
        T["wkva"].ap().rearrange("c p d -> p c d")), inc=(("qg", 16),))
    p.em("gpsimd", lambda e: e.dma_start(wkvb_sb[:, :], T["wkvb"].ap()), inc=(("qg", 16),))
    wld = p.n("qg")
    ct2f = sbl("ct2f", [128, S], F32)
    st2f = sbl("st2f", [128, S], F32)
    p.em("sync", lambda e: e.dma_start(ct2f[:, :], T["cs2c"].ap()), inc=(("qs", 16),))
    p.em("sync", lambda e: e.dma_start(st2f[:, :], T["cs2s"].ap()), inc=(("qs", 16),))
    tabld = p.n("qs")

    z1f = sbl("z1f", [128, NCH * S], BF16)
    xnc = [sbl(f"xnc{b}", [128, NCH * TCH], BF16) for b in range(2)]
    sqz = [sbl(f"sqz{b}", [128, TCH], BF16) for b in range(2)]
    rms1 = sbl("rms1", [1, TCH], F32)
    rinv1 = sbl("rinv1", [1, TCH], F32)
    rb1 = [sbl(f"rb1_{b}", [128, TCH], F32) for b in range(2)]
    DP = 4
    qraw = [sbl(f"qraw2{b}", [128, TCH], BF16) for b in range(DP)]
    qc_t = [sbl(f"qc_t2{b}", [128, TCH], F32) for b in range(DP)]
    qr_t = [sbl(f"qr_t2{b}", [128, TCH], F32) for b in range(DP)]
    qacc = [sbl(f"qacc2{b}", [128, S], BF16) for b in range(2)]
    sq_t = [sbl(f"sq_t2{b}", [128, TCH], BF16) for b in range(2)]
    kvl = [sbl(f"kvl{b}", [128, TCH], BF16) for b in range(2)]
    krr = sbl("krr", [128, TCH], BF16)
    kmb = [sbl(f"kmb{b}", [128, TCH], BF16) for b in range(2)]
    ktb = [sbl(f"ktb{b}", [128, H * 33], BF16) for b in range(2)]
    nmax = sbl("nmax2w", [1, 2 * NCH * NTC], F32)
    nbred = sbl("nbred2", [1, 2 * NCH], F32)
    nbsq = sbl("nbsq2", [1, 2 * NCH], F32)

    pj = [psl(f"pj2{b}", [128, TCH], F32) for b in range(3)]
    rot = [psl(f"rot2{b}", [128, TCH], F32) for b in range(3)]
    statb = psl("stat2", [128, TCH], F32)
    ktp = psl("ktp", [128, TCH], F32)

    for b in range(2):
        p.em("vector", lambda e, _b=b: e.memset(
            ktb[_b][:, :].rearrange("p (h c) -> p h c", c=33)[:, :, 32:33], 1.0),
            inc=(("dve", 1),))

    z1_done = [("dve", 0)] * NTC
    ldc = [0] * NCH
    statmm = [0, 0]
    sqrt_act = 0
    recip_dve = 0
    scrout = [0, 0]
    bcin = [0, 0]
    for tc in range(NTC):
        b = tc % 2
        for c in range(NCH):
            wx = []
            if c == 0 and tc >= 2:
                wx = [z1_done[tc - 2]]
            p.em("sync", lambda e, _c=c, _tc=tc, _b=b: e.dma_start(
                xnc[_b][:, _c * TCH:(_c + 1) * TCH],
                T["xn_s"].ap()[_c][:, _tc * TCH:(_tc + 1) * TCH]),
                wait=wx, inc=(("qs", 16),))
            ldc[c] = p.n("qs")
        for c in range(NCH):
            cb = c % 2
            p.em("scalar", lambda e, _c=c, _cb=cb, _b=b: e.activation(
                sqz[_cb][:, :], xnc[_b][:, _c * TCH:(_c + 1) * TCH], AF.Square),
                wait=(("qs", ldc[_c if False else c]), ("pe", statmm[cb])),
                inc=(("act", 1),))
            sq_a = p.n("act")
            wmm = [("act", sq_a)]
            if c == 0:
                wmm.append(("act", sqrt_act))
            p.em("tensor", lambda e, _c=c, _cb=cb: e.matmul(
                statb[0:1, :], P["ones_b"][:, 0:1], sqz[_cb][:, :],
                start=(_c == 0), stop=(_c == NCH - 1)),
                wait=wmm, inc=(("pe", 1),))
            statmm[cb] = p.n("pe")
        p.em("scalar", lambda e: e.activation(rms1[0:1, :], statb[0:1, :], AF.Sqrt,
                                              bias=P["epsv"][0:1, 0:1], scale=1.0 / D),
             wait=(("pe", statmm[(NCH - 1) % 2]), ("dve", recip_dve)), inc=(("act", 1),))
        sqrt_act = p.n("act")
        p.em("vector", lambda e: e.reciprocal(rinv1[0:1, :], rms1[0:1, :]),
             wait=[("act", sqrt_act)] + ([("qg", scrout[(tc - 1) % 2])] if tc >= 1 else []),
             inc=(("dve", 1),))
        recip_dve = p.n("dve")
        p.em("gpsimd", lambda e, _s=tc % 2: e.dma_start(
            T["rb1_scr"].ap()[_s:_s + 1, :], rinv1[0:1, :]),
            wait=[("dve", recip_dve)] + ([("qg", bcin[tc % 2])] if tc >= 2 else []),
            inc=(("qg", 16),))
        scrout[tc % 2] = p.n("qg")
        wbc = [("qg", scrout[tc % 2])]
        if tc >= 2:
            wbc.append(z1_done[tc - 2])
        p.em("gpsimd", lambda e, _b=b, _s=tc % 2: e.dma_start(
            rb1[_b][:, :], bass.AP(T["rb1_scr"], _s * TCH, [[0, 128], [1, TCH]])),
            wait=wbc, inc=(("qg", 16),))
        bcin[tc % 2] = p.n("qg")
        for c in range(NCH):
            wz = [("qg", bcin[tc % 2])] if c == 0 else []
            p.em("vector", lambda e, _c=c, _b=b, _tc=tc: e.scalar_tensor_tensor(
                z1f[:, _c * S + _tc * TCH:_c * S + (_tc + 1) * TCH],
                xnc[_b][:, _c * TCH:(_c + 1) * TCH],
                P["wln1"][:, _c:_c + 1], rb1[_b][:, :], op0=ALU.mult, op1=ALU.mult),
                wait=wz, inc=(("dve", 1),))
        z1_done[tc] = ("dve", p.n("dve"))

    pj_evac = [0, 0, 0]
    raw_used_tt = [0] * DP
    rot_used = [0, 0, 0]
    qadd_gp = [("gp", 0)] * DP
    qacc_st = [0, 0]
    qacc_sq = [0, 0]
    prev_red = 0
    nrm_pe = [0, 0]

    def emit_pending_q(pd):
        nonlocal prev_red
        ob, rb, tc, act_raw, ab, o = pd
        p.em("tensor", lambda e, _ob=ob, _rb=rb: e.matmul(
            rot[_rb][:, :], P["Rm"][:, :], qraw[_ob][:, :], start=True, stop=True),
            wait=(("act", act_raw), ("dve", rot_used[rb])), inc=(("pe", 1),))
        pe_rot = p.n("pe")
        wtt = [("act", act_raw), qadd_gp[ob]]
        if o == 0 and tc == 0:
            wtt.append(("qs", tabld))
        p.em("vector", lambda e, _ob=ob, _tc=tc: e.tensor_tensor(
            qc_t[_ob][:, :], qraw[_ob][:, :],
            ct2f[:, _tc * TCH:(_tc + 1) * TCH], op=ALU.mult),
            wait=wtt, inc=(("dve", 1),))
        raw_used_tt[ob] = p.n("dve")
        p.em("vector", lambda e, _ob=ob, _rb=rb, _tc=tc: e.tensor_tensor(
            qr_t[_ob][:, :], rot[_rb][:, :],
            st2f[:, _tc * TCH:(_tc + 1) * TCH], op=ALU.mult),
            wait=(("pe", pe_rot), qadd_gp[ob]), inc=(("dve", 1),))
        rot_used[rb] = p.n("dve")
        dv2 = p.n("dve")
        wadd = [("dve", dv2)]
        if tc == 0:
            wadd += [("qc", qacc_st[ab]), ("act", qacc_sq[ab])]
        p.em("gpsimd", lambda e, _ob=ob, _ab=ab, _tc=tc: e.tensor_tensor(
            qacc[_ab][:, _tc * TCH:(_tc + 1) * TCH],
            qc_t[_ob][:, :], qr_t[_ob][:, :], op=ALU.add),
            wait=wadd, inc=(("gp", 1),))
        qadd_gp[ob] = ("gp", p.n("gp"))
        gp_add = p.n("gp")
        sb = tc % 2
        p.em("scalar", lambda e, _ab=ab, _tc=tc, _sb=sb: e.activation(
            sq_t[_sb][:, :], qacc[_ab][:, _tc * TCH:(_tc + 1) * TCH], AF.Square),
            wait=(("gp", gp_add), ("pe", nrm_pe[sb])), inc=(("act", 1),))
        qacc_sq[ab] = p.n("act")
        p.em("tensor", lambda e, _sb=sb: e.matmul(
            ktp[0:1, :], P["ones_b"][:, 0:1], sq_t[_sb][:, :],
            start=True, stop=True),
            wait=(("act", p.n("act")), ("dve", prev_red)), inc=(("pe", 1),))
        nrm_pe[sb] = p.n("pe")
        col = o * NTC + tc
        p.em("vector", lambda e, _col=col: e.tensor_reduce(
            nmax[0:1, _col:_col + 1], ktp[0:1, :], axis=AX.X, op=ALU.max),
            wait=(("pe", nrm_pe[sb]),), inc=(("dve", 1),))
        prev_red = p.n("dve")

    def flush_q(o, ab):
        gp_all = p.n("gp")
        p.em("scalar", lambda e, _o=o, _ab=ab: e.dma_start(
            T["qm_s"].ap()[_o], qacc[_ab][:, :]),
            wait=(("gp", gp_all), ("act", qacc_sq[ab])), inc=(("qc", 16),))
        qacc_st[ab] = p.n("qc")

    pend_q = None
    pend_fq = None
    for o in range(NCH):
        ab = o % 2
        for tc in range(NTC):
            g = o * NTC + tc
            ob = g % DP
            pb = g % 3
            wproj = [("qg", wld), ("act", pj_evac[pb])]
            if o == 0:
                wproj.append(z1_done[tc])
            for c in range(NCH):
                p.em("tensor", lambda e, _c=c, _o=o, _pb=pb, _tc=tc: e.matmul(
                    pj[_pb][:, :], wq2_sb[:, _c * D + _o * 128:_c * D + _o * 128 + 128],
                    z1f[:, _c * S + _tc * TCH:_c * S + (_tc + 1) * TCH],
                    start=(_c == 0), stop=(_c == NCH - 1)),
                    wait=wproj if c == 0 else (),
                    inc=(("pe", 1),) if c == NCH - 1 else ())
            pe_pj = p.n("pe")
            if pend_q is not None:
                emit_pending_q(pend_q)
                if pend_fq is not None:
                    flush_q(*pend_fq)
                    pend_fq = None
            p.em("scalar", lambda e, _ob=ob, _pb=pb: e.copy(
                qraw[_ob][:, :], pj[_pb][:, :]),
                wait=(("pe", pe_pj), ("dve", raw_used_tt[ob])), inc=(("act", 1),))
            pj_evac[pb] = p.n("act")
            pend_q = (ob, g % 3, tc, p.n("act"), ab, o)
            if tc == NTC - 1:
                pend_fq = (o, ab)
    emit_pending_q(pend_q)
    flush_q(*pend_fq)
    q_red_last = prev_red
    q_nrm_last = max(nrm_pe)

    kvl_act = 0
    krr_tt = 0
    kvp_bank = [0, 0]
    kmb_st = [0, 0]
    kmb_sq = [0, 0]
    kvp_ev = 0
    ktb_st = [0, 0]
    ktp_copy = 0
    first_ktok = True
    first_sq = True
    for tc in range(NTC):
        kb = tc % 2
        pb = tc % 3
        for c in range(NCH):
            p.em("tensor", lambda e, _c=c, _pb=pb, _tc=tc: e.matmul(
                pj[_pb][:, :], wkva_sb[:, _c * 160:_c * 160 + 128],
                z1f[:, _c * S + _tc * TCH:_c * S + (_tc + 1) * TCH],
                start=(_c == 0), stop=(_c == NCH - 1)),
                wait=(("act", pj_evac[pb]),) if c == 0 else (),
                inc=(("pe", 1),) if c == NCH - 1 else ())
        pe_kvl = p.n("pe")
        p.em("scalar", lambda e, _kb=kb, _pb=pb: e.copy(kvl[_kb][:, :], pj[_pb][:, :]),
             wait=(("pe", pe_kvl),), inc=(("act", 1),))
        pj_evac[pb] = p.n("act")
        kvl_act = p.n("act")
        for c in range(NCH):
            p.em("tensor", lambda e, _c=c, _tc=tc: e.matmul(
                rot[0][32:64, :], wkva_sb[:, _c * 160 + 128:_c * 160 + 160],
                z1f[:, _c * S + _tc * TCH:_c * S + (_tc + 1) * TCH],
                start=(_c == 0), stop=(_c == NCH - 1), tile_position=(0, 32)),
                wait=(("dve", rot_used[0]),) if c == 0 else (),
                inc=(("pe", 1),) if c == NCH - 1 else ())
        pe_kr = p.n("pe")
        p.em("scalar", lambda e: e.copy(krr[32:64, :], rot[0][32:64, :]),
             wait=(("pe", pe_kr), ("dve", krr_tt)), inc=(("act", 1),))
        act_krr = p.n("act")
        p.em("tensor", lambda e: e.matmul(
            rot[1][32:64, :], P["Rm"][32:64, 32:64], krr[32:64, :],
            start=True, stop=True, tile_position=(32, 32)),
            wait=(("act", act_krr), ("dve", rot_used[1])), inc=(("pe", 1),))
        pe_krot = p.n("pe")
        p.em("vector", lambda e, _tc=tc: e.tensor_tensor(
            qc_t[0][32:64, :], krr[32:64, :],
            ct2f[32:64, _tc * TCH:(_tc + 1) * TCH], op=ALU.mult),
            wait=(("act", act_krr), qadd_gp[0]), inc=(("dve", 1),))
        krr_tt = p.n("dve")
        p.em("vector", lambda e, _tc=tc: e.tensor_tensor(
            qr_t[0][32:64, :], rot[1][32:64, :],
            st2f[32:64, _tc * TCH:(_tc + 1) * TCH], op=ALU.mult),
            wait=(("pe", pe_krot), qadd_gp[0]), inc=(("dve", 1),))
        rot_used[0] = p.n("dve")
        rot_used[1] = p.n("dve")
        dvkr = p.n("dve")
        p.em("gpsimd", lambda e: e.tensor_tensor(
            kmb[0][32:64, :], qc_t[0][32:64, :], qr_t[0][32:64, :], op=ALU.add),
            wait=(("dve", dvkr), ("qs", kmb_st[0]), ("act", kvp_ev)), inc=(("gp", 1),))
        qadd_gp[0] = ("gp", p.n("gp"))
        p.em("gpsimd", lambda e: e.tensor_copy(kmb[1][32:64, :], kmb[0][32:64, :]),
             wait=(("gp", p.n("gp")), ("qs", kmb_st[1])), inc=(("gp", 1),))
        gp_rope = p.n("gp")
        pjB = pj[(tc + 1) % 3]
        for c in range(NCH):
            cb = c % 2
            bank = rot[2] if cb == 0 else pjB
            wkv = [("act", kvp_bank[cb])]
            if c == 0:
                wkv += [("act", kvl_act), ("dve", rot_used[2])]
            if c == 1:
                wkv.append(("act", pj_evac[(tc + 1) % 3]))
            p.em("tensor", lambda e, _c=c, _kb=kb, _bk=bank: e.matmul(
                _bk[0:64, :], wkvb_sb[:, _c * 64:(_c + 1) * 64], kvl[_kb][:, :],
                start=True, stop=True),
                wait=wkv, inc=(("pe", 1),))
            pe_kv = p.n("pe")
            rot_used[2] = 0
            wcp = [("pe", pe_kv), ("qs", kmb_st[cb]), ("gp", kmb_sq[cb])]
            if c < 2:
                wcp.append(("gp", gp_rope))
            p.em("scalar", lambda e, _cb=cb, _bk=bank: e.copy(
                kmb[_cb][0:32, :], _bk[0:32, :]),
                 wait=wcp, inc=(("act", 1),))
            p.em("scalar", lambda e, _cb=cb, _bk=bank: e.copy(
                kmb[_cb][64:96, :], _bk[32:64, :]),
                 inc=(("act", 1),))
            kvp_ev = p.n("act")
            kvp_bank[cb] = kvp_ev
            wsq = [("act", kvp_ev)]
            if first_sq:
                wsq.append(("pe", q_nrm_last))
                first_sq = False
            p.em("gpsimd", lambda e, _cb=cb: e.tensor_tensor(
                sq_t[_cb][0:96, :], kmb[_cb][0:96, :], kmb[_cb][0:96, :], op=ALU.mult),
                wait=wsq, inc=(("gp", 1),))
            kmb_sq[cb] = p.n("gp")
            p.em("tensor", lambda e, _cb=cb: e.matmul(
                statb[0:1, :], P["ones_b"][0:96, 0:1], sq_t[_cb][0:96, :],
                start=True, stop=True),
                wait=(("gp", kmb_sq[cb]), ("dve", prev_red)), inc=(("pe", 1),))
            pe_nrm = p.n("pe")
            col = (NCH + c) * NTC + tc
            p.em("vector", lambda e, _col=col: e.tensor_reduce(
                nmax[0:1, _col:_col + 1], statb[0:1, :], axis=AX.X, op=ALU.max),
                wait=(("pe", pe_nrm),), inc=(("dve", 1),))
            prev_red = p.n("dve")
            p.em("sync", lambda e, _c=c, _cb=cb, _tc=tc: e.dma_start(
                T["km_s"].ap()[_c][:, _tc * TCH:(_tc + 1) * TCH], kmb[_cb][0:96, :]),
                wait=(("act", kvp_ev), ("gp", gp_rope)), inc=(("qs", 16),))
            kmb_st[cb] = p.n("qs")
        pj_evac[(tc + 1) % 3] = kvp_ev
        for i4 in range(4):
            ib = i4 % 2
            wkt = [("act", ktp_copy)]
            if first_ktok:
                wkt.append(("dve", q_red_last))
                first_ktok = False
            if i4 == 0:
                wkt.append(("act", kvl_act))
            p.em("tensor", lambda e, _i=i4, _kb=kb: e.matmul(
                ktp[:, :], kvl[_kb][:, _i * 128:(_i + 1) * 128], wkvb_sb[:, :],
                start=True, stop=True),
                wait=wkt, inc=(("pe", 1),))
            pe_kt = p.n("pe")
            p.em("scalar", lambda e, _ib=ib: e.copy(
                ktb[_ib][:, :].rearrange("p (h c) -> p h c", c=33)[:, :, 0:32],
                ktp[:, :].rearrange("p (h c) -> p h c", c=32)),
                wait=(("pe", pe_kt), ("qs", ktb_st[_ib if False else ib])), inc=(("act", 1),))
            ktp_copy = p.n("act")
            p.em("sync", lambda e, _ib=ib, _t=tc * 4 + i4: e.dma_start(
                T["kt_s"].ap()[_t * 128:(_t + 1) * 128, :], ktb[_ib][:, :]),
                wait=(("act", ktp_copy),), inc=(("qs", 16),))
            ktb_st[ib] = p.n("qs")

    finalize_bounds(p, P, nmax, nbred, nbsq, statb, "negB2", 1)
    p.close()
    les.close()


def ph2b(k, T, P):
    nc = k.nc
    p = Ph(k)
    les = ExitStack()
    sbl = lambda n, shp, dt: les.enter_context(nc.sbuf_tensor(n, shp, dt))
    psl = lambda n, shp, dt: les.enter_context(nc.psum_tensor(n, shp, dt))

    wo2_sb = sbl("wo2_sb", [96, NCH * D], BF16)
    p.em("gpsimd", lambda e: e.dma_start(
        wo2_sb[:, :].rearrange("p (r d) -> p r d", d=D),
        T["wo_m"].ap().rearrange("r p d -> p r d")), inc=(("qg", 16),))
    wld = p.n("qg")

    km_sb = sbl("km_sb", [128, NCH * S], BF16)
    kt_sb = sbl("kt_sb", [128, NT * H * 33], BF16)
    qp2 = [sbl(f"qp2{b}", [128, TCH], BF16) for b in range(2)]
    xo = [sbl(f"xo{b}", [128, TCH], BF16) for b in range(2)]
    pt = [sbl(f"pt2{b}", [128, 1024], BF16) for b in range(3)]
    rcb = sbl("rcb2", [128, TCH], F32)
    yn2 = sbl("yn2", [96, NCH * TCH], BF16)
    ab2 = [sbl(f"ab2{b}", [128, TCH], F32) for b in range(2)]

    sAB = psl("sAB2", [128, 2048], F32)
    yA = [psl(f"yA2{b}", [128, TCH], F32) for b in range(2)]
    yB = [psl(f"yB2{b}", [128, TCH], F32) for b in range(2)]

    p.em("vector", lambda e: e.memset(yn2[32:64, :], 0.0), inc=(("dve", 1),))

    qp_used = [0, 0]
    yn_used = 0
    exp_last = 0
    exp_kt = {}
    av_kt = {}
    rb_read = [0, 0]
    rb_slot = [0, 0, 0, 0]
    rc_out = 0
    wo_bank = [("dve", 0)] * 4
    yn_tt = [0] * NCH
    wo_dve_all = 0
    xo_used = [0, 0]
    ab_st = [0, 0]

    prev_ld = p.n("qs")
    for qc in range(NTC):
        for c in range(NCH):
            p.em("sync", lambda e, _c=c, _qc=qc: e.dma_start(
                km_sb[0:96, _c * S + _qc * TCH:_c * S + (_qc + 1) * TCH],
                T["km_s"].ap()[_c][:, _qc * TCH:(_qc + 1) * TCH]),
                wait=(("qs", prev_ld),) if c == 0 else (), inc=(("qs", 16),))
            p.em("sync", lambda e, _c=c, _qc=qc: e.dma_start(
                km_sb[96:128, _c * S + _qc * TCH:_c * S + (_qc + 1) * TCH],
                T["km_s"].ap()[_c][32:64, _qc * TCH:(_qc + 1) * TCH]), inc=(("qs", 16),))
        p.em("sync", lambda e, _qc=qc: e.dma_start(
            kt_sb[:, _qc * 4 * H * 33:(_qc + 1) * 4 * H * 33].rearrange(
                "p (k c) -> p k c", c=H * 33),
            T["kt_s"].ap()[_qc * TCH:(_qc + 1) * TCH, :].rearrange(
                "(k p) c -> p k c", p=128)), inc=(("qs", 16),))
        ld = p.n("qs")
        prev_ld = ld

        for c in range(NCH):
            hA, hB = 2 * c, 2 * c + 1
            cb_ = c % 2
            p.em("sync", lambda e, _c=c, _qc=qc, _b=c % 2: e.dma_start(
                qp2[_b][:, :], T["qm_s"].ap()[_c][:, _qc * TCH:(_qc + 1) * TCH]),
                wait=(("pe", qp_used[c % 2]),), inc=(("qs", 16),))
            ld_q = p.n("qs")
            nkt = 4 * qc + 4

            def _scores(kt):
                r = kt % 2
                rp = kt % 3
                edge = kt >= 4 * qc
                jj = kt - 4 * qc
                qoff = 128 * jj if edge else 0
                N = 512 - qoff
                wsc = [("act", exp_kt.get(kt - 2, exp_last))]
                if kt == 0:
                    wsc += [("qs", ld), ("qs", ld_q), ("qg", wld)]
                if kt < 2:
                    wsc.append(("dve", wo_dve_all))
                for hh, (rlo, sbase) in enumerate(((0, 0), (64, 512))):
                    p.em("tensor", lambda e, _c=c, _kt=kt, _rlo=rlo, _r=r, _b=cb_,
                         _sb=sbase, _n=N, _qo=qoff, _e=edge: e.matmul(
                        sAB[:, _r * 1024 + _sb:_r * 1024 + _sb + _n],
                        km_sb[_rlo:_rlo + 64, _c * S + _kt * 128:_c * S + (_kt + 1) * 128],
                        qp2[_b][_rlo:_rlo + 64, _qo:TCH],
                        start=True, stop=not _e,
                        tile_position=(_rlo, 0)),
                        wait=wsc if hh == 0 else (),
                        inc=(("pe", 1),) if (hh == 1 and not edge) else ())
                    if edge:
                        p.em("tensor", lambda e, _r=r, _sb=sbase: e.matmul(
                            sAB[:, _r * 1024 + _sb:_r * 1024 + _sb + 128],
                            P["negI"][:, :], P["U01"][:, :],
                            start=False, stop=True),
                            inc=(("pe", 1),) if hh == 1 else ())
                pe_sc = p.n("pe")
                wexp = [("pe", pe_sc)]
                if kt >= 3:
                    wexp.append(("pe", av_kt[kt - 3]))
                p.em("scalar", lambda e, _r=r, _rp=rp, _X=N, _c=c: e.activation(
                    pt[_rp][:, :].rearrange("p (g x) -> p g x", g=2)[:, :, 0:_X],
                    sAB[:, _r * 1024:(_r + 1) * 1024].rearrange(
                        "p (g x) -> p g x", g=2)[:, :, 0:_X],
                    AF.Exp, bias=P["negB2"][:, _c:_c + 1], scale=0.125),
                    wait=wexp, inc=(("act", 1),))
                exp_kt[kt] = p.n("act")

            def _av(kt):
                rp = kt % 3
                edge = kt >= 4 * qc
                jj = kt - 4 * qc
                qoff = 128 * jj if edge else 0
                N = 512 - qoff
                wav = [("act", exp_kt[kt])]
                if kt == 0:
                    wav += [("dve", rb_read[cb_])]
                for hh, (yps, ybase, ytp, sbase, h) in enumerate(((yA[cb_], 0, (0, 0), 0, hA), (yB[cb_], 64, (0, 64), 512, hB))):
                    p.em("tensor", lambda e, _y=yps, _rp=rp, _sb=sbase,
                         _n=N, _qo=qoff, _kt=kt, _h=h, _yb=ybase, _yt=ytp,
                         _st=(kt == 0), _sp=(kt == nkt - 1): e.matmul(
                        _y[_yb:_yb + 33, _qo:_qo + _n],
                        kt_sb[:, _kt * H * 33 + _h * 33:_kt * H * 33 + (_h + 1) * 33],
                        pt[_rp][:, _sb:_sb + _n],
                        start=_st, stop=_sp, tile_position=_yt),
                        wait=wav if (hh == 0 or kt == 0) else (),
                        inc=(("pe", 1),) if hh == 1 else ())
                av_kt[kt] = p.n("pe")

            for kt in range(nkt):
                _scores(kt)
                if kt >= 2:
                    _av(kt - 2)
            if nkt >= 2:
                _av(nkt - 2)
            _av(nkt - 1)
            exp_last = exp_kt[nkt - 1]
            qp_used[cb_] = p.n("pe")
            pe_av_all = p.n("pe")
            for hh2, (yps, h, dlo, klo) in enumerate(
                    ((yA[cb_], hA, 32, 0), (yB[cb_], hB, 96, 64))):
                slot = cb_ * 2 + hh2
                p.em("vector", lambda e, _y=yps, _d=dlo: e.reciprocal(
                    rcb[_d:_d + 1, :], _y[_d:_d + 1, :]),
                    wait=(("pe", pe_av_all), ("qg", rc_out)), inc=(("dve", 1),))
                dv_rc = p.n("dve")
                p.em("gpsimd", lambda e, _d=dlo, _s=slot: e.dma_start(
                    T["rb2_scr"].ap()[_s:_s + 1, :], rcb[_d:_d + 1, :]),
                    wait=(("dve", dv_rc), ("qg", rb_slot[slot])), inc=(("qg", 16),))
                rc_out = p.n("qg")
                p.em("gpsimd", lambda e, _k=klo, _s=slot: e.dma_start(
                    rcb[_k:_k + 32, :], bass.AP(T["rb2_scr"], _s * TCH, [[0, 32], [1, TCH]])),
                    wait=(("dve", rb_read[cb_]), ("qg", rc_out)), inc=(("qg", 16),))
                rb_in = p.n("qg")
                rb_slot[slot] = rb_in
                p.em("vector", lambda e, _y=yps, _c=c, _k=klo: e.tensor_tensor(
                    yn2[_k:_k + 32, _c * TCH:(_c + 1) * TCH], _y[_k:_k + 32, :],
                    rcb[_k:_k + 32, :], op=ALU.mult),
                    wait=(("qg", rb_in), ("pe", yn_used)), inc=(("dve", 1),))
                rb_read[cb_] = p.n("dve")
                if hh2 == 1:
                    yn_tt[c] = p.n("dve")
        for o in range(NCH):
            r4 = o % 4
            ob = o % 2
            p.em("sync", lambda e, _o=o, _ob=ob, _qc=qc: e.dma_start(
                xo[_ob][:, :], T["xn_s"].ap()[_o][:, _qc * TCH:(_qc + 1) * TCH]),
                wait=(("dve", xo_used[ob]),), inc=(("qs", 16),))
            ld_x = p.n("qs")
            wwo = [wo_bank[r4], ("act", exp_last), ("dve", yn_tt[4])]
            for r in range(NCH):
                wh = wwo if r == 0 else \
                    ([("dve", yn_tt[r])] if r >= 5 else ())
                p.em("tensor", lambda e, _o=o, _r=r, _r4=r4: e.matmul(
                    sAB[:, _r4 * 512:(_r4 + 1) * 512],
                    wo2_sb[0:96, _r * D + _o * 128:_r * D + _o * 128 + 128],
                    yn2[0:96, _r * TCH:(_r + 1) * TCH],
                    start=(_r == 0), stop=(_r == NCH - 1)),
                    wait=wh, inc=(("pe", 1),) if r == NCH - 1 else ())
            pe_wo = p.n("pe")
            p.em("vector", lambda e, _o=o, _ob=ob, _r4=r4: e.tensor_tensor(
                ab2[_ob][:, :], sAB[:, _r4 * 512:(_r4 + 1) * 512], xo[_ob][:, :],
                op=ALU.add),
                wait=(("pe", pe_wo), ("qc", ab_st[ob]), ("qs", ld_x)), inc=(("dve", 1),))
            wo_bank[r4] = ("dve", p.n("dve"))
            xo_used[ob] = p.n("dve")
            p.em("scalar", lambda e, _o=o, _ob=ob, _qc=qc: e.dma_start(
                T["a_s"].ap()[_o][:, _qc * TCH:(_qc + 1) * TCH], ab2[_ob][:, :]),
                wait=(("dve", p.n("dve")),), inc=(("qc", 16),))
            ab_st[ob] = p.n("qc")
        wo_dve_all = p.n("dve")
        yn_used = p.n("pe")
    p.close()
    les.close()


def ph3(k, T, P):
    nc = k.nc
    p = Ph(k)
    les = ExitStack()
    sbl = lambda n, shp, dt: les.enter_context(nc.sbuf_tensor(n, shp, dt))
    psl = lambda n, shp, dt: les.enter_context(nc.psum_tensor(n, shp, dt))
    CH3 = 1024
    NCHK = S // CH3
    NE = E + 1

    wgate_sb = sbl("wgate_sb", [128, NCH * E], BF16)
    oh_sb = sbl("oh_sb", [E, E * 128], F32)
    p.em("gpsimd", lambda e: e.dma_start(
        wgate_sb[:, :].rearrange("p (c d) -> p c d", d=E),
        T["wgate"].ap().rearrange("c p d -> p c d")), inc=(("qg", 16),))
    p.em("gpsimd", lambda e: e.dma_start(oh_sb[0:E, :], T["onehot"].ap()),
         inc=(("qg", 16),))
    wgate_ld = p.n("qg")

    h_all = sbl("h_all", [128, NE * 4 * CH3], BF16)
    z2 = sbl("z2", [128, NCH * CH3], BF16)
    wg_sb = [sbl(f"wg_sb{b}", [128, NCH * II], BF16) for b in range(2)]
    wu_sb = [sbl(f"wu_sb{b}", [128, NCH * II], BF16) for b in range(2)]
    wd_sb = [sbl(f"wd_sb{b}", [128, 4 * D], BF16) for b in range(2)]
    rb2 = sbl("rb2", [128, CH3], F32)
    cbb = sbl("cbb", [128, CH3], F32)
    comb_f = sbl("comb_f", [8, CH3], F32)
    a1 = [sbl(f"a1_{b}", [128, CH3], F32) for b in range(2)]
    sqa = [sbl(f"sqa{b}", [128, TCH], BF16) for b in range(2)]
    rms3 = sbl("rms3", [1, TCH], F32)
    rinv3 = sbl("rinv3", [1, TCH], F32)
    sg = [sbl(f"sg{b}", [128, TCH], BF16) for b in range(2)]
    us = [sbl(f"us{b}", [128, TCH], BF16) for b in range(2)]
    lg = sbl("lg", [128, E], F32)
    lg2 = sbl("lg2", [128, E], F32)
    mx = sbl("mx3", [128, 1], F32)
    eg = sbl("eg", [128, E], F32)
    srt = sbl("srt", [128, E], F32)
    msk = sbl("msk", [128, E], F32)
    wun = sbl("wun", [128, E], F32)
    s12 = sbl("s12", [128, 1], F32)
    rs12 = sbl("rs12", [128, 1], F32)
    combt = sbl("combt", [128, E], F32)
    a3 = [sbl(f"a3_{b}", [128, 256], F32) for b in range(2)]
    A3 = [sbl(f"A3_{b}", [128, 256], F32) for b in range(2)]
    s1b = [sbl(f"s1b{b}", [128, 256], F32) for b in range(2)]
    s2b = [sbl(f"s2b{b}", [128, 256], F32) for b in range(2)]
    x3 = [sbl(f"x3_{b}", [128, D], F32) for b in range(2)]
    out_sb = [sbl(f"out_sb{b}", [128, D], F32) for b in range(2)]

    gu_g = psl("gu_g", [128, TCH], F32)
    gu_u = psl("gu_u", [128, TCH], F32)
    moe_ps = psl("moe_ps", [128, 2048], F32)
    tp3 = psl("tp3", [128, 256], F32)
    stat3 = psl("stat3", [128, TCH], F32)

    a1_act = [0, 0]
    a1_dve = [("dve", 0), ("dve", 0)]
    sqa_used = [0, 0]
    stat3_free = ("act", 0)
    z2_used = 0
    h_used = 0
    rb2_used = ("dve", 0)
    gug_sil = 0
    uu_tt = 0
    cbb_rd = 0
    silu_rd = [0, 0]
    uu_rd = [0, 0]
    wld_g = [0, 0]
    wldd = [0, 0]
    wst_g = [0, 0]
    wst_d = [0, 0]
    gate_dve = 0
    tpg_evac = 0
    lg_evac = 0
    cb_out_prev = 0
    cbb_in = 0
    moe_evac = 0
    a3_used = [0, 0]
    A3_used = [0, 0]
    s1_used = [0, 0]
    s2_used = [0, 0]
    x3_used = [0, 0]
    out_st = [0, 0]
    tp3_used = 0
    prev_batch = p.n("qs")

    for ch in range(NCHK):
        for o in range(NCH):
            ob = o % 2
            w = [("act", a1_act[ob]), a1_dve[ob]]
            if o == 0:
                w.append(("qs", prev_batch))
            p.em("sync", lambda e, _o=o, _ob=ob, _ch=ch: e.dma_start(
                a1[_ob][:, :], T["a_s"].ap()[_o][:, _ch * CH3:(_ch + 1) * CH3]),
                wait=w, inc=(("qs", 16),))
            ld = p.n("qs")
            for sub in range(2):
                bank = gu_g if sub == 0 else gu_u
                bank_war = ("act", gug_sil) if sub == 0 else ("dve", uu_tt)
                p.em("scalar", lambda e, _ob=ob, _sub=sub: e.activation(
                    sqa[_sub][:, :], a1[_ob][:, _sub * TCH:(_sub + 1) * TCH], AF.Square),
                    wait=(("qs", ld), ("pe", sqa_used[sub])), inc=(("act", 1),))
                sq_act = p.n("act")
                wmm = [("act", sq_act)]
                if o == 0:
                    wmm += [stat3_free, bank_war]
                p.em("tensor", lambda e, _o=o, _sub=sub, _bk=bank: e.matmul(
                    _bk[0:1, :], P["ones_b"][:, 0:1], sqa[_sub][:, :],
                    start=(_o == 0), stop=(_o == NCH - 1)),
                    wait=wmm, inc=(("pe", 1),))
                sqa_used[sub] = p.n("pe")
            a1_act[ob] = p.n("act")
        for sub in range(2):
            bank = gu_g if sub == 0 else gu_u
            p.em("scalar", lambda e, _bk=bank: e.activation(
                rms3[0:1, :], _bk[0:1, :], AF.Sqrt, bias=P["epsv"][0:1, 0:1],
                scale=1.0 / D), wait=(("pe", p.n("pe")), ("dve", p.n("dve"))),
                inc=(("act", 1),))
            p.em("vector", lambda e: e.reciprocal(rinv3[0:1, :], rms3[0:1, :]),
                 wait=(("act", p.n("act")),), inc=(("dve", 1),))
            p.em("tensor", lambda e: e.matmul(
                stat3[:, :], P["ones_f"][0:1, :], rinv3[0:1, :], start=True, stop=True),
                wait=(("dve", p.n("dve")), stat3_free), inc=(("pe", 1),))
            p.em("scalar", lambda e, _sub=sub: e.copy(
                rb2[:, _sub * TCH:(_sub + 1) * TCH], stat3[:, :]),
                wait=(("pe", p.n("pe")), rb2_used), inc=(("act", 1),))
            stat3_free = ("act", p.n("act"))
        rb2_act = p.n("act")
        gug_sil = rb2_act
        for o in range(NCH):
            ob = o % 2
            w = [("act", a1_act[ob]), a1_dve[ob]]
            if o == 0:
                w.append(("qs", p.n("qs")))
            p.em("sync", lambda e, _o=o, _ob=ob, _ch=ch: e.dma_start(
                a1[_ob][:, :], T["a_s"].ap()[_o][:, _ch * CH3:(_ch + 1) * CH3]),
                wait=w, inc=(("qs", 16),))
            ld = p.n("qs")
            p.em("vector", lambda e, _o=o, _ob=ob: e.scalar_tensor_tensor(
                z2[:, _o * CH3:(_o + 1) * CH3], a1[_ob][:, :],
                P["wln2"][:, _o:_o + 1], rb2[:, :], op0=ALU.mult, op1=ALU.mult),
                wait=(("qs", ld), ("act", rb2_act), ("pe", z2_used)),
                inc=(("dve", 1),))
            a1_dve[ob] = ("dve", p.n("dve"))
            rb2_used = ("dve", p.n("dve"))
        z2_dve = ("dve", p.n("dve"))
        prev_batch = p.n("qs")
        for tt in range(CH3 // 128):
            wlg = [z2_dve, ("qg", wgate_ld), ("act", lg_evac), stat3_free]
            for o in range(NCH):
                p.em("tensor", lambda e, _o=o, _tt=tt: e.matmul(
                    stat3[0:128, 0:E], z2[:, _o * CH3 + _tt * 128:_o * CH3 + (_tt + 1) * 128],
                    wgate_sb[:, _o * E:(_o + 1) * E],
                    start=(_o == 0), stop=(_o == NCH - 1)),
                    wait=wlg if o == 0 else (), inc=(("pe", 1),) if o == NCH - 1 else ())
            p.em("scalar", lambda e: e.copy(lg[:, :], stat3[0:128, 0:E]),
                 wait=(("pe", p.n("pe")), ("dve", gate_dve)), inc=(("act", 1),))
            lg_evac = p.n("act")
            gate_act = p.n("act")

            def gch(build):
                p.em("vector", build, wait=(("dve", p.n("dve")), ("act", gate_act)),
                     inc=(("dve", 1),))
            gch(lambda e: e.tensor_tensor(lg2[:, :], lg[:, :], P["ebb"][:, 0:E], op=ALU.add))
            gch(lambda e: e.tensor_reduce(mx[:, 0:1], lg2[:, :], axis=AX.X, op=ALU.max,
                                          negate=True))
            p.em("scalar", lambda e: e.activation(eg[:, :], lg2[:, :], AF.Exp,
                                                  bias=mx[:, 0:1]),
                 wait=(("dve", p.n("dve")),), inc=(("act", 1),))
            gate_act = p.n("act")
            gch(lambda e: e.max(srt[:, :], eg[:, :]))
            gch(lambda e: e.tensor_scalar(msk[:, :], eg[:, :], srt[:, 1:2], None,
                                          op0=ALU.is_ge))
            gch(lambda e: e.tensor_tensor(wun[:, :], eg[:, :], msk[:, :], op=ALU.mult))
            gch(lambda e: e.tensor_scalar(s12[:, 0:1], srt[:, 1:2], 1.0, None, op0=ALU.add))
            gch(lambda e: e.reciprocal(rs12[:, 0:1], s12[:, 0:1]))
            gch(lambda e: e.tensor_scalar(combt[:, :], wun[:, :], rs12[:, 0:1], None,
                                          op0=ALU.mult))
            gate_dve = p.n("dve")
            p.em("tensor", lambda e: e.transpose(
                tp3[0:E, 0:128], combt[:, :], P["idn_f"][:, :]),
                wait=(("dve", gate_dve), ("dve", tp3_used)), inc=(("pe", 1),))
            p.em("scalar", lambda e, _tt=tt: e.copy(
                comb_f[0:E, _tt * 128:(_tt + 1) * 128], tp3[0:E, 0:128]),
                wait=(("pe", p.n("pe")),), inc=(("act", 1),))
            tpg_evac = p.n("act")
        stat3_free = ("act", tpg_evac)
        uu_tt_p = uu_tt
        for e_i in range(NE):
            eb = e_i % 2
            wldw = [("pe", wld_g[eb]), ("qc", wst_g[eb])]
            if ch == 0:
                gsrc = T["wg"].ap()[e_i] if e_i < E else T["swg"].ap()
                usrc = T["wu"].ap()[e_i] if e_i < E else T["swu"].ap()
                p.em("gpsimd", lambda e, _b=eb, _s=gsrc: e.dma_start(
                    wg_sb[_b][:, :].rearrange("p (c d) -> p c d", d=II),
                    _s.rearrange("c p d -> p c d")),
                    wait=wldw, inc=(("qg", 16),))
                p.em("gpsimd", lambda e, _b=eb, _s=usrc: e.dma_start(
                    wu_sb[_b][:, :].rearrange("p (c d) -> p c d", d=II),
                    _s.rearrange("c p d -> p c d")), inc=(("qg", 16),))
                wld = p.n("qg")
                p.em("scalar", lambda e, _b=eb, _e=e_i: e.dma_start(
                    T["wgb_s"].ap()[_e], wg_sb[_b][:, :]),
                    wait=(("qg", wld),), inc=(("qc", 16),))
                p.em("scalar", lambda e, _b=eb, _e=e_i: e.dma_start(
                    T["wub_s"].ap()[_e], wu_sb[_b][:, :]), inc=(("qc", 16),))
                wst_g[eb] = p.n("qc")
            else:
                p.em("gpsimd", lambda e, _b=eb, _e=e_i: e.dma_start(
                    wg_sb[_b][:, :], T["wgb_s"].ap()[_e]),
                    wait=wldw, inc=(("qg", 16),))
                p.em("gpsimd", lambda e, _b=eb, _e=e_i: e.dma_start(
                    wu_sb[_b][:, :], T["wub_s"].ap()[_e]), inc=(("qg", 16),))
                wld = p.n("qg")
            if e_i == 0:
                p.em("gpsimd", lambda e: e.dma_start(T["cb_scr"].ap(), comb_f[0:E, :]),
                     wait=(("act", tpg_evac), ("qg", cb_out_prev)), inc=(("qg", 16),))
                cb_out = p.n("qg")
            if e_i < E:
                p.em("gpsimd", lambda e, _e=e_i: e.dma_start(
                    cbb[:, :], bass.AP(T["cb_scr"], _e * 1024, [[0, 128], [1, 1024]])),
                    wait=(("qg", cb_out), ("dve", cbb_rd)), inc=(("qg", 16),))
                cbb_in = p.n("qg")
                if e_i == E - 1:
                    cb_out_prev = cbb_in
            cbb_act = p.n("act")
            for sub in range(2):
                for ich in range(4):
                    gb = ich % 2
                    wg0 = [("qg", wld), z2_dve, ("act", gug_sil)]
                    for c in range(NCH):
                        p.em("tensor", lambda e, _c=c, _b=eb, _i=ich, _s=sub: e.matmul(
                            gu_g[:, :], wg_sb[_b][:, _c * II + _i * 128:_c * II + (_i + 1) * 128],
                            z2[:, _c * CH3 + _s * TCH:_c * CH3 + (_s + 1) * TCH],
                            start=(_c == 0), stop=(_c == NCH - 1)),
                            wait=wg0 if c == 0 else (),
                            inc=(("pe", 1),) if c == NCH - 1 else ())
                    pe_g = p.n("pe")
                    p.em("scalar", lambda e, _gb=gb: e.activation(
                        sg[_gb][:, :], gu_g[:, :], AF.Silu),
                        wait=(("pe", pe_g), ("dve", silu_rd[gb])), inc=(("act", 1),))
                    gug_sil = p.n("act")
                    sil_act = p.n("act")
                    wu0 = [("dve", uu_tt), ("act", rb2_act)]
                    for c in range(NCH):
                        p.em("tensor", lambda e, _c=c, _b=eb, _i=ich, _s=sub: e.matmul(
                            gu_u[:, :], wu_sb[_b][:, _c * II + _i * 128:_c * II + (_i + 1) * 128],
                            z2[:, _c * CH3 + _s * TCH:_c * CH3 + (_s + 1) * TCH],
                            start=(_c == 0), stop=(_c == NCH - 1)),
                            wait=wu0 if c == 0 else (),
                            inc=(("pe", 1),) if c == NCH - 1 else ())
                    pe_u = p.n("pe")
                    if e_i < E:
                        p.em("vector", lambda e, _gb=gb, _s=sub: e.tensor_tensor(
                            us[_gb][:, :], gu_u[:, :], cbb[:, _s * TCH:(_s + 1) * TCH],
                            op=ALU.mult),
                            wait=(("pe", pe_u), ("qg", cbb_in), ("dve", uu_rd[gb])),
                            inc=(("dve", 1),))
                    else:
                        p.em("vector", lambda e, _gb=gb: e.tensor_copy(
                            us[_gb][:, :], gu_u[:, :]),
                            wait=(("pe", pe_u), ("dve", uu_rd[gb])), inc=(("dve", 1),))
                    uu_tt = p.n("dve")
                    cbb_rd = p.n("dve")
                    p.em("vector", lambda e, _gb=gb, _e=e_i, _i=ich, _s=sub: e.tensor_tensor(
                        h_all[:, (_e * 4 + _i) * CH3 + _s * TCH:(_e * 4 + _i) * CH3 + (_s + 1) * TCH],
                        sg[_gb][:, :], us[_gb][:, :], op=ALU.mult),
                        wait=(("act", sil_act), ("dve", uu_tt), ("pe", h_used)),
                        inc=(("dve", 1),))
                    silu_rd[gb] = p.n("dve")
                    uu_rd[gb] = p.n("dve")
            wld_g[eb] = p.n("pe")
        z2_used = p.n("pe")
        h_dve = p.n("dve")
        for s4 in range(4):
            for e_i in range(NE):
                eb = e_i % 2
                wldw = [("pe", wldd[eb]), ("qc", wst_d[eb])]
                if ch == 0 and s4 == 0:
                    dsrc = T["wd"].ap()[e_i] if e_i < E else T["swd"].ap()
                    p.em("gpsimd", lambda e, _b=eb, _s=dsrc: e.dma_start(
                        wd_sb[_b][:, :].rearrange("p (c d) -> p c d", d=D),
                        _s.rearrange("c p d -> p c d")),
                        wait=wldw, inc=(("qg", 16),))
                    wld = p.n("qg")
                    p.em("scalar", lambda e, _b=eb, _e=e_i: e.dma_start(
                        T["wdb_s"].ap()[_e], wd_sb[_b][:, :]),
                        wait=(("qg", wld),), inc=(("qc", 16),))
                    wst_d[eb] = p.n("qc")
                else:
                    p.em("gpsimd", lambda e, _b=eb, _e=e_i: e.dma_start(
                        wd_sb[_b][:, :], T["wdb_s"].ap()[_e]),
                        wait=wldw, inc=(("qg", 16),))
                    wld = p.n("qg")
                wv0 = [("qg", wld), ("dve", h_dve)]
                if e_i == 0:
                    wv0.append(("dve", moe_evac))
                for o in range(NCH):
                    for ich in range(4):
                        st_flag = (e_i == 0 and ich == 0 and o % 2 == 0)
                        sp_flag = (e_i == NE - 1 and ich == 3 and o % 2 == 1)
                        p.em("tensor", lambda e, _o=o, _i=ich, _b=eb, _e=e_i, _s4=s4,
                             _st=st_flag, _sp=sp_flag: e.matmul(
                            moe_ps[:, _o * 256:(_o + 1) * 256],
                            wd_sb[_b][:, _i * D + _o * 128:_i * D + _o * 128 + 128],
                            h_all[:, (_e * 4 + _i) * CH3 + _s4 * 256:(_e * 4 + _i) * CH3 + (_s4 + 1) * 256],
                            start=_st, stop=_sp),
                            wait=wv0 if (o == 0 and ich == 0) else (),
                            inc=(("pe", 1),) if (o == NCH - 1 and ich == 3) else ())
                wldd[eb] = p.n("pe")
            pe_moe = p.n("pe")
            for jj in range(2):
                t_tile = ch * 8 + s4 * 2 + jj
                w = [("dve", x3_used[jj])]
                if jj == 0:
                    w.append(("qs", prev_batch))
                p.em("sync", lambda e, _t=t_tile, _j=jj: e.dma_start(
                    x3[_j][:, :], T["x"].ap()[_t * 128:(_t + 1) * 128, :]),
                    wait=w, inc=(("qs", 16),))
            for o in range(NCH):
                ob = o % 2
                p.em("sync", lambda e, _o=o, _ob=ob, _ch=ch, _s4=s4: e.dma_start(
                    a3[_ob][:, :],
                    T["a_s"].ap()[_o][:, _ch * CH3 + _s4 * 256:_ch * CH3 + (_s4 + 1) * 256]),
                    wait=(("dve", a3_used[ob]),), inc=(("qs", 16),))
                p.em("sync", lambda e, _o=o, _ob=ob, _ch=ch, _s4=s4: e.dma_start(
                    A3[_ob][:, :],
                    T["A_s"].ap()[_o][:, _ch * CH3 + _s4 * 256:_ch * CH3 + (_s4 + 1) * 256]),
                    wait=(("dve", A3_used[ob]),), inc=(("qs", 16),))
                ld3 = p.n("qs")
                p.em("vector", lambda e, _o=o, _ob=ob: e.tensor_tensor(
                    s1b[_ob][:, :], moe_ps[:, _o * 256:(_o + 1) * 256], a3[_ob][:, :],
                    op=ALU.add),
                    wait=(("qs", ld3), ("pe", pe_moe), ("dve", s1_used[ob])),
                    inc=(("dve", 1),))
                a3_used[ob] = p.n("dve")
                p.em("vector", lambda e, _ob=ob: e.tensor_tensor(
                    s2b[_ob][:, :], s1b[_ob][:, :], A3[_ob][:, :], op=ALU.add),
                    wait=(("dve", p.n("dve")), ("pe", s2_used[ob])), inc=(("dve", 1),))
                A3_used[ob] = p.n("dve")
                s1_used[ob] = p.n("dve")
                dv_s2 = p.n("dve")
                if o == NCH - 1:
                    moe_evac = dv_s2
                for jj in range(2):
                    p.em("tensor", lambda e, _ob=ob, _j=jj: e.transpose(
                        tp3[:, _j * 128:(_j + 1) * 128],
                        s2b[_ob][:, _j * 128:(_j + 1) * 128], P["idn_f"][:, :]),
                        wait=(("dve", dv_s2), ("dve", tp3_used), ("act", tpg_evac)),
                        inc=(("pe", 1),))
                    s2_used[ob] = p.n("pe")
                    pe_tp = p.n("pe")
                    p.em("vector", lambda e, _j=jj, _o=o: e.tensor_tensor(
                        out_sb[_j][:, _o * 128:(_o + 1) * 128],
                        tp3[:, _j * 128:(_j + 1) * 128],
                        x3[_j][:, _o * 128:(_o + 1) * 128], op=ALU.add),
                        wait=(("pe", pe_tp), ("qs", p.n("qs")), ("qc", out_st[jj])),
                        inc=(("dve", 1),))
                    tp3_used = p.n("dve")
                    x3_used[jj] = p.n("dve")
            for jj in range(2):
                t_tile = ch * 8 + s4 * 2 + jj
                p.em("scalar", lambda e, _t=t_tile, _j=jj: e.dma_start(
                    T["out"].ap()[_t * 128:(_t + 1) * 128, :], out_sb[_j][:, :]),
                    wait=(("dve", tp3_used),), inc=(("qc", 16),))
                out_st[jj] = p.n("qc")
            prev_batch = p.n("qs")
        h_used = p.n("pe")
    p.close()
    les.close()



def layout_inputs(ins):
    g = lambda n: np.asarray(ins[n])
    d = {}
    d["wq_a"] = g("L0_attn_Wq").reshape(NCH, 128, D)
    d["wk_a"] = g("L0_attn_Wk").reshape(NCH, 128, D)
    d["wv_a"] = g("L0_attn_Wv").reshape(NCH, 128, D)
    d["wo_a"] = g("L0_attn_Wo").reshape(H, HD, D)
    d["bq_a"] = g("L0_attn_bq").reshape(NCH, 128, 1)
    d["bk_a"] = g("L0_attn_bk").reshape(NCH, 128, 1)
    d["bv_a"] = g("L0_attn_bv").reshape(1, D)
    d["bo_a"] = g("L0_attn_bo").reshape(NCH, 128, 1)
    d["wln1"] = g("L0_ffn_w_ln1").reshape(NCH, 128, 1)
    d["wln2"] = g("L0_ffn_w_ln2").reshape(NCH, 128, 1)
    d["wq_m"] = g("L0_ffn_Wq").reshape(NCH, 128, D)
    d["wkva"] = g("L0_ffn_Wkva").reshape(NCH, 128, RLAT + RP)
    d["wkvb"] = g("L0_ffn_Wkvb")
    wo2 = g("L0_ffn_Wo").reshape(H, HD, D)[:, 0:32, :]
    wo_m = np.zeros((NCH, 96, D), np.float32)
    wo_m[:, 0:32] = wo2[0::2]
    wo_m[:, 64:96] = wo2[1::2]
    d["wo_m"] = wo_m
    d["wgate"] = g("L0_ffn_Wgate").reshape(NCH, 128, E)
    d["ebias"] = g("L0_ffn_expert_bias").reshape(1, E)
    d["wg"] = g("L0_ffn_Wg").reshape(E, NCH, 128, II)
    d["wu"] = g("L0_ffn_Wu").reshape(E, NCH, 128, II)
    d["wd"] = g("L0_ffn_Wd").reshape(E, 4, 128, D)
    d["swg"] = g("L0_ffn_sWg").reshape(NCH, 128, II)
    d["swu"] = g("L0_ffn_sWu").reshape(NCH, 128, II)
    d["swd"] = g("L0_ffn_sWd").reshape(4, 128, D)
    return d


def kernel(**inputs):
    nc = build()
    shared = layout_inputs(inputs)
    shared.update(host_constants())
    x_full = np.asarray(inputs["L0_attn_x"])
    in_maps = []
    for core in range(8):
        m = dict(shared)
        m["x"] = np.ascontiguousarray(x_full[core])
        in_maps.append(m)
    res = run_bass_kernel_spmd(nc, in_maps, core_ids=list(range(8)))
    return np.stack([r["out"] for r in res.results], axis=0)
